# version 07
# baseline (speedup 1.0000x reference)
import numpy as np
from contextlib import ExitStack

import concourse.bass as bass
import concourse.mybir as mybir
from concourse.bass_utils import run_bass_kernel_spmd

F32 = mybir.dt.float32
BF16 = mybir.dt.bfloat16
Act = mybir.ActivationFunctionType
Alu = mybir.AluOpType
AX = mybir.AxisListType

T, D = 2048, 1024
NT = T // 128
KD = D // 128
MH, ROPE, LAT = 16, 32, 128
E, FF = 8, 4096
NCORES = 8

ENGINES = ("tensor", "vector", "scalar", "gpsimd", "sync")


class Sem:
    __slots__ = ("h", "n", "name")

    def __init__(self, h, name):
        self.h = h
        self.n = 0
        self.name = name


class Prog:
    def __init__(self, nc, stack):
        self.nc = nc
        self.stack = stack
        self.q = {e: [] for e in ENGINES}
        self.lastwait = {e: {} for e in ENGINES}
        self.lastinc = {e: None for e in ENGINES}

    def sem(self, name):
        h = self.stack.enter_context(self.nc.semaphore(name))
        return Sem(h, name)

    def emit(self, eng, fn, waits=(), inc=None):
        wl = []
        for s, v in waits:
            if v <= 0:
                continue
            assert v <= s.n, f"wait {v} > emitted {s.n} on {s.name} (engine {eng})"
            if self.lastwait[eng].get(s.name, -1) < v:
                self.lastwait[eng][s.name] = v
                wl.append((s.h, v))
        il = None
        if inc is not None:
            s, k = inc
            s.n += k
            il = (s.h, k)
            self.lastinc[eng] = (s, s.n)

        def thunk(Eng, wl=wl, il=il, fn=fn):
            for h, v in wl:
                Eng.wait_ge(h, v)
            ins = fn(Eng)
            if il is not None:
                assert ins is not None
                ins.then_inc(*il)

        self.q[eng].append(thunk)

    def wait(self, eng, *waits):
        self.emit(eng, lambda Eng: None, waits=waits)

    def flush(self, name=None):
        qs = {e: fns for e, fns in self.q.items() if fns}
        if qs:
            with self.nc.Block(name) as block:
                for e, fns in qs.items():
                    getattr(block, e)(lambda Eng, fns=fns: [f(Eng) for f in fns])
        self.q = {e: [] for e in ENGINES}


def build_nc(taps=(), upto=None):
    nc = bass.Bass("TRN2", target_bir_lowering=False)
    d_in = {}

    def din(name, shape):
        d_in[name] = nc.dram_tensor(name, shape, F32, kind="ExternalInput").ap()
        return d_in[name]

    x_d = din("x", [T, D])
    aWq = din("aWq", [D, D]); aWkv = din("aWkv", [D, LAT])
    aWk = din("aWk", [LAT, MH * ROPE]); aWv = din("aWv", [LAT, MH * 32])
    aWo = din("aWo", [D, D])
    qg_d = din("qg", [64]); kg_d = din("kg", [ROPE])
    g1_d = din("g1", [D]); bb1_d = din("bb1", [D])
    g2_d = din("g2", [D]); bb2_d = din("bb2", [D])
    fWq = din("fWq", [D, D]); fbq = din("fbq", [D])
    fWk = din("fWk", [D, D]); fbk = din("fbk", [D])
    fWv = din("fWv", [D, D]); fbv = din("fbv", [D])
    fWo = din("fWo", [D, D]); fbo = din("fbo", [D])
    Wr_d = din("Wr", [D, E]); br_d = din("br", [E])
    ew1 = din("ew1", [E, D, FF]); eb1 = din("eb1", [E, FF])
    ew2 = din("ew2", [E, FF, D]); eb2 = din("eb2", [E, D])

    y_d = nc.dram_tensor("y", [T, D], F32, kind="ExternalOutput").ap()
    a1sp = nc.dram_tensor("a1sp", [T, D], F32, kind="Internal").ap()

    taps = list(taps)
    tap_d = {}

    with ExitStack() as st:
        ec = st.enter_context

        stream_h = ec(nc.sbuf_tensor("stream", [128, NT * D], F32))
        fmbuf_h = ec(nc.sbuf_tensor("fmbuf", [128, KD * T], BF16))
        stream = stream_h[:]
        fmbuf = fmbuf_h[:]
        stream_bf = stream_h.ap().bitcast(BF16)

        identb = ec(nc.sbuf_tensor("identb", [128, 128], BF16))
        identf = ec(nc.sbuf_tensor("identf", [128, 128], F32))
        lnT = ec(nc.sbuf_tensor("lnT", [128, 32], F32))
        b1T = ec(nc.sbuf_tensor("b1T", [128, E * 32], F32))
        b2bf = ec(nc.sbuf_tensor("b2bf", [8, D], BF16))
        wrbf = ec(nc.sbuf_tensor("wrbf", [128, KD * E], BF16))
        brbf = ec(nc.sbuf_tensor("brbf", [1, E], BF16))
        wcol = ec(nc.sbuf_tensor("wcol", [128, NT * E], F32))
        wfm = ec(nc.sbuf_tensor("wfm", [8, T], BF16))
        ones_bf = ec(nc.sbuf_tensor("ones_bf", [128, 128], BF16))
        stage = ec(nc.sbuf_tensor("stage", [128, 128], F32))
        junkb = ec(nc.sbuf_tensor("junkb", [128, 2 * 1024], BF16))
        sA = ec(nc.sbuf_tensor("sA", [128, 16], F32))
        sB = ec(nc.sbuf_tensor("sB", [128, 16], F32))
        sC = ec(nc.sbuf_tensor("sC", [128, 16], F32))
        colA = ec(nc.sbuf_tensor("colA", [128, 8], F32))
        m16 = ec(nc.sbuf_tensor("m16", [128, 1], BF16))
        nm32 = ec(nc.sbuf_tensor("nm32", [128, 1], F32))
        nm16 = ec(nc.sbuf_tensor("nm16", [128, 1], BF16))
        sr16 = ec(nc.sbuf_tensor("sr16", [128, 1], BF16))
        nmS = ec(nc.sbuf_tensor("nmS", [128, 16], BF16))[:]
        srS = ec(nc.sbuf_tensor("srS", [128, 16], BF16))[:]
        l8 = ec(nc.sbuf_tensor("l8", [128, 8], F32))
        t8a = ec(nc.sbuf_tensor("t8a", [128, 8], F32))
        t8b = ec(nc.sbuf_tensor("t8b", [128, 8], F32))
        w8b = ec(nc.sbuf_tensor("w8b", [128, 8], BF16))
        qgcol = ec(nc.sbuf_tensor("qgcol", [128, 1], F32))
        kgcol = ec(nc.sbuf_tensor("kgcol", [128, 1], F32))
        epsT = ec(nc.sbuf_tensor("epsT", [128, 2], F32))

        P = Prog(nc, st)
        dS = P.sem("dS"); dO = P.sem("dO"); dT = P.sem("dT")
        d_ln = P.sem("d_ln"); d_b1 = P.sem("d_b1"); d_gn = P.sem("d_gn")
        d_mc = P.sem("d_mc"); d_aw = P.sem("d_aw"); d_wo = P.sem("d_wo")
        vg = P.sem("vg"); sg = P.sem("sg"); gG = P.sem("gG"); tT = P.sem("tT")
        gg = P.sem("gg")

        def dma(eng, sem, dst, src, waits=()):
            P.emit(eng, lambda Eng, d=dst, s=src: Eng.dma_start(out=d, in_=s),
                   waits=waits, inc=(sem, 16))

        def v_op(fn, waits=(), guard=False, inc=None):
            w = list(waits)
            if guard and P.lastinc["vector"] is not None:
                w.append(P.lastinc["vector"])
            P.emit("vector", fn, waits=w, inc=inc if inc is not None else (vg, 1))

        def s_op(fn, waits=(), guard=False, inc=None):
            w = list(waits)
            if guard and P.lastinc["scalar"] is not None:
                w.append(P.lastinc["scalar"])
            P.emit("scalar", fn, waits=w, inc=inc if inc is not None else (sg, 1))

        def g_op(fn, inc=None, waits=(), guard=False):
            w = list(waits)
            if guard and P.lastinc["gpsimd"] is not None:
                w.append(P.lastinc["gpsimd"])
            P.emit("gpsimd", fn, waits=w, inc=inc if inc is not None else (gg, 1))

        def tap(name, ap, waits):
            if name in taps:
                sh = list(ap.shape)
                tap_d[name] = nc.dram_tensor("tap_" + name, sh, F32,
                                             kind="ExternalOutput").ap()
                dma("gpsimd", dT, tap_d[name], ap, waits=waits)

        ph0 = ExitStack()
        pE = ph0.enter_context(nc.psum_tensor("pE", [128, 512], F32))
        pE2 = ph0.enter_context(nc.psum_tensor("pE2", [128, 512], F32))

        dma("sync", dS, stream.rearrange("p (n d) -> p n d", d=D),
            x_d.rearrange("(n p) d -> p n d", p=128))
        x_loaded = dS.n

        for t_ in (identb, identf):
            g_op(lambda Eng, t_=t_: Eng.memset(t_[:], 0.0))
            g_op(lambda Eng, t_=t_: Eng.affine_select(
                out=t_[:], in_=t_[:], compare_op=Alu.not_equal, fill=1.0,
                base=0, pattern=[[-1, 128]], channel_multiplier=1), guard=True)
        g_op(lambda Eng: Eng.memset(ones_bf[:], 1.0))
        g_op(lambda Eng: Eng.memset(epsT[:, 0:1], 1e-6))
        g_op(lambda Eng: Eng.memset(epsT[:, 1:2], 1e-5), inc=(gG, 1))
        ident_ready = gG.n

        dma("sync", d_ln, stage[0:8, :], g1_d.rearrange("(c n) -> c n", n=128))
        dma("sync", d_ln, stage[8:16, :], bb1_d.rearrange("(c n) -> c n", n=128))
        dma("sync", d_ln, stage[16:24, :], g2_d.rearrange("(c n) -> c n", n=128))
        dma("sync", d_ln, stage[24:32, :], bb2_d.rearrange("(c n) -> c n", n=128))
        ln_staged = d_ln.n
        P.emit("tensor", lambda Eng: Eng.transpose(pE[:, 0:128][0:128, 0:32], stage[0:32, :], identf[0:32, 0:32]),
               waits=[(d_ln, ln_staged), (gG, ident_ready)], inc=(tT, 1))
        v_op(lambda Eng: Eng.tensor_copy(out=lnT[:], in_=pE[0:128, 0:32]),
             waits=[(tT, tT.n)])
        lnT_ready = vg.n

        for g in range(2):
            P.emit("sync", lambda Eng, g=g: Eng.dma_start(
                out=stage[:],
                in_=eb1[4 * g:4 * g + 4].rearrange("e (c n) -> (e c) n", n=128)),
                waits=[(vg, lnT_ready)] if g == 0 else [(vg, vg.n)], inc=(dS, 16))
            b1_staged = dS.n
            P.emit("tensor", lambda Eng: Eng.transpose(pE[:, 0:128], stage[:], identf[:]),
                   waits=[(dS, b1_staged), (vg, vg.n)], inc=(tT, 1))
            v_op(lambda Eng, g=g: Eng.tensor_copy(
                out=b1T[:, 128 * g:128 * g + 128], in_=pE[0:128, 0:128]),
                waits=[(tT, tT.n)])
        b1T_ready = vg.n

        dma("gpsimd", d_mc, b2bf[:], eb2)
        dma("gpsimd", d_mc, wrbf[:].rearrange("p (c e) -> p c e", e=E),
            Wr_d.rearrange("(c p) e -> p c e", p=128))
        dma("gpsimd", d_mc, brbf[:], br_d.unsqueeze(0))

        qgrow = ph0.enter_context(nc.sbuf_tensor("qgrow", [1, 128], F32))
        kgrow = ph0.enter_context(nc.sbuf_tensor("kgrow", [1, 128], F32))
        dma("sync", d_gn, stage[0:1, 0:64], qg_d.unsqueeze(0),
            waits=[(vg, b1T_ready)])
        dma("sync", d_gn, stage[32:33, 0:32], kg_d.unsqueeze(0))
        gains_staged = d_gn.n
        v_op(lambda Eng: Eng.tensor_scalar(
            out=qgrow[0:1, :].rearrange("a (j r) -> a j r", r=32),
            in0=stage[0:1, 0:32].unsqueeze(1).to_broadcast([1, 4, 32]),
            scalar1=0.125, scalar2=None, op0=Alu.mult),
            waits=[(d_gn, gains_staged)])
        v_op(lambda Eng: Eng.tensor_copy(
            out=kgrow[0:1, :].rearrange("a (j r) -> a j r", r=32),
            in_=stage[32:33, 0:32].unsqueeze(1).to_broadcast([1, 4, 32])))
        gains_rows = vg.n
        P.emit("tensor", lambda Eng: Eng.transpose(pE[:, 0:128][0:128, 0:1], qgrow[:], identf[0:1, 0:1]),
               waits=[(vg, gains_rows)], inc=(tT, 1))
        v_op(lambda Eng: Eng.tensor_copy(out=qgcol[:], in_=pE[0:128, 0:1]),
             waits=[(tT, tT.n)])
        P.emit("tensor", lambda Eng: Eng.transpose(pE2[:, 0:128][0:128, 0:1], kgrow[:], identf[0:1, 0:1]),
               inc=(tT, 1))
        v_op(lambda Eng: Eng.tensor_copy(out=kgcol[:], in_=pE2[0:128, 0:1]),
             waits=[(tT, tT.n)])
        gains_ready = vg.n

        x_fm = fmbuf
        trf = P.sem("trf")
        tslots = [pE[:, 0:128], pE2[:, 0:128], pE[:, 128:256], pE2[:, 128:256]]
        n_tr = 0
        for i in range(NT):
            for c in range(KD):
                slot = tslots[n_tr % 4]
                w = [(dS, x_loaded), (gG, ident_ready), (vg, gains_ready)]
                if n_tr >= 2:
                    w = [(trf, n_tr - 1)]
                P.emit("tensor", lambda Eng, s=slot, i=i, c=c: Eng.transpose(
                    s, stream[:, 1024 * i + 128 * c:1024 * i + 128 * (c + 1)], identf[:]),
                    waits=w, inc=(tT, 1))
                v_op(lambda Eng, s=slot, i=i, c=c: Eng.tensor_copy(
                    out=x_fm[:, 2048 * c + 128 * i:2048 * c + 128 * (i + 1)], in_=s),
                    waits=[(tT, tT.n)], inc=(trf, 1))
                n_tr += 1
        x_fm_done = trf.n

        tap("x_fm", x_fm, [(trf, x_fm_done)])
        P.flush("ph0")
        ph0.close()
        if upto == "ph0":
            for i in range(NT):
                P.emit("sync", lambda Eng, i=i: Eng.dma_start(
                    out=y_d[128 * i:128 * (i + 1), :],
                    in_=stream[:, 1024 * i:1024 * (i + 1)]),
                    waits=[(trf, x_fm_done)], inc=(dO, 16))
            P.wait("sync", (dO, dO.n))
            P.wait("gpsimd", (dT, dT.n))
            P.flush("tail")
            return nc, tap_d

        ph1 = ExitStack()
        mask128 = ph1.enter_context(nc.sbuf_tensor("mask128", [128, 128], F32))
        mask896 = ph1.enter_context(nc.sbuf_tensor("mask896", [128, 896], F32))
        negm_all = [ph1.enter_context(nc.sbuf_tensor(f"negma{j}", [128, T], BF16))
                    for j in range(2)]
        srec_all = [ph1.enter_context(nc.sbuf_tensor(f"sreca{j}", [128, T], BF16))
                    for j in range(2)]
        phat = ph1.enter_context(nc.sbuf_tensor("phat", [128, 3 * 512], BF16))
        qn_sc = ph1.enter_context(nc.sbuf_tensor("qn_sc", [128, 1024], BF16))

        ph1m = ExitStack()
        q_fm = ph1m.enter_context(nc.sbuf_tensor("q_fm", [128, 4 * T], BF16))[:]
        k_fm = ph1m.enter_context(nc.sbuf_tensor("k_fm", [128, 4 * T], BF16))[:]
        v_tm = ph1m.enter_context(nc.sbuf_tensor("v_tm", [128, NT * 512], BF16))[:]
        kv_fm = ph1m.enter_context(nc.sbuf_tensor("kv_fm", [128, T], BF16))[:]

        phP = ExitStack()
        pA = phP.enter_context(nc.psum_tensor("pA", [128, 2048], F32))
        pB = [phP.enter_context(nc.psum_tensor(f"pB{j}", [128, 512], F32))
              for j in range(3)]
        pMisc_h = phP.enter_context(nc.psum_tensor("pMisc", [128, 1024], BF16))
        pMisc = pMisc_h[:]
        pMiscF = pMisc_h.ap().bitcast(F32)
        pA_bf = pA.bitcast(BF16)

        class TransSlots:

            def __init__(self, slots):
                self.slots = slots
                self.last = [None] * len(slots)
                self.i = 0

            def acquire(self):
                j = self.i % len(self.slots)
                self.i += 1
                self.cur = j
                w = [self.last[j]] if self.last[j] is not None else []
                return self.slots[j][0], self.slots[j][1], w

            def read(self, sem):
                self.last[self.cur] = (sem, sem.n)

        TS = TransSlots([
            (pMisc[:, 0:128], pMiscF[:, 0:128]),
            (pA_bf[:, 2048:2176], pA[:, 1024:1152]),
            (pA_bf[:, 3072:3200], pA[:, 1536:1664]),
        ])

        g_op(lambda Eng: Eng.memset(mask128[:], 0.0))
        g_op(lambda Eng: Eng.affine_select(
            out=mask128[:], in_=mask128[:], compare_op=Alu.is_ge, fill=-1e30,
            base=0, pattern=[[-1, 128]], channel_multiplier=1), guard=True)
        g_op(lambda Eng: Eng.memset(mask896[:], 0.0))
        g_op(lambda Eng: Eng.affine_select(
            out=mask896[:], in_=mask896[:], compare_op=Alu.is_ge, fill=-1e30,
            base=-384, pattern=[[1, 896]], channel_multiplier=-1),
            guard=True, inc=(gG, 1))
        masks_ready = gG.n

        ph1w = ExitStack()
        wq_bufs = [ph1w.enter_context(nc.sbuf_tensor(f"awq{j}", [128, 8 * 256], BF16))
                   for j in range(2)]
        wkv_b = ph1w.enter_context(nc.sbuf_tensor("awkv", [128, 8 * 128], BF16))
        wk_b = ph1w.enter_context(nc.sbuf_tensor("awk", [128, 512], BF16))
        wv_b = ph1w.enter_context(nc.sbuf_tensor("awv", [128, 512], BF16))
        wqfree = P.sem("wqfree")
        d_wq = [P.sem("d_wq0"), P.sem("d_wq1")]
        wq_loads = {}

        def load_wq(oc):
            w = [] if oc < 2 else [(wqfree, oc - 1)]
            P.emit("gpsimd", lambda Eng, oc=oc: Eng.dma_start(
                out=wq_bufs[oc % 2][:].rearrange("p (c n) -> p c n", n=256),
                in_=aWq[:, 256 * oc:256 * (oc + 1)].rearrange("(c p) n -> p c n", p=128)),
                waits=w, inc=(d_wq[oc % 2], 16))
            wq_loads[oc] = (d_wq[oc % 2], d_wq[oc % 2].n)

        load_wq(0)
        load_wq(1)
        dma("gpsimd", d_aw, wkv_b[:].rearrange("p (c n) -> p c n", n=128),
            aWkv.rearrange("(c p) n -> p c n", p=128))
        dma("gpsimd", d_aw, wk_b[:], aWk)
        dma("gpsimd", d_aw, wv_b[:], aWv)
        akv_loaded = d_aw.n

        kvT = P.sem("kvT")
        for tc in range(4):
            pt = pB[tc % 2]
            w = [(d_aw, akv_loaded), (trf, x_fm_done)]
            if tc >= 2:
                w.append((kvT, tc - 1))
            for c in range(KD):
                P.emit("tensor", lambda Eng, pt=pt, c=c, tc=tc: Eng.matmul(
                    pt[:, 0:512],
                    wkv_b[:, 128 * c:128 * (c + 1)],
                    x_fm[:, 2048 * c + 512 * tc:2048 * c + 512 * (tc + 1)],
                    start=(c == 0), stop=(c == KD - 1)),
                    waits=w if c == 0 else (), inc=(tT, 1) if c == KD - 1 else None)
            s_op(lambda Eng, pt=pt, tc=tc: Eng.copy(
                out=kv_fm[:, 512 * tc:512 * (tc + 1)], in_=pt[:, 0:512]),
                waits=[(tT, tT.n)], inc=(kvT, 1))
        kv_done = kvT.n

        kvuse = P.sem("kvuse")
        kfmd = P.sem("kfmd")
        for i in range(NT):
            pk = pB[i % 2]
            w = [(kvT, kv_done), (d_aw, akv_loaded)]
            if i >= 2:
                w.append((kvuse, 2 * (i - 2) + 1))
            P.emit("tensor", lambda Eng, pk=pk, i=i: Eng.matmul(
                pk[:, 0:512], kv_fm[:, 128 * i:128 * (i + 1)], wk_b[:],
                start=True, stop=True), waits=w, inc=(tT, 1))
            k_mm = tT.n
            for h in range(16):
                s_op(lambda Eng, pk=pk, h=h: Eng.activation(
                    out=junkb[:, 0:32], in_=pk[:, 32 * h:32 * (h + 1)],
                    func=Act.Square, accum_out=sA[:, h:h + 1]),
                    waits=[(tT, k_mm), (vg, vg.n)] if h == 0 else (), guard=True)
            s_op(lambda Eng: Eng.activation(
                out=sB[:, 0:16], in_=sA[:, 0:16], func=Act.Sqrt,
                scale=1.0 / 32, bias=epsT[:, 0:1]), guard=True)
            s_rms = sg.n
            v_op(lambda Eng: Eng.reciprocal(sC[:, 0:16], sB[:, 0:16]),
                 waits=[(sg, s_rms)], guard=True)
            v_rstd = vg.n
            for h in range(16):
                v_op(lambda Eng, pk=pk, h=h: Eng.tensor_scalar(
                    out=qn_sc[:, 32 * h:32 * (h + 1)], in0=pk[:, 32 * h:32 * (h + 1)],
                    scalar1=sC[:, h:h + 1], scalar2=None, op0=Alu.mult),
                    waits=[(vg, v_rstd)],
                    inc=(kvuse, 1) if h == 15 else (vg, 1))
            kn_done = kvuse.n
            for b in range(4):
                tsb, tsf, tsw = TS.acquire()
                wtr = [(kvuse, kn_done), (vg, vg.n), (gG, ident_ready)] + tsw
                P.emit("tensor", lambda Eng, b=b, tsb=tsb: Eng.transpose(
                    tsb, qn_sc[:, 128 * b:128 * (b + 1)], identb[:]),
                    waits=wtr, inc=(tT, 1))
                ktr = tT.n
                v_op(lambda Eng, b=b, i=i, tsb=tsb: Eng.tensor_scalar(
                    out=k_fm[:, 2048 * b + 128 * i:2048 * b + 128 * (i + 1)],
                    in0=tsb,
                    scalar1=kgcol[:], scalar2=None, op0=Alu.mult),
                    waits=[(tT, ktr)], inc=(kfmd, 1))
                TS.read(kfmd)
            wv = [(d_aw, akv_loaded)]
            if i >= 1:
                wv.append((kvuse, 2 * (i - 1) + 2))
            P.emit("tensor", lambda Eng, i=i: Eng.matmul(
                pB[2][:, 0:512], kv_fm[:, 128 * i:128 * (i + 1)], wv_b[:],
                start=True, stop=True), waits=wv, inc=(tT, 1))
            v_mm = tT.n
            s_op(lambda Eng, i=i: Eng.copy(
                out=v_tm[:, 512 * i:512 * (i + 1)], in_=pB[2][:, 0:512]),
                waits=[(tT, v_mm)], inc=(kvuse, 1))
        kfm_all = kfmd.n
        vtm_all = kvuse.n

        qfmd = P.sem("qfmd")
        quse = P.sem("quse")
        for oc in range(4):
            for i in range(NT):
                n = oc * NT + i
                pq = pA[:, 512 * (n % 2):512 * (n % 2) + 256]
                w = [wq_loads[oc], (trf, x_fm_done), (kvuse, vtm_all)]
                if n >= 2:
                    w.append((quse, n - 1))
                last_inc = None
                for c in range(KD):
                    if c == KD - 1:
                        last_inc = (wqfree, 1) if i == NT - 1 else (tT, 1)
                    P.emit("tensor", lambda Eng, pq=pq, c=c, i=i, oc=oc: Eng.matmul(
                        pq,
                        x_fm[:, 2048 * c + 128 * i:2048 * c + 128 * (i + 1)],
                        wq_bufs[oc % 2][:, 256 * c:256 * (c + 1)],
                        start=(c == 0), stop=(c == KD - 1)),
                        waits=w if c == 0 else (),
                        inc=last_inc if c == KD - 1 else None)
                q_mm_wait = (wqfree, oc + 1) if i == NT - 1 else (tT, tT.n)
                for h in range(4):
                    s_op(lambda Eng, pq=pq, h=h: Eng.activation(
                        out=junkb[:, 0:64], in_=pq[0:128, 64 * h:64 * (h + 1)],
                        func=Act.Square, accum_out=sA[:, h:h + 1]),
                        waits=[q_mm_wait, (vg, vg.n)] if h == 0 else (), guard=True)
                s_op(lambda Eng: Eng.activation(
                    out=sB[:, 0:4], in_=sA[:, 0:4], func=Act.Sqrt,
                    scale=1.0 / 64, bias=epsT[:, 0:1]), guard=True)
                s_rms = sg.n
                v_op(lambda Eng: Eng.reciprocal(sC[:, 0:4], sB[:, 0:4]),
                     waits=[(sg, s_rms)], guard=True)
                v_rstd = vg.n
                for h in range(4):
                    v_op(lambda Eng, pq=pq, h=h: Eng.tensor_scalar(
                        out=qn_sc[:, 32 * h:32 * (h + 1)],
                        in0=pq[0:128, 64 * h:64 * h + 32],
                        scalar1=sC[:, h:h + 1], scalar2=None, op0=Alu.mult),
                        waits=[(vg, v_rstd)],
                        inc=(quse, 1) if h == 3 else (vg, 1))
                tsb, tsf, tsw = TS.acquire()
                wtr = [(quse, quse.n), (vg, vg.n), (kfmd, kfm_all)] + tsw
                P.emit("tensor", lambda Eng, tsb=tsb: Eng.transpose(
                    tsb, qn_sc[:, 0:128], identb[:]),
                    waits=wtr, inc=(tT, 1))
                qtr = tT.n
                v_op(lambda Eng, oc=oc, i=i, tsb=tsb: Eng.tensor_scalar(
                    out=q_fm[:, 2048 * oc + 128 * i:2048 * oc + 128 * (i + 1)],
                    in0=tsb, scalar1=qgcol[:], scalar2=None, op0=Alu.mult),
                    waits=[(tT, qtr)], inc=(qfmd, 1))
                TS.read(qfmd)
            if oc + 2 < 4:
                load_wq(oc + 2)
        qfm_all = qfmd.n

        ph1w.close()
        tap("q_fm", q_fm, [(qfmd, qfm_all)])
        tap("k_fm", k_fm, [(kfmd, kfm_all)])
        tap("v_tm", v_tm, [(kvuse, vtm_all)])
        tap("kv_fm", kv_fm, [(kvT, kv_done)])
        if upto == "proj":
            P.wait("sync", (qfmd, qfm_all), (kfmd, kfm_all), (kvuse, vtm_all))
            for i in range(NT):
                P.emit("sync", lambda Eng, i=i: Eng.dma_start(
                    out=y_d[128 * i:128 * (i + 1), :],
                    in_=stream[:, 1024 * i:1024 * (i + 1)]), inc=(dO, 16))
            P.wait("sync", (dO, dO.n))
            P.flush("tail")
            ph1m.close()
            ph1.close()
            return nc, tap_d

        def attention(tagp, NH, HDK, HDV, qf, kf, vt, o_fm, vt_stride,
                      prereq, o_extra=lambda h, tc: []):
            hpc = 128 // HDK
            opc = 128 // HDV
            str_ = P.sem(tagp + "str")
            sexp = P.sem(tagp + "sexp")
            rowd = P.sem(tagp + "rowd")
            expd = P.sem(tagp + "expd")
            pvp = P.sem(tagp + "pvp")
            oev = P.sem(tagp + "oev")
            bmm = P.sem(tagp + "bmm")
            bias_cnt = {-2: 0, -1: 0}
            st_slots = [pB[0][:, 0:512], pB[1][:, 0:512], pMiscF[:, 0:512]]
            st_mark = []
            st_n = [0]

            def st_acquire():
                n = st_n[0]; st_n[0] += 1
                w = []
                if n >= 3:
                    w.append(st_mark[n - 3])
                assert len(st_mark) == n
                return st_slots[n % 3], w

            def stats_mms(h, i):
                cch = h // hpc
                r0 = HDK * (h % hpc)
                tpr = (r0 // 32) * 32
                W = 128 * (i + 1)
                w = list(prereq)
                w.append((sexp, sexp.n))
                nsub = (W + 511) // 512
                smm0 = str_.n
                P.emit("tensor", lambda Eng: Eng.ldweights(identb[:, 0:1]))
                for j in range(nsub):
                    ww = min(512, W - 512 * j)
                    P.emit("tensor", lambda Eng, j=j, ww=ww, cch=cch, r0=r0,
                           tpr=tpr, i=i: Eng.matmul(
                               pA[:, 512 * j:512 * j + ww],
                               qf[r0:r0 + HDK, 2048 * cch + 128 * i:
                                  2048 * cch + 128 * (i + 1)],
                               kf[r0:r0 + HDK, 2048 * cch + 512 * j:
                                  2048 * cch + 512 * j + ww],
                               start=True, stop=True, tile_position=(tpr, 0)),
                           waits=w if j == 0 else (), inc=(str_, 1))
                for j in range(nsub):
                    ww = min(512, W - 512 * j)
                    bw = [(str_, smm0 + j + 1)]
                    if j == nsub - 1:
                        v_op(lambda Eng, W=W: Eng.tensor_tensor(
                            out=pA[:, W - 128:W], in0=pA[:, W - 128:W],
                            in1=mask128[:], op=Alu.add),
                            waits=bw + [(gG, masks_ready)])
                        bw = []
                    v_op(lambda Eng, j=j, ww=ww: Eng.reduce_max(
                        sA[:, 4 + j:5 + j], pA[:, 512 * j:512 * j + ww],
                        axis=AX.X), waits=bw, guard=True)
                v_op(lambda Eng, nsub=nsub: Eng.reduce_max(
                    sA[:, 0:1], sA[:, 4:4 + nsub], axis=AX.X),
                    waits=[(sexp, sexp.n)], guard=True)
                v_op(lambda Eng: Eng.tensor_copy(out=m16[:], in_=sA[:, 0:1]),
                     guard=True)
                v_op(lambda Eng: Eng.tensor_scalar(
                    out=nm32[:], in0=m16[:], scalar1=-1.0, scalar2=None,
                    op0=Alu.mult), guard=True)
                v_op(lambda Eng, i=i: Eng.tensor_scalar(
                    out=nmS[:, i:i + 1], in0=m16[:], scalar1=-1.0,
                    scalar2=None, op0=Alu.mult), guard=True)
                vbias_of[(h, i)] = vg.n

            def stats_exp(h, i, ch):
                W = 128 * (i + 1)
                nch = (W + 1023) // 1024
                cw = min(1024, W - 1024 * ch)
                dst = sB if ch == 0 else sC
                s_op(lambda Eng, ch=ch, cw=cw, dst=dst: Eng.activation(
                    out=junkb[:, 1024 * ch:1024 * ch + cw],
                    in_=pA[:, 1024 * ch:1024 * ch + cw], func=Act.Exp,
                    bias=nm32[:], accum_out=dst[:, 1:2]),
                    waits=[(vg, vbias_of[(h, i)])], guard=True, inc=(sexp, 1))
                if ch < nch - 1:
                    return
                s_exp = sexp.n
                if nch == 2:
                    v_op(lambda Eng: Eng.tensor_add(
                        sB[:, 1:2], sB[:, 1:2], sC[:, 1:2]),
                        waits=[(sexp, s_exp)], guard=True)
                v_op(lambda Eng: Eng.reciprocal(sC[:, 2:3], sB[:, 1:2]),
                     waits=[(sexp, s_exp)], guard=True)
                v_op(lambda Eng, i=i: Eng.tensor_copy(
                    out=srS[:, i:i + 1], in_=sC[:, 2:3]), guard=True,
                    inc=(stch, 1))

            def stats_repl(h, i):
                par = h % 2
                need = stch_of[(h, i)]
                slot, w = st_acquire()
                P.emit("tensor", lambda Eng, slot=slot, i=i: Eng.matmul(
                    slot[:, 0:128],
                    nmS[:, i:i + 1].to_broadcast([128, 128]),
                    identb[:], start=True, stop=True),
                    waits=w + [(stch, need)])
                P.emit("tensor", lambda Eng, slot=slot, i=i: Eng.matmul(
                    slot[:, 128:256],
                    srS[:, i:i + 1].to_broadcast([128, 128]),
                    identb[:], start=True, stop=True), inc=(tT, 1))
                t2 = tT.n
                rg = [(bmm, bias_cnt[h - 2])]
                v_op(lambda Eng, slot=slot, par=par, i=i: Eng.tensor_copy(
                    out=negm_all[par][:, 128 * i:128 * (i + 1)],
                    in_=slot[:, 0:128]),
                    waits=[(tT, t2)] + rg, inc=(rowd, 1))
                v_op(lambda Eng, slot=slot, par=par, i=i: Eng.tensor_copy(
                    out=srec_all[par][:, 128 * i:128 * (i + 1)],
                    in_=slot[:, 128:256]),
                    inc=(rowd, 1))
                st_mark.append((rowd, rowd.n))

            ecnt = [0]
            pv_cnt = {}
            exp_of = {}

            def einsum_front(h, tc, sc, rows_ready):
                par = h % 2
                cch = h // hpc
                r0 = HDK * (h % hpc)
                tpr = (r0 // 32) * 32
                m = ecnt[0]
                ecnt[0] += 1
                pstt, w = st_acquire()
                w.append((rowd, rows_ready))
                P.emit("tensor", lambda Eng: Eng.ldweights(identb[:, 0:1]))
                P.emit("tensor", lambda Eng, pstt=pstt, sc=sc, tc=tc, cch=cch,
                       r0=r0, tpr=tpr: Eng.matmul(
                           pstt[:, 0:512],
                           kf[r0:r0 + HDK, 2048 * cch + 128 * sc:
                              2048 * cch + 128 * (sc + 1)],
                           qf[r0:r0 + HDK, 2048 * cch + 512 * tc:
                              2048 * cch + 512 * (tc + 1)],
                           start=True, stop=False, tile_position=(tpr, 0)),
                       waits=w)
                P.emit("tensor", lambda Eng, pstt=pstt, par=par, tc=tc,
                       tpr=tpr: Eng.matmul(
                    pstt[:, 0:512], ones_bf[tpr:tpr + 1, :],
                    negm_all[par][tpr:tpr + 1, 512 * tc:512 * (tc + 1)],
                    start=False, stop=True, tile_position=(tpr, 0)),
                    inc=(bmm, 1))
                stmm = bmm.n
                off = 128 * sc - 512 * tc
                slot = m % 3
                exp_w = [(bmm, stmm)]
                if off >= 0:
                    v_op(lambda Eng, pstt=pstt, off=off: Eng.tensor_tensor(
                        out=pstt[:, 0:512], in0=pstt[:, 0:512],
                        in1=mask896[:, 384 - off:896 - off], op=Alu.add),
                        waits=[(bmm, stmm), (gG, masks_ready)])
                    exp_w = [(vg, vg.n)]
                if m >= 3:
                    exp_w.append((pvp, pv_cnt[m - 3]))
                s_op(lambda Eng, pstt=pstt, slot=slot: Eng.activation(
                    out=phat[:, 512 * slot:512 * (slot + 1)],
                    in_=pstt[:, 0:512], func=Act.Exp),
                    waits=exp_w, inc=(expd, 1))
                st_mark.append((expd, expd.n))
                exp_of[m] = expd.n
                return (m, h, tc, sc, slot, expd.n)

            def pv_emit(job):
                m, h, tc, sc, slot, pexp = job
                orow = HDV * (h % opc)
                nsc = 4 * (tc + 1)
                pv_w = [(expd, pexp)]
                if sc == 0:
                    pv_w.append((oev, oev.n))
                P.emit("tensor", lambda Eng: Eng.ldweights(identb[:, 0:1]))
                P.emit("tensor", lambda Eng, sc=sc, h=h, slot=slot,
                       orow=orow, nsc=nsc: Eng.matmul(
                    pB[2][orow:orow + HDV, 0:512],
                    vt[:, vt_stride * sc + HDV * h:vt_stride * sc + HDV * (h + 1)],
                    phat[:, 512 * slot:512 * (slot + 1)],
                    start=(sc == 0), stop=(sc == nsc - 1),
                    tile_position=(0, orow)),
                    waits=pv_w, inc=(pvp, 1))
                pv_cnt[m] = pvp.n
                if sc == nsc - 1:
                    o_evac(h, tc)

            def o_evac(h, tc):
                par = h % 2
                orow = HDV * (h % opc)
                ocol = T * (h // opc)
                v_op(lambda Eng, tc=tc, orow=orow, ocol=ocol, par=par:
                     Eng.tensor_tensor(
                         out=o_fm[orow:orow + HDV,
                                  ocol + 512 * tc:ocol + 512 * (tc + 1)],
                         in0=pB[2][orow:orow + HDV, 0:512],
                         in1=srec_all[par][orow:orow + HDV,
                                           512 * tc:512 * (tc + 1)],
                         op=Alu.mult),
                     waits=[(pvp, pvp.n)] + o_extra(h, tc),
                     inc=(oev, 1))

            stch = P.sem(tagp + "sch")
            stch_of = {}
            vbias_of = {}

            def nch_of(i):
                return (128 * (i + 1) + 1023) // 1024

            NHrun = NH
            if upto == "attn1":
                NHrun = 1
            rows_ready = {}
            for i in range(NT):
                stats_mms(0, i)
                for ch in range(nch_of(i)):
                    stats_exp(0, i, ch)
                stch_of[(0, i)] = stch.n
                stats_repl(0, i)
            rows_ready[0] = rowd.n
            if upto == "attnS":
                return rowd, rowd.n, str_, bmm
            for h in range(NHrun):
                nxt = h + 1 < NHrun
                acts = []
                if nxt:
                    for j in range(NT):
                        acts.append(("mm", j))
                        for ch in range(nch_of(j)):
                            acts.append(("ex", j, ch))
                        if j >= 4:
                            acts.append(("rp", j - 4))
                    for j in range(NT - 4, NT):
                        acts.append(("rp", j))

                def do_act(a):
                    if a[0] == "mm":
                        stats_mms(h + 1, a[1])
                    elif a[0] == "ex":
                        stats_exp(h + 1, a[1], a[2])
                        if a[2] == nch_of(a[1]) - 1:
                            stch_of[(h + 1, a[1])] = stch.n
                    else:
                        stats_repl(h + 1, a[1])

                ai = 0
                pend_pv = []
                slots = [(tc, sc) for tc in range(4) for sc in range(4 * (tc + 1))]
                nsl = len(slots)
                for si, (tc, sc) in enumerate(slots):
                    pend_pv.append(einsum_front(h, tc, sc, rows_ready[h]))
                    if len(pend_pv) > 2:
                        pv_emit(pend_pv.pop(0))
                    want = (len(acts) * (si + 1)) // nsl
                    while ai < want:
                        do_act(acts[ai])
                        ai += 1
                for job in pend_pv:
                    pv_emit(job)
                assert ai == len(acts)
                if nxt:
                    rows_ready[h + 1] = rowd.n
                bias_cnt[h] = bmm.n
            return oev, oev.n, str_, bmm

        o_fm_mla = fmbuf[:, 0:4 * T]
        oevM, oevM_n, strM, bmmM = attention(
            "ma", MH, ROPE, 32, q_fm, k_fm, v_tm, o_fm_mla, 512,
            prereq=[(qfmd, qfm_all), (kfmd, kfm_all), (kvuse, vtm_all)])

        tap("o_fm", o_fm_mla, [(oevM, oevM_n)])
        tap("negm0", negm_all[0][0:1, :], [(oevM, oevM_n)])
        tap("srec0", srec_all[0][0:1, :], [(oevM, oevM_n)])
        if upto in ("attn", "attn1", "attnS"):
            P.wait("sync", (oevM, oevM_n))
            for i in range(NT):
                P.emit("sync", lambda Eng, i=i: Eng.dma_start(
                    out=y_d[128 * i:128 * (i + 1), :],
                    in_=stream[:, 1024 * i:1024 * (i + 1)]), inc=(dO, 16))
            P.wait("sync", (dO, dO.n))
            P.flush("tail")
            ph1m.close()
            ph1.close()
            return nc, tap_d

        wosel = ph1m.enter_context(nc.sbuf_tensor("wosel", [128, 4 * 1024], BF16))
        for h in range(16):
            P.emit("gpsimd", lambda Eng, h=h: Eng.dma_start(
                out=wosel[32 * (h % 4):32 * (h % 4) + 32,
                          1024 * (h // 4):1024 * (h // 4) + 1024],
                in_=aWo[64 * h:64 * h + 32, :]),
                waits=[(wqfree, 4), (kvT, kv_done), (kvuse, vtm_all),
                       (qfmd, qfm_all)] if h == 0 else (),
                inc=(d_wo, 16))
        wosel_loaded = d_wo.n

        pl_sem = P.sem("pl")
        woev = P.sem("woev")
        for i in range(NT):
            for half in range(2):
                n = 2 * i + half
                pq = pB[n % 2]
                w = [(oevM, oevM_n), (d_wo, wosel_loaded)]
                if n >= 2:
                    w.append((woev, n - 1))
                for g in range(4):
                    P.emit("tensor", lambda Eng, pq=pq, g=g, i=i, half=half: Eng.matmul(
                        pq[:, 0:512],
                        o_fm_mla[:, 2048 * g + 128 * i:2048 * g + 128 * (i + 1)],
                        wosel[:, 1024 * g + 512 * half:1024 * g + 512 * (half + 1)],
                        start=(g == 0), stop=(g == 3)),
                        waits=w if g == 0 else (), inc=(tT, 1) if g == 3 else None)
                wo_mm = tT.n
                v_op(lambda Eng, pq=pq, i=i, half=half: Eng.tensor_tensor(
                    out=stream[:, 1024 * i + 512 * half:1024 * i + 512 * (half + 1)],
                    in0=pq[:, 0:512],
                    in1=stream[:, 1024 * i + 512 * half:1024 * i + 512 * (half + 1)],
                    op=Alu.add),
                    waits=[(tT, wo_mm)], inc=(woev, 1))
            s_added = woev.n
            v_op(lambda Eng, i=i: Eng.reduce_sum(
                colA[:, 0:1], stream[:, 1024 * i:1024 * (i + 1)], axis=AX.X),
                waits=[(woev, s_added)], guard=True)
            s_op(lambda Eng, i=i: Eng.activation(
                out=junkb[:, 0:1024], in_=stream[:, 1024 * i:1024 * (i + 1)],
                func=Act.Square, accum_out=colA[:, 1:2]),
                waits=[(woev, s_added), (vg, vg.n)], guard=True)
            s_sq = sg.n
            v_op(lambda Eng: Eng.tensor_scalar(
                out=colA[:, 2:3], in0=colA[:, 0:1], scalar1=1.0 / 1024, scalar2=None,
                op0=Alu.mult), guard=True)
            v_op(lambda Eng: Eng.tensor_scalar(
                out=colA[:, 3:4], in0=colA[:, 1:2], scalar1=1.0 / 1024, scalar2=None,
                op0=Alu.mult), waits=[(sg, s_sq)], guard=True)
            v_op(lambda Eng: Eng.tensor_tensor(
                out=colA[:, 4:5], in0=colA[:, 2:3], in1=colA[:, 2:3], op=Alu.mult),
                guard=True)
            v_op(lambda Eng: Eng.tensor_sub(
                colA[:, 5:6], colA[:, 3:4], colA[:, 4:5]), guard=True)
            v_var = vg.n
            s_op(lambda Eng: Eng.activation(
                out=colA[:, 6:7], in_=colA[:, 5:6], func=Act.Sqrt, bias=epsT[:, 1:2]),
                waits=[(vg, v_var)], guard=True)
            s_sqrt = sg.n
            v_op(lambda Eng: Eng.reciprocal(colA[:, 7:8], colA[:, 6:7]),
                 waits=[(sg, s_sqrt)], guard=True)
            v_op(lambda Eng, i=i: Eng.tensor_scalar(
                out=stream[:, 1024 * i:1024 * (i + 1)],
                in0=stream[:, 1024 * i:1024 * (i + 1)],
                scalar1=colA[:, 2:3], scalar2=colA[:, 7:8],
                op0=Alu.subtract, op1=Alu.mult), guard=True, inc=(pl_sem, 1))
        a1_done = pl_sem.n

        tap("a1", stream, [(pl_sem, a1_done)])

        P.flush("ph1")
        ph1m.close()

        ph2 = ExitStack()
        v_tm2 = ph2.enter_context(nc.sbuf_tensor("v_tm2", [128, NT * 1024], BF16))[:]
        ascr_h = ph2.enter_context(nc.sbuf_tensor("ascr", [128, 2 * 1024], F32))
        ascr = ascr_h[:]
        borow = ph2.enter_context(nc.sbuf_tensor("borow", [1, D], BF16))
        q_fm2 = stream_bf[:, 0:16384]
        k_fm2 = stream_bf[:, 16384:32768]
        h_fm = fmbuf
        d_sp = P.sem("d_sp"); d_rl = P.sem("d_rl"); d_fb = P.sem("d_fb")
        d_fwo = P.sem("d_fwo")
        lnr = P.sem("lnr"); hfm = P.sem("hfm")

        sq_last = 0
        for i in range(NT):
            rw = [(pl_sem, a1_done)]
            v_op(lambda Eng, i=i: Eng.reduce_sum(
                colA[:, 0:1], stream[:, 1024 * i:1024 * (i + 1)], axis=AX.X),
                waits=rw, guard=True)
            s_op(lambda Eng, i=i: Eng.activation(
                out=junkb[:, 0:1024], in_=stream[:, 1024 * i:1024 * (i + 1)],
                func=Act.Square, accum_out=colA[:, 1:2]),
                waits=rw + [(vg, vg.n)], guard=True)
            sq_last = sg.n
            v_op(lambda Eng: Eng.tensor_scalar(
                out=colA[:, 2:3], in0=colA[:, 0:1], scalar1=1.0 / 1024, scalar2=None,
                op0=Alu.mult), guard=True)
            v_op(lambda Eng: Eng.tensor_scalar(
                out=colA[:, 3:4], in0=colA[:, 1:2], scalar1=1.0 / 1024, scalar2=None,
                op0=Alu.mult), waits=[(sg, sq_last)], guard=True)
            v_op(lambda Eng: Eng.tensor_tensor(
                out=colA[:, 4:5], in0=colA[:, 2:3], in1=colA[:, 2:3], op=Alu.mult),
                guard=True)
            v_op(lambda Eng: Eng.tensor_sub(
                colA[:, 5:6], colA[:, 3:4], colA[:, 4:5]), guard=True)
            v_var = vg.n
            s_op(lambda Eng: Eng.activation(
                out=colA[:, 6:7], in_=colA[:, 5:6], func=Act.Sqrt, bias=epsT[:, 1:2]),
                waits=[(vg, v_var)], guard=True)
            s_rt = sg.n
            v_op(lambda Eng: Eng.reciprocal(colA[:, 7:8], colA[:, 6:7]),
                 waits=[(sg, s_rt)], guard=True)
            v_op(lambda Eng, i=i: Eng.tensor_scalar(
                out=qn_sc[:, 0:1024], in0=stream[:, 1024 * i:1024 * (i + 1)],
                scalar1=colA[:, 2:3], scalar2=colA[:, 7:8],
                op0=Alu.subtract, op1=Alu.mult), guard=True, inc=(lnr, 1))
            nh_done = lnr.n
            for c in range(KD):
                tsb, tsf, tsw = TS.acquire()
                P.emit("tensor", lambda Eng, c=c, tsb=tsb: Eng.transpose(
                    tsb, qn_sc[:, 128 * c:128 * (c + 1)], identb[:]),
                    waits=[(lnr, nh_done)] + tsw, inc=(tT, 1))
                htr = tT.n
                v_op(lambda Eng, c=c, i=i, tsb=tsb: Eng.scalar_tensor_tensor(
                    out=h_fm[:, 2048 * c + 128 * i:2048 * c + 128 * (i + 1)],
                    in0=tsb, scalar=lnT[:, c:c + 1],
                    in1=lnT[:, 8 + c:9 + c].to_broadcast([128, 128]),
                    op0=Alu.mult, op1=Alu.add),
                    waits=[(tT, htr)], inc=(hfm, 1))
                TS.read(hfm)
        hfm_all = hfm.n

        dma("sync", d_sp, a1sp.rearrange("(n p) d -> p n d", p=128),
            stream.rearrange("p (n d) -> p n d", d=D),
            waits=[(lnr, lnr.n), (sg, sq_last)])

        ph2w = ExitStack()
        fwb = [ph2w.enter_context(nc.sbuf_tensor(f"fwb{j}", [128, 8 * 256], BF16))
               for j in range(2)]
        brow_qkv = [ph2w.enter_context(nc.sbuf_tensor(f"brqkv{j}", [1, D], BF16))
                    for j in range(3)]
        for j, bd in enumerate((fbq, fbk, fbv)):
            dma("gpsimd", d_fb, brow_qkv[j][:], bd.unsqueeze(0),
                waits=[(pl_sem, a1_done)] if j == 0 else ())
        dma("gpsimd", d_fb, borow[:], fbo.unsqueeze(0))
        fb_loaded = d_fb.n

        d_fq = [P.sem("d_fq0"), P.sem("d_fq1")]
        fqfree = P.sem("fqfree")
        wsrc = [fWq, fWk, fWv]
        fq_loads = {}

        def load_fw(idx):
            W, oc = wsrc[idx // 4], idx % 4
            w = [(pl_sem, a1_done)] if idx < 2 else [(fqfree, idx - 1)]
            P.emit("gpsimd", lambda Eng, W=W, oc=oc, idx=idx: Eng.dma_start(
                out=fwb[idx % 2][:].rearrange("p (c n) -> p c n", n=256),
                in_=W[:, 256 * oc:256 * (oc + 1)].rearrange("(c p) n -> p c n", p=128)),
                waits=w, inc=(d_fq[idx % 2], 16))
            fq_loads[idx] = (d_fq[idx % 2], d_fq[idx % 2].n)

        load_fw(0)
        load_fw(1)
        qfm2 = P.sem("qfm2"); kfm2 = P.sem("kfm2"); vtm2 = P.sem("vtm2")
        proj_ev = {}
        for idx in range(12):
            kind, oc = idx // 4, idx % 4
            for i in range(NT):
                n = idx * NT + i
                pq = pA[:, 512 * (n % 2):512 * (n % 2) + 256]
                w = [fq_loads[idx], (hfm, hfm_all), (d_sp, 16)]
                if n >= 2:
                    w.append(proj_ev[n - 2])
                for c in range(KD):
                    P.emit("tensor", lambda Eng, pq=pq, c=c, i=i, idx=idx: Eng.matmul(
                        pq,
                        h_fm[:, 2048 * c + 128 * i:2048 * c + 128 * (i + 1)],
                        fwb[idx % 2][:, 256 * c:256 * (c + 1)],
                        start=(c == 0), stop=False),
                        waits=w if c == 0 else ())
                last_inc = (fqfree, 1) if i == NT - 1 else (tT, 1)
                P.emit("tensor", lambda Eng, pq=pq, kind=kind, oc=oc: Eng.matmul(
                    pq, ones_bf[0:1, :],
                    brow_qkv[kind][0:1, 256 * oc:256 * (oc + 1)],
                    start=False, stop=True),
                    waits=[(d_fb, fb_loaded)], inc=last_inc)
                mm_wait = (fqfree, (idx // 4) * 0 + (fqfree.n)) if i == NT - 1 else (tT, tT.n)
                if kind == 2:
                    s_op(lambda Eng, pq=pq, i=i, oc=oc: Eng.copy(
                        out=v_tm2[:, 1024 * i + 256 * oc:1024 * i + 256 * (oc + 1)],
                        in_=pq), waits=[mm_wait], inc=(vtm2, 1))
                    proj_ev[n] = (vtm2, vtm2.n)
                else:
                    sc1 = 0.125 if kind == 0 else 1.0
                    v_op(lambda Eng, pq=pq, sc1=sc1: Eng.tensor_scalar(
                        out=qn_sc[:, 0:256], in0=pq,
                        scalar1=sc1, scalar2=None, op0=Alu.mult),
                        waits=[mm_wait, (hfm, hfm_all)], guard=True)
                    proj_ev[n] = (vg, vg.n)
                    dest = q_fm2 if kind == 0 else k_fm2
                    dsem2 = qfm2 if kind == 0 else kfm2
                    for b in range(2):
                        ch = 2 * oc + b
                        tsb, tsf, tsw = TS.acquire()
                        P.emit("tensor", lambda Eng, b=b, tsb=tsb: Eng.transpose(
                            tsb, qn_sc[:, 128 * b:128 * (b + 1)], identb[:]),
                            waits=[(vg, vg.n)] + tsw, inc=(tT, 1))
                        qtr = tT.n
                        v_op(lambda Eng, dest=dest, ch=ch, i=i, tsb=tsb:
                             Eng.tensor_copy(
                                 out=dest[:, 2048 * ch + 128 * i:
                                          2048 * ch + 128 * (i + 1)],
                                 in_=tsb),
                             waits=[(tT, qtr)], inc=(dsem2, 1))
                        TS.read(dsem2)
            if idx + 2 < 12:
                load_fw(idx + 2)
        qfm2_all = qfm2.n
        kfm2_all = kfm2.n
        vtm2_all = vtm2.n

        tap("h_fm", h_fm, [(hfm, hfm_all)])
        tap("qf2", q_fm2, [(qfm2, qfm2_all)])
        tap("kf2", k_fm2, [(kfm2, kfm2_all)])
        tap("vt2", v_tm2, [(vtm2, vtm2_all)])
        if upto == "fproj":
            P.wait("sync", (qfm2, qfm2_all), (kfm2, kfm2_all), (vtm2, vtm2_all))
            P.emit("sync", lambda Eng: Eng.dma_start(
                out=y_d[0:128, :], in_=stream[:, 0:1024]),
                waits=[(qfm2, qfm2_all)], inc=(dO, 16))
            P.wait("sync", (dO, dO.n))
            P.wait("gpsimd", (dT, dT.n))
            P.flush("tail")
            ph2w.close()
            ph2.close()
            ph1.close()
            return nc, tap_d

        o_fm_ffn = fmbuf
        oevF, oevF_n, strF, bmmF = attention(
            "ff", 16, 64, 64, q_fm2, k_fm2, v_tm2, o_fm_ffn, 1024,
            prereq=[(qfm2, qfm2_all), (kfm2, kfm2_all), (vtm2, vtm2_all)])

        tap("o_ffn", o_fm_ffn, [(oevF, oevF_n)])

        dma("sync", d_rl, stream.rearrange("p (n d) -> p n d", d=D),
            a1sp.rearrange("(n p) d -> p n d", p=128),
            waits=[(strF, strF.n), (bmmF, bmmF.n), (oevF, oevF_n)])

        ph2w.close()
        ph2wo = ExitStack()
        fwo = [ph2wo.enter_context(nc.sbuf_tensor(f"fwo{j}", [128, 8 * 256], BF16))
               for j in range(4)]
        for oc in range(4):
            P.emit("gpsimd", lambda Eng, oc=oc: Eng.dma_start(
                out=fwo[oc][:].rearrange("p (c n) -> p c n", n=256),
                in_=fWo[:, 256 * oc:256 * (oc + 1)].rearrange("(c p) n -> p c n", p=128)),
                waits=[(fqfree, 12), (oevF, oevF_n)] if oc == 0 else (),
                inc=(d_fwo, 16))
        fwo_loaded = d_fwo.n

        woev2 = P.sem("woev2"); ln2r = P.sem("ln2r"); h2f = P.sem("h2f")
        wfmd = P.sem("wfmd"); bfold = P.sem("bfold")
        wo_ev = {}
        rt_ev = None
        e8_cnt = 0
        for i in range(NT):
            half = 1024 * (i % 2)
            for oc in range(4):
                n = 4 * i + oc
                pq = pB[n % 2]
                w = [(oevF, oevF_n), (d_fwo, fwo_loaded)]
                if n >= 2:
                    w.append(wo_ev[n - 2])
                for g in range(KD):
                    P.emit("tensor", lambda Eng, pq=pq, g=g, i=i, oc=oc: Eng.matmul(
                        pq[:, 0:256],
                        o_fm_ffn[:, 2048 * g + 128 * i:2048 * g + 128 * (i + 1)],
                        fwo[oc][:, 256 * g:256 * (g + 1)],
                        start=(g == 0), stop=False),
                        waits=w if g == 0 else ())
                P.emit("tensor", lambda Eng, pq=pq, oc=oc: Eng.matmul(
                    pq[:, 0:256], ones_bf[0:1, :],
                    borow[0:1, 256 * oc:256 * (oc + 1)],
                    start=False, stop=True),
                    waits=[(d_fb, fb_loaded)], inc=(tT, 1))
                wo_mm = tT.n
                v_op(lambda Eng, pq=pq, i=i, oc=oc, half=half: Eng.tensor_tensor(
                    out=ascr[:, half + 256 * oc:half + 256 * (oc + 1)],
                    in0=pq[:, 0:256],
                    in1=stream[:, 1024 * i + 256 * oc:1024 * i + 256 * (oc + 1)],
                    op=Alu.add),
                    waits=[(tT, wo_mm), (d_rl, 16)], inc=(woev2, 1))
                wo_ev[n] = (woev2, woev2.n)
            a_done = woev2.n
            v_op(lambda Eng, half=half: Eng.reduce_sum(
                colA[:, 0:1], ascr[:, half:half + 1024], axis=AX.X),
                waits=[(woev2, a_done)], guard=True)
            s_op(lambda Eng, half=half: Eng.activation(
                out=junkb[:, 0:1024], in_=ascr[:, half:half + 1024],
                func=Act.Square, accum_out=colA[:, 1:2]),
                waits=[(woev2, a_done), (vg, vg.n)], guard=True)
            sq2 = sg.n
            v_op(lambda Eng: Eng.tensor_scalar(
                out=colA[:, 2:3], in0=colA[:, 0:1], scalar1=1.0 / 1024, scalar2=None,
                op0=Alu.mult), guard=True)
            v_op(lambda Eng: Eng.tensor_scalar(
                out=colA[:, 3:4], in0=colA[:, 1:2], scalar1=1.0 / 1024, scalar2=None,
                op0=Alu.mult), waits=[(sg, sq2)], guard=True)
            v_op(lambda Eng: Eng.tensor_tensor(
                out=colA[:, 4:5], in0=colA[:, 2:3], in1=colA[:, 2:3], op=Alu.mult),
                guard=True)
            v_op(lambda Eng: Eng.tensor_sub(
                colA[:, 5:6], colA[:, 3:4], colA[:, 4:5]), guard=True)
            v_var = vg.n
            s_op(lambda Eng: Eng.activation(
                out=colA[:, 6:7], in_=colA[:, 5:6], func=Act.Sqrt, bias=epsT[:, 1:2]),
                waits=[(vg, v_var)], guard=True)
            s_rt = sg.n
            v_op(lambda Eng: Eng.reciprocal(colA[:, 7:8], colA[:, 6:7]),
                 waits=[(sg, s_rt)], guard=True)
            v_op(lambda Eng, half=half: Eng.tensor_scalar(
                out=qn_sc[:, 0:1024], in0=ascr[:, half:half + 1024],
                scalar1=colA[:, 2:3], scalar2=colA[:, 7:8],
                op0=Alu.subtract, op1=Alu.mult), guard=True, inc=(ln2r, 1))
            nh2_done = ln2r.n
            for c in range(KD):
                tsb, tsf, tsw = TS.acquire()
                P.emit("tensor", lambda Eng, c=c, tsb=tsb: Eng.transpose(
                    tsb, qn_sc[:, 128 * c:128 * (c + 1)], identb[:]),
                    waits=[(ln2r, nh2_done)] + tsw, inc=(tT, 1))
                htr = tT.n
                v_op(lambda Eng, c=c, i=i, tsb=tsb: Eng.scalar_tensor_tensor(
                    out=fmbuf[:, 2048 * c + 128 * i:2048 * c + 128 * (i + 1)],
                    in0=tsb, scalar=lnT[:, 16 + c:17 + c],
                    in1=lnT[:, 24 + c:25 + c].to_broadcast([128, 128]),
                    op0=Alu.mult, op1=Alu.add),
                    waits=[(tT, htr)], inc=(h2f, 1))
                TS.read(h2f)
            h2_tile = h2f.n
            w = [(h2f, h2_tile), (d_mc, d_mc.n)]
            if rt_ev is not None:
                w.append(rt_ev)
            for c in range(KD):
                P.emit("tensor", lambda Eng, c=c, i=i: Eng.matmul(
                    pB[2][:, 0:8],
                    fmbuf[:, 2048 * c + 128 * i:2048 * c + 128 * (i + 1)],
                    wrbf[:, 8 * c:8 * (c + 1)],
                    start=(c == 0), stop=False),
                    waits=w if c == 0 else ())
            P.emit("tensor", lambda Eng: Eng.matmul(
                pB[2][:, 0:8], ones_bf[0:1, :], brbf[:],
                start=False, stop=True), inc=(tT, 1))
            rt_mm = tT.n
            v_op(lambda Eng: Eng.tensor_copy(out=l8[:], in_=pB[2][:, 0:8]),
                 waits=[(tT, rt_mm), (sg, e8_cnt)], guard=True)
            rt_ev = (vg, vg.n)
            v_op(lambda Eng: Eng.reduce_max(t8a[:, 0:1], l8[:], axis=AX.X, negate=True),
                 guard=True)
            v_op(lambda Eng: Eng.tensor_scalar(
                out=t8b[:], in0=l8[:], scalar1=t8a[:, 0:1], scalar2=0.0,
                op0=Alu.add, op1=Alu.is_equal), guard=True)
            v_op(lambda Eng: Eng.scalar_tensor_tensor(
                out=t8b[:], in0=t8b[:], scalar=-1e30, in1=l8[:],
                op0=Alu.mult, op1=Alu.add), guard=True)
            v_op(lambda Eng: Eng.reduce_max(t8a[:, 1:2], t8b[:], axis=AX.X, negate=True),
                 guard=True)
            v_op(lambda Eng: Eng.tensor_scalar(
                out=t8b[:], in0=l8[:], scalar1=t8a[:, 1:2], scalar2=0.0,
                op0=Alu.add, op1=Alu.is_ge), guard=True)
            v_km = vg.n
            s_op(lambda Eng: Eng.activation(
                out=sA[:, 0:8], in_=l8[:], func=Act.Exp, bias=t8a[:, 0:1]),
                waits=[(vg, v_km)], guard=True)
            e8_cnt = sg.n
            v_op(lambda Eng: Eng.tensor_tensor(
                out=sA[:, 0:8], in0=sA[:, 0:8], in1=t8b[:], op=Alu.mult),
                waits=[(sg, e8_cnt)], guard=True)
            v_op(lambda Eng: Eng.reduce_sum(t8a[:, 2:3], sA[:, 0:8], axis=AX.X),
                 guard=True)
            v_op(lambda Eng: Eng.reciprocal(t8a[:, 3:4], t8a[:, 2:3]), guard=True)
            v_op(lambda Eng, i=i: Eng.tensor_scalar(
                out=wcol[:, 8 * i:8 * (i + 1)], in0=sA[:, 0:8],
                scalar1=t8a[:, 3:4], scalar2=None, op0=Alu.mult), guard=True)
            v_op(lambda Eng, i=i: Eng.tensor_copy(
                out=w8b[:], in_=wcol[:, 8 * i:8 * (i + 1)]), guard=True)
            v_w8 = vg.n
            tsb, tsf, tsw = TS.acquire()
            P.emit("tensor", lambda Eng, tsb=tsb: Eng.transpose(
                tsb[0:8, 0:128], w8b[:], identb[:]),
                waits=[(vg, v_w8)] + tsw, inc=(tT, 1))
            wtr = tT.n
            v_op(lambda Eng, i=i, tsb=tsb: Eng.tensor_copy(
                out=wfm[0:8, 128 * i:128 * (i + 1)], in_=tsb[0:8, 0:128]),
                waits=[(tT, wtr)], inc=(wfmd, 1))
            TS.read(wfmd)
            v_op(lambda Eng, i=i, half=half: Eng.tensor_tensor(
                out=stream[:, 1024 * i:1024 * (i + 1)],
                in0=ascr[:, half:half + 1024],
                in1=stream[:, 1024 * i:1024 * (i + 1)], op=Alu.add),
                guard=True, inc=(bfold, 1))
        bfold_all = bfold.n
        wfm_all = wfmd.n

        tap("B", stream, [(bfold, bfold_all)])
        tap("wcolt", wcol[:], [(bfold, bfold_all)])
        tap("h2fm", fmbuf, [(h2f, h2f.n)])

        P.flush("ph2")

        ph2wo.close()
        ph2.close()
        ph1.close()
        ph3 = ExitStack()
        mid_sb = ph3.enter_context(nc.sbuf_tensor("mid_sb", [128, 32 * 1024], BF16))[:]
        w1b = [ph3.enter_context(nc.sbuf_tensor(f"w1b{j}", [128, 8 * 256], BF16))
               for j in range(2)]
        w2b = [ph3.enter_context(nc.sbuf_tensor(f"w2b{j}", [128, 16 * 256], BF16))
               for j in range(2)]
        ysc = ph3.enter_context(nc.sbuf_tensor("ysc", [128, 2 * 1024], F32))[:]

        h2_fm = fmbuf
        pMF = pMisc_h.ap().bitcast(F32)
        eo_tile = [pA[:, 0:512], pA[:, 512:1024], pA[:, 1024:1536],
                   pA[:, 1536:2048], pB[0][:], pB[1][:], pB[2][:],
                   pMF[:, 0:512]]

        d_w1 = [P.sem("d_w10"), P.sem("d_w11")]
        d_w2 = [P.sem("d_w20"), P.sem("d_w21")]
        w1free = P.sem("w1free"); w2free = P.sem("w2free")
        midr = P.sem("midr"); eor = P.sem("eor")
        wb2d = P.sem("wb2d"); yout = P.sem("yout")
        dOp = [P.sem("dOp0"), P.sem("dOp1")]

        for i in range(NT):
            for hf2 in range(2):
                P.emit("tensor", lambda Eng, i=i, hf2=hf2: Eng.matmul(
                    pB[0][:, 0:512], wfm[0:8, 128 * i:128 * (i + 1)],
                    b2bf[0:8, 512 * hf2:512 * (hf2 + 1)],
                    start=True, stop=True),
                    waits=[(wfmd, wfm_all), (d_mc, d_mc.n), (bfold, bfold_all),
                           (wb2d, wb2d.n)],
                    inc=(tT, 1))
                wbm = tT.n
                v_op(lambda Eng, i=i, hf2=hf2: Eng.tensor_tensor(
                    out=stream[:, 1024 * i + 512 * hf2:1024 * i + 512 * (hf2 + 1)],
                    in0=pB[0][:, 0:512],
                    in1=stream[:, 1024 * i + 512 * hf2:1024 * i + 512 * (hf2 + 1)],
                    op=Alu.add),
                    waits=[(tT, wbm)], inc=(wb2d, 1))
        wb2_all = wb2d.n

        w1_loads = {}
        w2_loads = {}
        n_w1 = [0]
        n_w2 = [0]

        def load_w1(e, tc, g):
            idx = n_w1[0]; n_w1[0] += 1
            w = [(wb2d, wb2_all)] if idx < 2 else [(w1free, idx - 1)]
            P.emit("gpsimd", lambda Eng, e=e, g=g, idx=idx: Eng.dma_start(
                out=w1b[idx % 2][:].rearrange("p (c n) -> p c n", n=256),
                in_=ew1[e][:, 256 * g:256 * (g + 1)].rearrange(
                    "(c p) n -> p c n", p=128)),
                waits=w, inc=(d_w1[idx % 2], 16))
            w1_loads[(e, tc, g)] = (d_w1[idx % 2], d_w1[idx % 2].n)

        def load_w2(e, tc, dq, fh):
            idx = n_w2[0]; n_w2[0] += 1
            w = [(wb2d, wb2_all)] if idx < 2 else [(w2free, idx - 1)]
            P.emit("gpsimd", lambda Eng, e=e, dq=dq, fh=fh, idx=idx: Eng.dma_start(
                out=w2b[idx % 2][:].rearrange("p (c n) -> p c n", n=256),
                in_=ew2[e][2048 * fh:2048 * (fh + 1),
                           256 * dq:256 * (dq + 1)].rearrange(
                    "(c p) n -> p c n", p=128)),
                waits=w, inc=(d_w2[idx % 2], 16))
            w2_loads[(e, tc, dq, fh)] = (d_w2[idx % 2], d_w2[idx % 2].n)

        load_w1(0, 0, 0)
        load_w1(0, 0, 1)
        load_w2(0, 0, 0, 0)
        load_w2(0, 0, 0, 1)

        mid_free = {}
        n_mid = [0]
        eo_ev = {t: None for t in range(8)}
        tile_done = {}

        E_RUN = 1 if upto == "moe1" else E
        for e in range(E_RUN):
            for tc in range(2):
                for g in range(16):
                    for fc2 in range(2):
                        fc = 2 * g + fc2
                        acq = n_mid[0]; n_mid[0] += 1
                        pm = pA[:, 1024 * (acq % 2):1024 * (acq % 2) + 1024]
                        w = [w1_loads[(e, tc, g)], (h2f, h2f.n),
                             (wb2d, wb2_all)]
                        if acq >= 2:
                            w.append(mid_free[acq - 2])
                        for t_eo in (2 * (acq % 2), 2 * (acq % 2) + 1):
                            if eo_ev[t_eo] is not None:
                                w.append(eo_ev[t_eo])
                        for k in range(KD):
                            for tq in range(2):
                                P.emit("tensor", lambda Eng, pm=pm, k=k, tq=tq,
                                       fc2=fc2, tc=tc, e=e, g=g: Eng.matmul(
                                    pm[:, 512 * tq:512 * (tq + 1)],
                                    w1b[(w1_loads[(e, tc, g)][0] is d_w1[1]) * 1][
                                        :, 256 * k + 128 * fc2:
                                        256 * k + 128 * (fc2 + 1)],
                                    h2_fm[:, 2048 * k + 1024 * tc + 512 * tq:
                                          2048 * k + 1024 * tc + 512 * (tq + 1)],
                                    start=(k == 0), stop=(k == KD - 1)),
                                    waits=w if (k == 0 and tq == 0) else (),
                                    inc=((w1free, 1) if fc2 == 1 else (midr, 1))
                                    if (k == KD - 1 and tq == 1) else None)
                        mmark = (w1free, w1free.n) if fc2 == 1 else (midr, midr.n)
                        s_op(lambda Eng, pm=pm, fc=fc, e=e: Eng.activation(
                            out=mid_sb[:, 1024 * fc:1024 * (fc + 1)],
                            in_=pm[:, 0:1024], func=Act.Gelu,
                            bias=b1T[:, 32 * e + fc:32 * e + fc + 1]),
                            waits=[mmark], guard=True)
                        mid_free[acq] = (sg, sg.n)
                    nxt = (e, tc, g + 2)
                    if g + 2 >= 16:
                        nxt = (e, tc + 1, g + 2 - 16) if tc == 0 else \
                              (e + 1, 0, g + 2 - 16)
                    if nxt[0] < E and nxt not in w1_loads:
                        load_w1(*nxt)
                mid_ready = (sg, mid_free[n_mid[0] - 1][1])
                for dq in range(4):
                    for fh in range(2):
                        wl = w2_loads[(e, tc, dq, fh)]
                        for fcl in range(16):
                            fc = 16 * fh + fcl
                            for tsub in range(8):
                                w = []
                                if fcl == 0 and fh == 0:
                                    w = [mid_ready, wl]
                                    if eo_ev[tsub] is not None:
                                        w.append(eo_ev[tsub])
                                elif fcl == 0:
                                    w = [wl]
                                inc = None
                                if fh == 1 and fcl == 15:
                                    inc = (w2free, 1) if tsub == 7 else (eor, 1)
                                elif fh == 0 and fcl == 15 and tsub == 7:
                                    inc = (w2free, 1)
                                P.emit("tensor", lambda Eng, fc=fc, tsub=tsub,
                                       fcl=fcl, dq=dq, wl=wl: Eng.matmul(
                                    eo_tile[tsub][:, 256 * (dq % 2):
                                                  256 * (dq % 2) + 256],
                                    mid_sb[:, 1024 * fc + 128 * tsub:
                                           1024 * fc + 128 * (tsub + 1)],
                                    w2b[(wl[0] is d_w2[1]) * 1][
                                        :, 256 * fcl:256 * (fcl + 1)],
                                    start=(fc == 0), stop=(fc == 31)),
                                    waits=w, inc=inc)
                        nxt = (e, tc, dq, fh + 1)
                        if nxt[3] >= 2:
                            nxt = (e, tc, dq + 1, 0)
                        if nxt[2] >= 4:
                            nxt = (e, tc + 1, 0, 0) if tc == 0 else (e + 1, 0, 0, 0)
                        if nxt[0] < E and (nxt not in w2_loads):
                            load_w2(*nxt)
                    eor_base = eor.n
                    w2_last = (w2free, w2free.n)
                    for tsub in range(8):
                        ti = 8 * tc + tsub
                        if tsub == 7:
                            stop_w = [w2_last]
                        else:
                            stop_w = [(eor, eor_base - 7 + tsub + 1)]
                        v_op(lambda Eng, t=tsub, ti=ti, dq=dq, e=e: Eng.scalar_tensor_tensor(
                            out=stream[:, 1024 * ti + 256 * dq:
                                       1024 * ti + 256 * (dq + 1)],
                            in0=eo_tile[t][:, 256 * (dq % 2):256 * (dq % 2) + 256],
                            scalar=wcol[:, 8 * ti + e:8 * ti + e + 1],
                            in1=stream[:, 1024 * ti + 256 * dq:
                                       1024 * ti + 256 * (dq + 1)],
                            op0=Alu.mult, op1=Alu.add),
                            waits=stop_w, guard=True)
                        eo_ev[tsub] = (vg, vg.n)
                        tile_done[ti] = (vg, vg.n)

        tap("preln", stream, [tile_done[i] for i in range(NT)])
        tap("mids", mid_sb, [mid_free[n_mid[0] - 1]])

        for i in range(NT):
            w = [tile_done[i]]
            v_op(lambda Eng, i=i: Eng.reduce_sum(
                colA[:, 0:1], stream[:, 1024 * i:1024 * (i + 1)], axis=AX.X),
                waits=w, guard=True)
            s_op(lambda Eng, i=i: Eng.activation(
                out=junkb[:, 0:1024], in_=stream[:, 1024 * i:1024 * (i + 1)],
                func=Act.Square, accum_out=colA[:, 1:2]),
                waits=w + [(vg, vg.n)], guard=True)
            sqf = sg.n
            v_op(lambda Eng: Eng.tensor_scalar(
                out=colA[:, 2:3], in0=colA[:, 0:1], scalar1=1.0 / 1024, scalar2=None,
                op0=Alu.mult), guard=True)
            v_op(lambda Eng: Eng.tensor_scalar(
                out=colA[:, 3:4], in0=colA[:, 1:2], scalar1=1.0 / 1024, scalar2=None,
                op0=Alu.mult), waits=[(sg, sqf)], guard=True)
            v_op(lambda Eng: Eng.tensor_tensor(
                out=colA[:, 4:5], in0=colA[:, 2:3], in1=colA[:, 2:3], op=Alu.mult),
                guard=True)
            v_op(lambda Eng: Eng.tensor_sub(
                colA[:, 5:6], colA[:, 3:4], colA[:, 4:5]), guard=True)
            v_var = vg.n
            s_op(lambda Eng: Eng.activation(
                out=colA[:, 6:7], in_=colA[:, 5:6], func=Act.Sqrt, bias=epsT[:, 1:2]),
                waits=[(vg, v_var)], guard=True)
            s_rt = sg.n
            v_op(lambda Eng: Eng.reciprocal(colA[:, 7:8], colA[:, 6:7]),
                 waits=[(sg, s_rt)], guard=True)
            yw = [] if i < 2 else [(dOp[i % 2], 16 * (i // 2))]
            v_op(lambda Eng, i=i: Eng.tensor_scalar(
                out=ysc[:, 1024 * (i % 2):1024 * (i % 2 + 1)],
                in0=stream[:, 1024 * i:1024 * (i + 1)],
                scalar1=colA[:, 2:3], scalar2=colA[:, 7:8],
                op0=Alu.subtract, op1=Alu.mult),
                waits=yw, guard=True, inc=(yout, 1))
            yv = yout.n
            P.emit("sync", lambda Eng, i=i: Eng.dma_start(
                out=y_d[128 * i:128 * (i + 1), :],
                in_=ysc[:, 1024 * (i % 2):1024 * (i % 2 + 1)]),
                waits=[(yout, yv)], inc=(dOp[i % 2], 16))
        P.wait("sync", (dOp[0], dOp[0].n), (dOp[1], dOp[1].n))
        P.wait("gpsimd", (dT, dT.n))
        P.flush("ph3")
        ph3.close()
        phP.close()

    return nc, tap_d



_IN_MAP_KEYS = {
    "aWq": "L0_attn_Wq", "aWkv": "L0_attn_Wkv", "aWk": "L0_attn_Wk",
    "aWv": "L0_attn_Wv", "aWo": "L0_attn_Wo", "qg": "L0_attn_q_g",
    "kg": "L0_attn_k_g", "g1": "L0_ffn_ln1_g", "bb1": "L0_ffn_ln1_b",
    "g2": "L0_ffn_ln2_g", "bb2": "L0_ffn_ln2_b",
    "fWq": "L0_ffn_Wq", "fbq": "L0_ffn_bq", "fWk": "L0_ffn_Wk",
    "fbk": "L0_ffn_bk", "fWv": "L0_ffn_Wv", "fbv": "L0_ffn_bv",
    "fWo": "L0_ffn_Wo", "fbo": "L0_ffn_bo", "Wr": "L0_ffn_Wr",
    "br": "L0_ffn_br", "ew1": "L0_ffn_e_w1", "eb1": "L0_ffn_e_b1",
    "ew2": "L0_ffn_e_w2", "eb2": "L0_ffn_e_b2",
}


def make_in_maps(inputs):
    xs = np.ascontiguousarray(inputs["L0_attn_x"])
    base = {k: np.ascontiguousarray(inputs[v]) for k, v in _IN_MAP_KEYS.items()}
    return [dict(base, x=np.ascontiguousarray(xs[i])) for i in range(NCORES)]


def kernel(**inputs):
    nc, _ = build_nc(taps=())
    res = run_bass_kernel_spmd(nc, make_in_maps(inputs), core_ids=list(range(NCORES)))
    return np.stack([r["y"] for r in res.results]).astype(np.float32)


# version 09
# speedup vs baseline: 1.0980x; 1.0980x over previous
import numpy as np
from contextlib import ExitStack

import concourse.bass as bass
import concourse.mybir as mybir
from concourse.bass_utils import run_bass_kernel_spmd

F32 = mybir.dt.float32
BF16 = mybir.dt.bfloat16
Act = mybir.ActivationFunctionType
Alu = mybir.AluOpType
AX = mybir.AxisListType

T, D = 2048, 1024
NT = T // 128
KD = D // 128
MH, ROPE, LAT = 16, 32, 128
E, FF = 8, 4096
NCORES = 8

ENGINES = ("tensor", "vector", "scalar", "gpsimd", "sync")


class Sem:
    __slots__ = ("h", "n", "name")

    def __init__(self, h, name):
        self.h = h
        self.n = 0
        self.name = name


class Prog:
    def __init__(self, nc, stack):
        self.nc = nc
        self.stack = stack
        self.q = {e: [] for e in ENGINES}
        self.lastwait = {e: {} for e in ENGINES}
        self.lastinc = {e: None for e in ENGINES}

    def sem(self, name):
        h = self.stack.enter_context(self.nc.semaphore(name))
        return Sem(h, name)

    def emit(self, eng, fn, waits=(), inc=None):
        wl = []
        for s, v in waits:
            if v <= 0:
                continue
            assert v <= s.n, f"wait {v} > emitted {s.n} on {s.name} (engine {eng})"
            if self.lastwait[eng].get(s.name, -1) < v:
                self.lastwait[eng][s.name] = v
                wl.append((s.h, v))
        il = None
        if inc is not None:
            s, k = inc
            s.n += k
            il = (s.h, k)
            self.lastinc[eng] = (s, s.n)

        def thunk(Eng, wl=wl, il=il, fn=fn):
            for h, v in wl:
                Eng.wait_ge(h, v)
            ins = fn(Eng)
            if il is not None:
                assert ins is not None
                ins.then_inc(*il)

        self.q[eng].append(thunk)

    def wait(self, eng, *waits):
        self.emit(eng, lambda Eng: None, waits=waits)

    def flush(self, name=None):
        qs = {e: fns for e, fns in self.q.items() if fns}
        if qs:
            with self.nc.Block(name) as block:
                for e, fns in qs.items():
                    getattr(block, e)(lambda Eng, fns=fns: [f(Eng) for f in fns])
        self.q = {e: [] for e in ENGINES}


def build_nc(taps=(), upto=None):
    nc = bass.Bass("TRN2", target_bir_lowering=False)
    d_in = {}

    def din(name, shape):
        d_in[name] = nc.dram_tensor(name, shape, F32, kind="ExternalInput").ap()
        return d_in[name]

    x_d = din("x", [T, D])
    aWq = din("aWq", [D, D]); aWkv = din("aWkv", [D, LAT])
    aWk = din("aWk", [LAT, MH * ROPE]); aWv = din("aWv", [LAT, MH * 32])
    aWo = din("aWo", [D, D])
    qg_d = din("qg", [64]); kg_d = din("kg", [ROPE])
    g1_d = din("g1", [D]); bb1_d = din("bb1", [D])
    g2_d = din("g2", [D]); bb2_d = din("bb2", [D])
    fWq = din("fWq", [D, D]); fbq = din("fbq", [D])
    fWk = din("fWk", [D, D]); fbk = din("fbk", [D])
    fWv = din("fWv", [D, D]); fbv = din("fbv", [D])
    fWo = din("fWo", [D, D]); fbo = din("fbo", [D])
    Wr_d = din("Wr", [D, E]); br_d = din("br", [E])
    ew1 = din("ew1", [E, D, FF]); eb1 = din("eb1", [E, FF])
    ew2 = din("ew2", [E, FF, D]); eb2 = din("eb2", [E, D])

    y_d = nc.dram_tensor("y", [T, D], F32, kind="ExternalOutput").ap()
    a1sp = nc.dram_tensor("a1sp", [T, D], F32, kind="Internal").ap()

    taps = list(taps)
    tap_d = {}

    with ExitStack() as st:
        ec = st.enter_context

        stream_h = ec(nc.sbuf_tensor("stream", [128, NT * D], F32))
        fmbuf_h = ec(nc.sbuf_tensor("fmbuf", [128, KD * T], BF16))
        stream = stream_h[:]
        fmbuf = fmbuf_h[:]
        stream_bf = stream_h.ap().bitcast(BF16)

        identb = ec(nc.sbuf_tensor("identb", [128, 128], BF16))
        identf = ec(nc.sbuf_tensor("identf", [128, 128], F32))
        lnT = ec(nc.sbuf_tensor("lnT", [128, 32], F32))
        b1T = ec(nc.sbuf_tensor("b1T", [128, E * 32], F32))
        b2bf = ec(nc.sbuf_tensor("b2bf", [8, D], BF16))
        wrbf = ec(nc.sbuf_tensor("wrbf", [128, KD * E], BF16))
        brbf = ec(nc.sbuf_tensor("brbf", [1, E], BF16))
        wcol = ec(nc.sbuf_tensor("wcol", [128, NT * E], F32))
        wfm = ec(nc.sbuf_tensor("wfm", [8, T], BF16))
        ones_bf = ec(nc.sbuf_tensor("ones_bf", [128, 128], BF16))
        stage = ec(nc.sbuf_tensor("stage", [128, 128], F32))
        junkb = ec(nc.sbuf_tensor("junkb", [128, 2 * 1024], BF16))
        sA = ec(nc.sbuf_tensor("sA", [128, 16], F32))
        sB = ec(nc.sbuf_tensor("sB", [128, 16], F32))
        sC = ec(nc.sbuf_tensor("sC", [128, 16], F32))
        colA = ec(nc.sbuf_tensor("colA", [128, 8], F32))
        m16 = ec(nc.sbuf_tensor("m16", [128, 1], BF16))
        nm32 = ec(nc.sbuf_tensor("nm32", [128, 1], F32))
        nm16 = ec(nc.sbuf_tensor("nm16", [128, 1], BF16))
        sr16 = ec(nc.sbuf_tensor("sr16", [128, 1], BF16))
        nmS = ec(nc.sbuf_tensor("nmS", [128, 16], BF16))[:]
        l8 = ec(nc.sbuf_tensor("l8", [128, 8], F32))
        t8a = ec(nc.sbuf_tensor("t8a", [128, 8], F32))
        t8b = ec(nc.sbuf_tensor("t8b", [128, 8], F32))
        w8b = ec(nc.sbuf_tensor("w8b", [128, 8], BF16))
        qgcol = ec(nc.sbuf_tensor("qgcol", [128, 1], F32))
        kgcol = ec(nc.sbuf_tensor("kgcol", [128, 1], F32))
        epsT = ec(nc.sbuf_tensor("epsT", [128, 2], F32))

        P = Prog(nc, st)
        dS = P.sem("dS"); dO = P.sem("dO"); dT = P.sem("dT")
        d_ln = P.sem("d_ln"); d_b1 = P.sem("d_b1"); d_gn = P.sem("d_gn")
        d_mc = P.sem("d_mc"); d_aw = P.sem("d_aw"); d_wo = P.sem("d_wo")
        vg = P.sem("vg"); sg = P.sem("sg"); gG = P.sem("gG"); tT = P.sem("tT")
        gg = P.sem("gg")

        def dma(eng, sem, dst, src, waits=()):
            P.emit(eng, lambda Eng, d=dst, s=src: Eng.dma_start(out=d, in_=s),
                   waits=waits, inc=(sem, 16))

        def v_op(fn, waits=(), guard=False, inc=None):
            w = list(waits)
            if guard and P.lastinc["vector"] is not None:
                w.append(P.lastinc["vector"])
            P.emit("vector", fn, waits=w, inc=inc if inc is not None else (vg, 1))

        def s_op(fn, waits=(), guard=False, inc=None):
            w = list(waits)
            if guard and P.lastinc["scalar"] is not None:
                w.append(P.lastinc["scalar"])
            P.emit("scalar", fn, waits=w, inc=inc if inc is not None else (sg, 1))

        def g_op(fn, inc=None, waits=(), guard=False):
            w = list(waits)
            if guard and P.lastinc["gpsimd"] is not None:
                w.append(P.lastinc["gpsimd"])
            P.emit("gpsimd", fn, waits=w, inc=inc if inc is not None else (gg, 1))

        def tap(name, ap, waits):
            if name in taps:
                sh = list(ap.shape)
                tap_d[name] = nc.dram_tensor("tap_" + name, sh, F32,
                                             kind="ExternalOutput").ap()
                dma("gpsimd", dT, tap_d[name], ap, waits=waits)

        ph0 = ExitStack()
        pE = ph0.enter_context(nc.psum_tensor("pE", [128, 512], F32))
        pE2 = ph0.enter_context(nc.psum_tensor("pE2", [128, 512], F32))

        dma("sync", dS, stream.rearrange("p (n d) -> p n d", d=D),
            x_d.rearrange("(n p) d -> p n d", p=128))
        x_loaded = dS.n

        for t_ in (identb, identf):
            g_op(lambda Eng, t_=t_: Eng.memset(t_[:], 0.0))
            g_op(lambda Eng, t_=t_: Eng.affine_select(
                out=t_[:], in_=t_[:], compare_op=Alu.not_equal, fill=1.0,
                base=0, pattern=[[-1, 128]], channel_multiplier=1), guard=True)
        g_op(lambda Eng: Eng.memset(ones_bf[:], 1.0))
        g_op(lambda Eng: Eng.memset(epsT[:, 0:1], 1e-6))
        g_op(lambda Eng: Eng.memset(epsT[:, 1:2], 1e-5), inc=(gG, 1))
        ident_ready = gG.n

        dma("sync", d_ln, stage[0:8, :], g1_d.rearrange("(c n) -> c n", n=128))
        dma("sync", d_ln, stage[8:16, :], bb1_d.rearrange("(c n) -> c n", n=128))
        dma("sync", d_ln, stage[16:24, :], g2_d.rearrange("(c n) -> c n", n=128))
        dma("sync", d_ln, stage[24:32, :], bb2_d.rearrange("(c n) -> c n", n=128))
        ln_staged = d_ln.n
        P.emit("tensor", lambda Eng: Eng.transpose(pE[:, 0:128][0:128, 0:32], stage[0:32, :], identf[0:32, 0:32]),
               waits=[(d_ln, ln_staged), (gG, ident_ready)], inc=(tT, 1))
        v_op(lambda Eng: Eng.tensor_copy(out=lnT[:], in_=pE[0:128, 0:32]),
             waits=[(tT, tT.n)])
        lnT_ready = vg.n

        for g in range(2):
            P.emit("sync", lambda Eng, g=g: Eng.dma_start(
                out=stage[:],
                in_=eb1[4 * g:4 * g + 4].rearrange("e (c n) -> (e c) n", n=128)),
                waits=[(vg, lnT_ready)] if g == 0 else [(vg, vg.n)], inc=(dS, 16))
            b1_staged = dS.n
            P.emit("tensor", lambda Eng: Eng.transpose(pE[:, 0:128], stage[:], identf[:]),
                   waits=[(dS, b1_staged), (vg, vg.n)], inc=(tT, 1))
            v_op(lambda Eng, g=g: Eng.tensor_copy(
                out=b1T[:, 128 * g:128 * g + 128], in_=pE[0:128, 0:128]),
                waits=[(tT, tT.n)])
        b1T_ready = vg.n

        dma("gpsimd", d_mc, b2bf[:], eb2)
        dma("gpsimd", d_mc, wrbf[:].rearrange("p (c e) -> p c e", e=E),
            Wr_d.rearrange("(c p) e -> p c e", p=128))
        dma("gpsimd", d_mc, brbf[:], br_d.unsqueeze(0))

        qgrow = ph0.enter_context(nc.sbuf_tensor("qgrow", [1, 128], F32))
        kgrow = ph0.enter_context(nc.sbuf_tensor("kgrow", [1, 128], F32))
        dma("sync", d_gn, stage[0:1, 0:64], qg_d.unsqueeze(0),
            waits=[(vg, b1T_ready)])
        dma("sync", d_gn, stage[32:33, 0:32], kg_d.unsqueeze(0))
        gains_staged = d_gn.n
        v_op(lambda Eng: Eng.tensor_scalar(
            out=qgrow[0:1, :].rearrange("a (j r) -> a j r", r=32),
            in0=stage[0:1, 0:32].unsqueeze(1).to_broadcast([1, 4, 32]),
            scalar1=0.125, scalar2=None, op0=Alu.mult),
            waits=[(d_gn, gains_staged)])
        v_op(lambda Eng: Eng.tensor_copy(
            out=kgrow[0:1, :].rearrange("a (j r) -> a j r", r=32),
            in_=stage[32:33, 0:32].unsqueeze(1).to_broadcast([1, 4, 32])))
        gains_rows = vg.n
        P.emit("tensor", lambda Eng: Eng.transpose(pE[:, 0:128][0:128, 0:1], qgrow[:], identf[0:1, 0:1]),
               waits=[(vg, gains_rows)], inc=(tT, 1))
        v_op(lambda Eng: Eng.tensor_copy(out=qgcol[:], in_=pE[0:128, 0:1]),
             waits=[(tT, tT.n)])
        P.emit("tensor", lambda Eng: Eng.transpose(pE2[:, 0:128][0:128, 0:1], kgrow[:], identf[0:1, 0:1]),
               inc=(tT, 1))
        v_op(lambda Eng: Eng.tensor_copy(out=kgcol[:], in_=pE2[0:128, 0:1]),
             waits=[(tT, tT.n)])
        gains_ready = vg.n

        x_fm = fmbuf
        trf = P.sem("trf")
        tslots = [pE[:, 0:128], pE2[:, 0:128], pE[:, 128:256], pE2[:, 128:256]]
        n_tr = 0
        for i in range(NT):
            for c in range(KD):
                slot = tslots[n_tr % 4]
                w = [(dS, x_loaded), (gG, ident_ready), (vg, gains_ready)]
                if n_tr >= 2:
                    w = [(trf, n_tr - 1)]
                P.emit("tensor", lambda Eng, s=slot, i=i, c=c: Eng.transpose(
                    s, stream[:, 1024 * i + 128 * c:1024 * i + 128 * (c + 1)], identf[:]),
                    waits=w, inc=(tT, 1))
                v_op(lambda Eng, s=slot, i=i, c=c: Eng.tensor_copy(
                    out=x_fm[:, 2048 * c + 128 * i:2048 * c + 128 * (i + 1)], in_=s),
                    waits=[(tT, tT.n)], inc=(trf, 1))
                n_tr += 1
        x_fm_done = trf.n

        tap("x_fm", x_fm, [(trf, x_fm_done)])
        P.flush("ph0")
        ph0.close()
        if upto == "ph0":
            for i in range(NT):
                P.emit("sync", lambda Eng, i=i: Eng.dma_start(
                    out=y_d[128 * i:128 * (i + 1), :],
                    in_=stream[:, 1024 * i:1024 * (i + 1)]),
                    waits=[(trf, x_fm_done)], inc=(dO, 16))
            P.wait("sync", (dO, dO.n))
            P.wait("gpsimd", (dT, dT.n))
            P.flush("tail")
            return nc, tap_d

        ph1 = ExitStack()
        mask128 = ph1.enter_context(nc.sbuf_tensor("mask128", [128, 128], F32))
        mask896 = ph1.enter_context(nc.sbuf_tensor("mask896", [128, 896], F32))
        negm_all = [ph1.enter_context(nc.sbuf_tensor(f"negma{j}", [128, T], BF16))
                    for j in range(2)]
        srow = ph1.enter_context(nc.sbuf_tensor("srow", [1, 512], BF16))
        srec_sb = ph1.enter_context(nc.sbuf_tensor("srec_sb", [128, 512], BF16))
        phat = ph1.enter_context(nc.sbuf_tensor("phat", [128, 3 * 512], BF16))
        qn_sc = ph1.enter_context(nc.sbuf_tensor("qn_sc", [128, 1024], BF16))

        ph1m = ExitStack()
        q_fm = ph1m.enter_context(nc.sbuf_tensor("q_fm", [128, 4 * T], BF16))[:]
        k_fm = ph1m.enter_context(nc.sbuf_tensor("k_fm", [128, 4 * T], BF16))[:]
        v_tm = ph1m.enter_context(nc.sbuf_tensor("v_tm", [128, NT * 528], BF16))[:]
        kv_fm = ph1m.enter_context(nc.sbuf_tensor("kv_fm", [128, T], BF16))[:]
        g_op(lambda Eng: Eng.memset(v_tm[:], 1.0), guard=True,
             waits=[(trf, x_fm_done)], inc=(gg, 1))
        vones1 = gg.n

        phP = ExitStack()
        pA = phP.enter_context(nc.psum_tensor("pA", [128, 2048], F32))
        pB = [phP.enter_context(nc.psum_tensor(f"pB{j}", [128, 512], F32))
              for j in range(3)]
        pMisc_h = phP.enter_context(nc.psum_tensor("pMisc", [128, 1024], BF16))
        pMisc = pMisc_h[:]
        pMiscF = pMisc_h.ap().bitcast(F32)
        pA_bf = pA.bitcast(BF16)

        class TransSlots:

            def __init__(self, slots):
                self.slots = slots
                self.last = [None] * len(slots)
                self.i = 0

            def acquire(self):
                j = self.i % len(self.slots)
                self.i += 1
                self.cur = j
                w = [self.last[j]] if self.last[j] is not None else []
                return self.slots[j][0], self.slots[j][1], w

            def read(self, sem):
                self.last[self.cur] = (sem, sem.n)

        TS = TransSlots([
            (pMisc[:, 0:128], pMiscF[:, 0:128]),
            (pA_bf[:, 2048:2176], pA[:, 1024:1152]),
            (pA_bf[:, 3072:3200], pA[:, 1536:1664]),
        ])

        g_op(lambda Eng: Eng.memset(mask128[:], 0.0))
        g_op(lambda Eng: Eng.affine_select(
            out=mask128[:], in_=mask128[:], compare_op=Alu.is_ge, fill=-1e30,
            base=0, pattern=[[-1, 128]], channel_multiplier=1), guard=True)
        g_op(lambda Eng: Eng.memset(mask896[:], 0.0))
        g_op(lambda Eng: Eng.affine_select(
            out=mask896[:], in_=mask896[:], compare_op=Alu.is_ge, fill=-1e30,
            base=-384, pattern=[[1, 896]], channel_multiplier=-1),
            guard=True, inc=(gG, 1))
        masks_ready = gG.n

        ph1w = ExitStack()
        wq_bufs = [ph1w.enter_context(nc.sbuf_tensor(f"awq{j}", [128, 8 * 256], BF16))
                   for j in range(2)]
        wkv_b = ph1w.enter_context(nc.sbuf_tensor("awkv", [128, 8 * 128], BF16))
        wk_b = ph1w.enter_context(nc.sbuf_tensor("awk", [128, 512], BF16))
        wv_b = ph1w.enter_context(nc.sbuf_tensor("awv", [128, 512], BF16))
        wqfree = P.sem("wqfree")
        d_wq = [P.sem("d_wq0"), P.sem("d_wq1")]
        wq_loads = {}

        def load_wq(oc):
            w = [] if oc < 2 else [(wqfree, oc - 1)]
            P.emit("gpsimd", lambda Eng, oc=oc: Eng.dma_start(
                out=wq_bufs[oc % 2][:].rearrange("p (c n) -> p c n", n=256),
                in_=aWq[:, 256 * oc:256 * (oc + 1)].rearrange("(c p) n -> p c n", p=128)),
                waits=w, inc=(d_wq[oc % 2], 16))
            wq_loads[oc] = (d_wq[oc % 2], d_wq[oc % 2].n)

        load_wq(0)
        load_wq(1)
        dma("gpsimd", d_aw, wkv_b[:].rearrange("p (c n) -> p c n", n=128),
            aWkv.rearrange("(c p) n -> p c n", p=128))
        dma("gpsimd", d_aw, wk_b[:], aWk)
        dma("gpsimd", d_aw, wv_b[:], aWv)
        akv_loaded = d_aw.n

        kvT = P.sem("kvT")
        for tc in range(4):
            pt = pB[tc % 2]
            w = [(d_aw, akv_loaded), (trf, x_fm_done)]
            if tc >= 2:
                w.append((kvT, tc - 1))
            for c in range(KD):
                P.emit("tensor", lambda Eng, pt=pt, c=c, tc=tc: Eng.matmul(
                    pt[:, 0:512],
                    wkv_b[:, 128 * c:128 * (c + 1)],
                    x_fm[:, 2048 * c + 512 * tc:2048 * c + 512 * (tc + 1)],
                    start=(c == 0), stop=(c == KD - 1)),
                    waits=w if c == 0 else (), inc=(tT, 1) if c == KD - 1 else None)
            s_op(lambda Eng, pt=pt, tc=tc: Eng.copy(
                out=kv_fm[:, 512 * tc:512 * (tc + 1)], in_=pt[:, 0:512]),
                waits=[(tT, tT.n)], inc=(kvT, 1))
        kv_done = kvT.n

        kvuse = P.sem("kvuse")
        kfmd = P.sem("kfmd")
        for i in range(NT):
            pk = pB[i % 2]
            w = [(kvT, kv_done), (d_aw, akv_loaded)]
            if i >= 2:
                w.append((kvuse, 2 * (i - 2) + 1))
            P.emit("tensor", lambda Eng, pk=pk, i=i: Eng.matmul(
                pk[:, 0:512], kv_fm[:, 128 * i:128 * (i + 1)], wk_b[:],
                start=True, stop=True), waits=w, inc=(tT, 1))
            k_mm = tT.n
            for h in range(16):
                s_op(lambda Eng, pk=pk, h=h: Eng.activation(
                    out=junkb[:, 0:32], in_=pk[:, 32 * h:32 * (h + 1)],
                    func=Act.Square, accum_out=sA[:, h:h + 1]),
                    waits=[(tT, k_mm), (vg, vg.n)] if h == 0 else (), guard=True)
            s_op(lambda Eng: Eng.activation(
                out=sB[:, 0:16], in_=sA[:, 0:16], func=Act.Sqrt,
                scale=1.0 / 32, bias=epsT[:, 0:1]), guard=True)
            s_rms = sg.n
            v_op(lambda Eng: Eng.reciprocal(sC[:, 0:16], sB[:, 0:16]),
                 waits=[(sg, s_rms)], guard=True)
            v_rstd = vg.n
            for h in range(16):
                v_op(lambda Eng, pk=pk, h=h: Eng.tensor_scalar(
                    out=qn_sc[:, 32 * h:32 * (h + 1)], in0=pk[:, 32 * h:32 * (h + 1)],
                    scalar1=sC[:, h:h + 1], scalar2=None, op0=Alu.mult),
                    waits=[(vg, v_rstd)],
                    inc=(kvuse, 1) if h == 15 else (vg, 1))
            kn_done = kvuse.n
            for b in range(4):
                tsb, tsf, tsw = TS.acquire()
                wtr = [(kvuse, kn_done), (vg, vg.n), (gG, ident_ready)] + tsw
                P.emit("tensor", lambda Eng, b=b, tsb=tsb: Eng.transpose(
                    tsb, qn_sc[:, 128 * b:128 * (b + 1)], identb[:]),
                    waits=wtr, inc=(tT, 1))
                ktr = tT.n
                v_op(lambda Eng, b=b, i=i, tsb=tsb: Eng.tensor_scalar(
                    out=k_fm[:, 2048 * b + 128 * i:2048 * b + 128 * (i + 1)],
                    in0=tsb,
                    scalar1=kgcol[:], scalar2=None, op0=Alu.mult),
                    waits=[(tT, ktr)], inc=(kfmd, 1))
                TS.read(kfmd)
            wv = [(d_aw, akv_loaded)]
            if i >= 1:
                wv.append((kvuse, 2 * (i - 1) + 2))
            P.emit("tensor", lambda Eng, i=i: Eng.matmul(
                pB[2][:, 0:512], kv_fm[:, 128 * i:128 * (i + 1)], wv_b[:],
                start=True, stop=True), waits=wv, inc=(tT, 1))
            v_mm = tT.n
            s_op(lambda Eng, i=i: Eng.copy(
                out=v_tm[:, 528 * i:528 * (i + 1)]
                .rearrange("p (h c) -> p h c", c=33)[:, :, 0:32],
                in_=pB[2][:, 0:512].rearrange("p (h c) -> p h c", c=32)),
                waits=[(tT, v_mm), (gg, vones1)], inc=(kvuse, 1))
        kfm_all = kfmd.n
        vtm_all = kvuse.n

        qfmd = P.sem("qfmd")
        quse = P.sem("quse")
        for oc in range(4):
            for i in range(NT):
                n = oc * NT + i
                pq = pA[:, 512 * (n % 2):512 * (n % 2) + 256]
                w = [wq_loads[oc], (trf, x_fm_done), (kvuse, vtm_all)]
                if n >= 2:
                    w.append((quse, n - 1))
                last_inc = None
                for c in range(KD):
                    if c == KD - 1:
                        last_inc = (wqfree, 1) if i == NT - 1 else (tT, 1)
                    P.emit("tensor", lambda Eng, pq=pq, c=c, i=i, oc=oc: Eng.matmul(
                        pq,
                        x_fm[:, 2048 * c + 128 * i:2048 * c + 128 * (i + 1)],
                        wq_bufs[oc % 2][:, 256 * c:256 * (c + 1)],
                        start=(c == 0), stop=(c == KD - 1)),
                        waits=w if c == 0 else (),
                        inc=last_inc if c == KD - 1 else None)
                q_mm_wait = (wqfree, oc + 1) if i == NT - 1 else (tT, tT.n)
                for h in range(4):
                    s_op(lambda Eng, pq=pq, h=h: Eng.activation(
                        out=junkb[:, 0:64], in_=pq[0:128, 64 * h:64 * (h + 1)],
                        func=Act.Square, accum_out=sA[:, h:h + 1]),
                        waits=[q_mm_wait, (vg, vg.n)] if h == 0 else (), guard=True)
                s_op(lambda Eng: Eng.activation(
                    out=sB[:, 0:4], in_=sA[:, 0:4], func=Act.Sqrt,
                    scale=1.0 / 64, bias=epsT[:, 0:1]), guard=True)
                s_rms = sg.n
                v_op(lambda Eng: Eng.reciprocal(sC[:, 0:4], sB[:, 0:4]),
                     waits=[(sg, s_rms)], guard=True)
                v_rstd = vg.n
                for h in range(4):
                    v_op(lambda Eng, pq=pq, h=h: Eng.tensor_scalar(
                        out=qn_sc[:, 32 * h:32 * (h + 1)],
                        in0=pq[0:128, 64 * h:64 * h + 32],
                        scalar1=sC[:, h:h + 1], scalar2=None, op0=Alu.mult),
                        waits=[(vg, v_rstd)],
                        inc=(quse, 1) if h == 3 else (vg, 1))
                tsb, tsf, tsw = TS.acquire()
                wtr = [(quse, quse.n), (vg, vg.n), (kfmd, kfm_all)] + tsw
                P.emit("tensor", lambda Eng, tsb=tsb: Eng.transpose(
                    tsb, qn_sc[:, 0:128], identb[:]),
                    waits=wtr, inc=(tT, 1))
                qtr = tT.n
                v_op(lambda Eng, oc=oc, i=i, tsb=tsb: Eng.tensor_scalar(
                    out=q_fm[:, 2048 * oc + 128 * i:2048 * oc + 128 * (i + 1)],
                    in0=tsb, scalar1=qgcol[:], scalar2=None, op0=Alu.mult),
                    waits=[(tT, qtr)], inc=(qfmd, 1))
                TS.read(qfmd)
            if oc + 2 < 4:
                load_wq(oc + 2)
        qfm_all = qfmd.n

        ph1w.close()
        tap("q_fm", q_fm, [(qfmd, qfm_all)])
        tap("k_fm", k_fm, [(kfmd, kfm_all)])
        tap("v_tm", v_tm, [(kvuse, vtm_all)])
        tap("kv_fm", kv_fm, [(kvT, kv_done)])
        if upto == "proj":
            P.wait("sync", (qfmd, qfm_all), (kfmd, kfm_all), (kvuse, vtm_all))
            for i in range(NT):
                P.emit("sync", lambda Eng, i=i: Eng.dma_start(
                    out=y_d[128 * i:128 * (i + 1), :],
                    in_=stream[:, 1024 * i:1024 * (i + 1)]), inc=(dO, 16))
            P.wait("sync", (dO, dO.n))
            P.flush("tail")
            ph1m.close()
            ph1.close()
            return nc, tap_d

        def attention(tagp, NH, HDK, HDV, qf, kf, vt, o_fm, vt_stride,
                      prereq, o_extra=lambda h, tc: []):
            hpc = 128 // HDK
            opc = 128 // HDV
            hstep = HDV + 1
            str_ = P.sem(tagp + "str")
            rowd = P.sem(tagp + "rowd")
            expd = P.sem(tagp + "expd")
            pvp = P.sem(tagp + "pvp")
            oev = P.sem(tagp + "oev")
            bmm = P.sem(tagp + "bmm")
            stch = P.sem(tagp + "sch")
            bias_cnt = {-2: 0, -1: 0}
            st_slots = [pB[0][:, 0:512], pB[1][:, 0:512], pMiscF[:, 0:512]]
            st_mark = []
            st_n = [0]
            pa_rel = [0]
            stch_of = {}
            srow_rd = [0]

            def strip_of(h):
                return 64 * (h % 2) if HDV == 32 else 0

            def st_acquire():
                n = st_n[0]; st_n[0] += 1
                w = []
                if n >= 3:
                    w.append(st_mark[n - 3])
                assert len(st_mark) == n
                return st_slots[n % 3], w

            def stats_mms(h, i):
                cch = h // hpc
                r0 = HDK * (h % hpc)
                tpr = (r0 // 32) * 32
                W = 128 * (i + 1)
                w = list(prereq)
                w.append((vg, pa_rel[0]))
                nsub = (W + 511) // 512
                smm0 = str_.n
                P.emit("tensor", lambda Eng: Eng.ldweights(identb[:, 0:1]))
                for j in range(nsub):
                    ww = min(512, W - 512 * j)
                    P.emit("tensor", lambda Eng, j=j, ww=ww, cch=cch, r0=r0,
                           tpr=tpr, i=i: Eng.matmul(
                               pA[:, 512 * j:512 * j + ww],
                               qf[r0:r0 + HDK, 2048 * cch + 128 * i:
                                  2048 * cch + 128 * (i + 1)],
                               kf[r0:r0 + HDK, 2048 * cch + 512 * j:
                                  2048 * cch + 512 * j + ww],
                               start=True, stop=True, tile_position=(tpr, 0)),
                           waits=w if j == 0 else (), inc=(str_, 1))
                for j in range(nsub):
                    ww = min(512, W - 512 * j)
                    bw = [(str_, smm0 + j + 1)]
                    if j == nsub - 1:
                        v_op(lambda Eng, W=W: Eng.tensor_tensor(
                            out=pA[:, W - 128:W], in0=pA[:, W - 128:W],
                            in1=mask128[:], op=Alu.add),
                            waits=bw + [(gG, masks_ready)])
                        bw = []
                    v_op(lambda Eng, j=j, ww=ww: Eng.reduce_max(
                        sA[:, 4 + j:5 + j], pA[:, 512 * j:512 * j + ww],
                        axis=AX.X), waits=bw, guard=True)
                pa_rel[0] = vg.n
                v_op(lambda Eng, nsub=nsub: Eng.reduce_max(
                    sA[:, 0:1], sA[:, 4:4 + nsub], axis=AX.X), guard=True)
                v_op(lambda Eng: Eng.tensor_copy(out=m16[:], in_=sA[:, 0:1]),
                     guard=True)
                v_op(lambda Eng, i=i: Eng.tensor_scalar(
                    out=nmS[:, i:i + 1], in0=m16[:], scalar1=-1.0,
                    scalar2=None, op0=Alu.mult), guard=True, inc=(stch, 1))
                stch_of[(h, i)] = stch.n

            def stats_repl(h, i):
                par = h % 2
                need = stch_of[(h, i)]
                slot, w = st_acquire()
                P.emit("tensor", lambda Eng, slot=slot, i=i: Eng.matmul(
                    slot[:, 0:128],
                    nmS[:, i:i + 1].to_broadcast([128, 128]),
                    identb[:], start=True, stop=True),
                    waits=w + [(stch, need)], inc=(tT, 1))
                t1 = tT.n
                rg = [(bmm, bias_cnt[h - 2])]
                v_op(lambda Eng, slot=slot, par=par, i=i: Eng.tensor_copy(
                    out=negm_all[par][:, 128 * i:128 * (i + 1)],
                    in_=slot[:, 0:128]),
                    waits=[(tT, t1)] + rg, inc=(rowd, 1))
                st_mark.append((rowd, rowd.n))

            ecnt = [0]
            pv_cnt = {}
            exp_of = {}

            def einsum_front(h, tc, sc, rows_ready):
                par = h % 2
                cch = h // hpc
                r0 = HDK * (h % hpc)
                tpr = (r0 // 32) * 32
                m = ecnt[0]
                ecnt[0] += 1
                pstt, w = st_acquire()
                w.append((rowd, rows_ready))
                P.emit("tensor", lambda Eng: Eng.ldweights(identb[:, 0:1]))
                P.emit("tensor", lambda Eng, pstt=pstt, sc=sc, tc=tc, cch=cch,
                       r0=r0, tpr=tpr: Eng.matmul(
                           pstt[:, 0:512],
                           kf[r0:r0 + HDK, 2048 * cch + 128 * sc:
                              2048 * cch + 128 * (sc + 1)],
                           qf[r0:r0 + HDK, 2048 * cch + 512 * tc:
                              2048 * cch + 512 * (tc + 1)],
                           start=True, stop=False, tile_position=(tpr, 0)),
                       waits=w)
                P.emit("tensor", lambda Eng, pstt=pstt, par=par, tc=tc,
                       tpr=tpr: Eng.matmul(
                    pstt[:, 0:512], ones_bf[tpr:tpr + 1, :],
                    negm_all[par][tpr:tpr + 1, 512 * tc:512 * (tc + 1)],
                    start=False, stop=True, tile_position=(tpr, 0)),
                    inc=(bmm, 1))
                stmm = bmm.n
                off = 128 * sc - 512 * tc
                slot = m % 3
                exp_w = [(bmm, stmm)]
                if off >= 0:
                    v_op(lambda Eng, pstt=pstt, off=off: Eng.tensor_tensor(
                        out=pstt[:, 0:512], in0=pstt[:, 0:512],
                        in1=mask896[:, 384 - off:896 - off], op=Alu.add),
                        waits=[(bmm, stmm), (gG, masks_ready)])
                    exp_w = [(vg, vg.n)]
                if m >= 3:
                    exp_w.append((pvp, pv_cnt[m - 3]))
                s_op(lambda Eng, pstt=pstt, slot=slot: Eng.activation(
                    out=phat[:, 512 * slot:512 * (slot + 1)],
                    in_=pstt[:, 0:512], func=Act.Exp),
                    waits=exp_w, inc=(expd, 1))
                st_mark.append((expd, expd.n))
                exp_of[m] = expd.n
                return (m, h, tc, sc, slot, expd.n)

            pend_sig = []

            def sig_emit(job):
                h, tc, v_rec = job
                strip = strip_of(h)
                orow = HDV * (h % opc)
                ocol = T * (h // opc)
                slot, w = st_acquire()
                P.emit("tensor", lambda Eng, slot=slot, strip=strip:
                       Eng.matmul(
                           slot[strip:strip + HDV, 0:512],
                           ones_bf[0:1, 0:HDV], srow[0:1, :],
                           start=True, stop=True, tile_position=(0, strip)),
                       waits=w + [(vg, v_rec)], inc=(tT, 1))
                srow_rd[0] = tT.n
                t1 = tT.n
                v_op(lambda Eng, strip=strip, slot=slot: Eng.tensor_copy(
                    out=srec_sb[strip:strip + HDV, :],
                    in_=slot[strip:strip + HDV, 0:512]),
                    waits=[(tT, t1)])
                st_mark.append((vg, vg.n))
                v_op(lambda Eng, tc=tc, strip=strip, orow=orow, ocol=ocol:
                     Eng.tensor_tensor(
                         out=o_fm[orow:orow + HDV,
                                  ocol + 512 * tc:ocol + 512 * (tc + 1)],
                         in0=pB[2][strip:strip + HDV, 0:512],
                         in1=srec_sb[strip:strip + HDV, :],
                         op=Alu.mult),
                     waits=o_extra(h, tc), inc=(oev, 1))

            def recip_fn(Eng, strip):
                with nc.allow_low_precision(reason="bf16 softmax denom row"):
                    return Eng.reciprocal(
                        srow[0:1, :],
                        pB[2][strip + HDV:strip + HDV + 1, 0:512])

            def pv_emit(job):
                m, h, tc, sc, slot, pexp = job
                strip = strip_of(h)
                nsc = 4 * (tc + 1)
                while pend_sig:
                    sig_emit(pend_sig.pop(0))
                pv_w = [(expd, pexp)]
                if sc == 0:
                    pv_w.append((oev, oev.n))
                P.emit("tensor", lambda Eng: Eng.ldweights(identb[:, 0:1]))
                P.emit("tensor", lambda Eng, sc=sc, h=h, slot=slot,
                       strip=strip, nsc=nsc: Eng.matmul(
                    pB[2][strip:strip + HDV + 1, 0:512],
                    vt[:, vt_stride * sc + hstep * h:
                       vt_stride * sc + hstep * h + hstep],
                    phat[:, 512 * slot:512 * (slot + 1)],
                    start=(sc == 0), stop=(sc == nsc - 1),
                    tile_position=(0, strip)),
                    waits=pv_w, inc=(pvp, 1))
                pv_cnt[m] = pvp.n
                if sc == nsc - 1:
                    v_op(lambda Eng, strip=strip: recip_fn(Eng, strip),
                         waits=[(pvp, pvp.n), (tT, srow_rd[0])])
                    pend_sig.append((h, tc, vg.n))

            NHrun = NH
            if upto == "attn1":
                NHrun = 1
            rows_ready = {}
            for i in range(NT):
                stats_mms(0, i)
                stats_repl(0, i)
            rows_ready[0] = rowd.n
            if upto == "attnS":
                return rowd, rowd.n, str_, bmm
            for h in range(NHrun):
                nxt = h + 1 < NHrun
                acts = []
                if nxt:
                    for j in range(NT):
                        acts.append(("mm", j))
                        if j >= 4:
                            acts.append(("rp", j - 4))
                    for j in range(NT - 4, NT):
                        acts.append(("rp", j))

                def do_act(a):
                    if a[0] == "mm":
                        stats_mms(h + 1, a[1])
                    else:
                        stats_repl(h + 1, a[1])

                ai = 0
                pend_pv = []
                slots = [(tc, sc) for tc in range(4) for sc in range(4 * (tc + 1))]
                nsl = len(slots)
                for si, (tc, sc) in enumerate(slots):
                    pend_pv.append(einsum_front(h, tc, sc, rows_ready[h]))
                    if len(pend_pv) > 2:
                        pv_emit(pend_pv.pop(0))
                    want = (len(acts) * (si + 1)) // nsl
                    while ai < want:
                        do_act(acts[ai])
                        ai += 1
                for job in pend_pv:
                    pv_emit(job)
                while pend_sig:
                    sig_emit(pend_sig.pop(0))
                assert ai == len(acts)
                if nxt:
                    rows_ready[h + 1] = rowd.n
                bias_cnt[h] = bmm.n
            return oev, oev.n, str_, bmm

        o_fm_mla = fmbuf[:, 0:4 * T]
        oevM, oevM_n, strM, bmmM = attention(
            "ma", MH, ROPE, 32, q_fm, k_fm, v_tm, o_fm_mla, 528,
            prereq=[(qfmd, qfm_all), (kfmd, kfm_all), (kvuse, vtm_all)])

        tap("o_fm", o_fm_mla, [(oevM, oevM_n)])
        tap("negm0", negm_all[0][0:1, :], [(oevM, oevM_n)])
        if upto in ("attn", "attn1", "attnS"):
            P.wait("sync", (oevM, oevM_n))
            for i in range(NT):
                P.emit("sync", lambda Eng, i=i: Eng.dma_start(
                    out=y_d[128 * i:128 * (i + 1), :],
                    in_=stream[:, 1024 * i:1024 * (i + 1)]), inc=(dO, 16))
            P.wait("sync", (dO, dO.n))
            P.flush("tail")
            ph1m.close()
            ph1.close()
            return nc, tap_d

        wosel = ph1m.enter_context(nc.sbuf_tensor("wosel", [128, 4 * 1024], BF16))
        for h in range(16):
            P.emit("gpsimd", lambda Eng, h=h: Eng.dma_start(
                out=wosel[32 * (h % 4):32 * (h % 4) + 32,
                          1024 * (h // 4):1024 * (h // 4) + 1024],
                in_=aWo[64 * h:64 * h + 32, :]),
                waits=[(wqfree, 4), (kvT, kv_done), (kvuse, vtm_all),
                       (qfmd, qfm_all)] if h == 0 else (),
                inc=(d_wo, 16))
        wosel_loaded = d_wo.n

        pl_sem = P.sem("pl")
        woev = P.sem("woev")
        for i in range(NT):
            for half in range(2):
                n = 2 * i + half
                pq = pB[n % 2]
                w = [(oevM, oevM_n), (d_wo, wosel_loaded)]
                if n >= 2:
                    w.append((woev, n - 1))
                for g in range(4):
                    P.emit("tensor", lambda Eng, pq=pq, g=g, i=i, half=half: Eng.matmul(
                        pq[:, 0:512],
                        o_fm_mla[:, 2048 * g + 128 * i:2048 * g + 128 * (i + 1)],
                        wosel[:, 1024 * g + 512 * half:1024 * g + 512 * (half + 1)],
                        start=(g == 0), stop=(g == 3)),
                        waits=w if g == 0 else (), inc=(tT, 1) if g == 3 else None)
                wo_mm = tT.n
                v_op(lambda Eng, pq=pq, i=i, half=half: Eng.tensor_tensor(
                    out=stream[:, 1024 * i + 512 * half:1024 * i + 512 * (half + 1)],
                    in0=pq[:, 0:512],
                    in1=stream[:, 1024 * i + 512 * half:1024 * i + 512 * (half + 1)],
                    op=Alu.add),
                    waits=[(tT, wo_mm)], inc=(woev, 1))
            s_added = woev.n
            v_op(lambda Eng, i=i: Eng.reduce_sum(
                colA[:, 0:1], stream[:, 1024 * i:1024 * (i + 1)], axis=AX.X),
                waits=[(woev, s_added)], guard=True)
            s_op(lambda Eng, i=i: Eng.activation(
                out=junkb[:, 0:1024], in_=stream[:, 1024 * i:1024 * (i + 1)],
                func=Act.Square, accum_out=colA[:, 1:2]),
                waits=[(woev, s_added), (vg, vg.n)], guard=True)
            s_sq = sg.n
            v_op(lambda Eng: Eng.tensor_scalar(
                out=colA[:, 2:3], in0=colA[:, 0:1], scalar1=1.0 / 1024, scalar2=None,
                op0=Alu.mult), guard=True)
            v_op(lambda Eng: Eng.tensor_scalar(
                out=colA[:, 3:4], in0=colA[:, 1:2], scalar1=1.0 / 1024, scalar2=None,
                op0=Alu.mult), waits=[(sg, s_sq)], guard=True)
            v_op(lambda Eng: Eng.tensor_tensor(
                out=colA[:, 4:5], in0=colA[:, 2:3], in1=colA[:, 2:3], op=Alu.mult),
                guard=True)
            v_op(lambda Eng: Eng.tensor_sub(
                colA[:, 5:6], colA[:, 3:4], colA[:, 4:5]), guard=True)
            v_var = vg.n
            s_op(lambda Eng: Eng.activation(
                out=colA[:, 6:7], in_=colA[:, 5:6], func=Act.Sqrt, bias=epsT[:, 1:2]),
                waits=[(vg, v_var)], guard=True)
            s_sqrt = sg.n
            v_op(lambda Eng: Eng.reciprocal(colA[:, 7:8], colA[:, 6:7]),
                 waits=[(sg, s_sqrt)], guard=True)
            v_op(lambda Eng, i=i: Eng.tensor_scalar(
                out=stream[:, 1024 * i:1024 * (i + 1)],
                in0=stream[:, 1024 * i:1024 * (i + 1)],
                scalar1=colA[:, 2:3], scalar2=colA[:, 7:8],
                op0=Alu.subtract, op1=Alu.mult), guard=True, inc=(pl_sem, 1))
        a1_done = pl_sem.n

        tap("a1", stream, [(pl_sem, a1_done)])

        P.flush("ph1")
        ph1m.close()

        ph2 = ExitStack()
        v_tm2 = ph2.enter_context(nc.sbuf_tensor("v_tm2", [128, NT * 1040], BF16))[:]
        ascr_h = ph2.enter_context(nc.sbuf_tensor("ascr", [128, 2 * 1024], F32))
        ascr = ascr_h[:]
        borow = ph2.enter_context(nc.sbuf_tensor("borow", [1, D], BF16))
        g_op(lambda Eng: Eng.memset(v_tm2[:], 1.0), guard=True,
             waits=[(pl_sem, a1_done)], inc=(gg, 1))
        vones2 = gg.n
        q_fm2 = stream_bf[:, 0:16384]
        k_fm2 = stream_bf[:, 16384:32768]
        h_fm = fmbuf
        d_sp = P.sem("d_sp"); d_rl = P.sem("d_rl"); d_fb = P.sem("d_fb")
        d_fwo = P.sem("d_fwo")
        lnr = P.sem("lnr"); hfm = P.sem("hfm")

        sq_last = 0
        for i in range(NT):
            rw = [(pl_sem, a1_done)]
            v_op(lambda Eng, i=i: Eng.reduce_sum(
                colA[:, 0:1], stream[:, 1024 * i:1024 * (i + 1)], axis=AX.X),
                waits=rw, guard=True)
            s_op(lambda Eng, i=i: Eng.activation(
                out=junkb[:, 0:1024], in_=stream[:, 1024 * i:1024 * (i + 1)],
                func=Act.Square, accum_out=colA[:, 1:2]),
                waits=rw + [(vg, vg.n)], guard=True)
            sq_last = sg.n
            v_op(lambda Eng: Eng.tensor_scalar(
                out=colA[:, 2:3], in0=colA[:, 0:1], scalar1=1.0 / 1024, scalar2=None,
                op0=Alu.mult), guard=True)
            v_op(lambda Eng: Eng.tensor_scalar(
                out=colA[:, 3:4], in0=colA[:, 1:2], scalar1=1.0 / 1024, scalar2=None,
                op0=Alu.mult), waits=[(sg, sq_last)], guard=True)
            v_op(lambda Eng: Eng.tensor_tensor(
                out=colA[:, 4:5], in0=colA[:, 2:3], in1=colA[:, 2:3], op=Alu.mult),
                guard=True)
            v_op(lambda Eng: Eng.tensor_sub(
                colA[:, 5:6], colA[:, 3:4], colA[:, 4:5]), guard=True)
            v_var = vg.n
            s_op(lambda Eng: Eng.activation(
                out=colA[:, 6:7], in_=colA[:, 5:6], func=Act.Sqrt, bias=epsT[:, 1:2]),
                waits=[(vg, v_var)], guard=True)
            s_rt = sg.n
            v_op(lambda Eng: Eng.reciprocal(colA[:, 7:8], colA[:, 6:7]),
                 waits=[(sg, s_rt)], guard=True)
            v_op(lambda Eng, i=i: Eng.tensor_scalar(
                out=qn_sc[:, 0:1024], in0=stream[:, 1024 * i:1024 * (i + 1)],
                scalar1=colA[:, 2:3], scalar2=colA[:, 7:8],
                op0=Alu.subtract, op1=Alu.mult), guard=True, inc=(lnr, 1))
            nh_done = lnr.n
            for c in range(KD):
                tsb, tsf, tsw = TS.acquire()
                P.emit("tensor", lambda Eng, c=c, tsb=tsb: Eng.transpose(
                    tsb, qn_sc[:, 128 * c:128 * (c + 1)], identb[:]),
                    waits=[(lnr, nh_done)] + tsw, inc=(tT, 1))
                htr = tT.n
                v_op(lambda Eng, c=c, i=i, tsb=tsb: Eng.scalar_tensor_tensor(
                    out=h_fm[:, 2048 * c + 128 * i:2048 * c + 128 * (i + 1)],
                    in0=tsb, scalar=lnT[:, c:c + 1],
                    in1=lnT[:, 8 + c:9 + c].to_broadcast([128, 128]),
                    op0=Alu.mult, op1=Alu.add),
                    waits=[(tT, htr)], inc=(hfm, 1))
                TS.read(hfm)
        hfm_all = hfm.n

        dma("sync", d_sp, a1sp.rearrange("(n p) d -> p n d", p=128),
            stream.rearrange("p (n d) -> p n d", d=D),
            waits=[(lnr, lnr.n), (sg, sq_last)])

        ph2w = ExitStack()
        fwb = [ph2w.enter_context(nc.sbuf_tensor(f"fwb{j}", [128, 8 * 256], BF16))
               for j in range(2)]
        brow_qkv = [ph2w.enter_context(nc.sbuf_tensor(f"brqkv{j}", [1, D], BF16))
                    for j in range(3)]
        for j, bd in enumerate((fbq, fbk, fbv)):
            dma("gpsimd", d_fb, brow_qkv[j][:], bd.unsqueeze(0),
                waits=[(pl_sem, a1_done)] if j == 0 else ())
        dma("gpsimd", d_fb, borow[:], fbo.unsqueeze(0))
        fb_loaded = d_fb.n

        d_fq = [P.sem("d_fq0"), P.sem("d_fq1")]
        fqfree = P.sem("fqfree")
        wsrc = [fWq, fWk, fWv]
        fq_loads = {}

        def load_fw(idx):
            W, oc = wsrc[idx // 4], idx % 4
            w = [(pl_sem, a1_done)] if idx < 2 else [(fqfree, idx - 1)]
            P.emit("gpsimd", lambda Eng, W=W, oc=oc, idx=idx: Eng.dma_start(
                out=fwb[idx % 2][:].rearrange("p (c n) -> p c n", n=256),
                in_=W[:, 256 * oc:256 * (oc + 1)].rearrange("(c p) n -> p c n", p=128)),
                waits=w, inc=(d_fq[idx % 2], 16))
            fq_loads[idx] = (d_fq[idx % 2], d_fq[idx % 2].n)

        load_fw(0)
        load_fw(1)
        qfm2 = P.sem("qfm2"); kfm2 = P.sem("kfm2"); vtm2 = P.sem("vtm2")
        proj_ev = {}
        for idx in range(12):
            kind, oc = idx // 4, idx % 4
            for i in range(NT):
                n = idx * NT + i
                pq = pA[:, 512 * (n % 2):512 * (n % 2) + 256]
                w = [fq_loads[idx], (hfm, hfm_all), (d_sp, 16)]
                if n >= 2:
                    w.append(proj_ev[n - 2])
                for c in range(KD):
                    P.emit("tensor", lambda Eng, pq=pq, c=c, i=i, idx=idx: Eng.matmul(
                        pq,
                        h_fm[:, 2048 * c + 128 * i:2048 * c + 128 * (i + 1)],
                        fwb[idx % 2][:, 256 * c:256 * (c + 1)],
                        start=(c == 0), stop=False),
                        waits=w if c == 0 else ())
                last_inc = (fqfree, 1) if i == NT - 1 else (tT, 1)
                P.emit("tensor", lambda Eng, pq=pq, kind=kind, oc=oc: Eng.matmul(
                    pq, ones_bf[0:1, :],
                    brow_qkv[kind][0:1, 256 * oc:256 * (oc + 1)],
                    start=False, stop=True),
                    waits=[(d_fb, fb_loaded)], inc=last_inc)
                mm_wait = (fqfree, (idx // 4) * 0 + (fqfree.n)) if i == NT - 1 else (tT, tT.n)
                if kind == 2:
                    s_op(lambda Eng, pq=pq, i=i, oc=oc: Eng.copy(
                        out=v_tm2[:, 1040 * i + 260 * oc:1040 * i + 260 * (oc + 1)]
                        .rearrange("p (h c) -> p h c", c=65)[:, :, 0:64],
                        in_=pq.rearrange("p (h c) -> p h c", c=64)),
                        waits=[mm_wait, (gg, vones2)], inc=(vtm2, 1))
                    proj_ev[n] = (vtm2, vtm2.n)
                else:
                    sc1 = 0.125 if kind == 0 else 1.0
                    v_op(lambda Eng, pq=pq, sc1=sc1: Eng.tensor_scalar(
                        out=qn_sc[:, 0:256], in0=pq,
                        scalar1=sc1, scalar2=None, op0=Alu.mult),
                        waits=[mm_wait, (hfm, hfm_all)], guard=True)
                    proj_ev[n] = (vg, vg.n)
                    dest = q_fm2 if kind == 0 else k_fm2
                    dsem2 = qfm2 if kind == 0 else kfm2
                    for b in range(2):
                        ch = 2 * oc + b
                        tsb, tsf, tsw = TS.acquire()
                        P.emit("tensor", lambda Eng, b=b, tsb=tsb: Eng.transpose(
                            tsb, qn_sc[:, 128 * b:128 * (b + 1)], identb[:]),
                            waits=[(vg, vg.n)] + tsw, inc=(tT, 1))
                        qtr = tT.n
                        v_op(lambda Eng, dest=dest, ch=ch, i=i, tsb=tsb:
                             Eng.tensor_copy(
                                 out=dest[:, 2048 * ch + 128 * i:
                                          2048 * ch + 128 * (i + 1)],
                                 in_=tsb),
                             waits=[(tT, qtr)], inc=(dsem2, 1))
                        TS.read(dsem2)
            if idx + 2 < 12:
                load_fw(idx + 2)
        qfm2_all = qfm2.n
        kfm2_all = kfm2.n
        vtm2_all = vtm2.n

        tap("h_fm", h_fm, [(hfm, hfm_all)])
        tap("qf2", q_fm2, [(qfm2, qfm2_all)])
        tap("kf2", k_fm2, [(kfm2, kfm2_all)])
        tap("vt2", v_tm2, [(vtm2, vtm2_all)])
        if upto == "fproj":
            P.wait("sync", (qfm2, qfm2_all), (kfm2, kfm2_all), (vtm2, vtm2_all))
            P.emit("sync", lambda Eng: Eng.dma_start(
                out=y_d[0:128, :], in_=stream[:, 0:1024]),
                waits=[(qfm2, qfm2_all)], inc=(dO, 16))
            P.wait("sync", (dO, dO.n))
            P.wait("gpsimd", (dT, dT.n))
            P.flush("tail")
            ph2w.close()
            ph2.close()
            ph1.close()
            return nc, tap_d

        o_fm_ffn = fmbuf
        oevF, oevF_n, strF, bmmF = attention(
            "ff", 16, 64, 64, q_fm2, k_fm2, v_tm2, o_fm_ffn, 1040,
            prereq=[(qfm2, qfm2_all), (kfm2, kfm2_all), (vtm2, vtm2_all)])

        tap("o_ffn", o_fm_ffn, [(oevF, oevF_n)])

        dma("sync", d_rl, stream.rearrange("p (n d) -> p n d", d=D),
            a1sp.rearrange("(n p) d -> p n d", p=128),
            waits=[(strF, strF.n), (bmmF, bmmF.n), (oevF, oevF_n)])

        ph2w.close()
        ph2wo = ExitStack()
        fwo = [ph2wo.enter_context(nc.sbuf_tensor(f"fwo{j}", [128, 8 * 256], BF16))
               for j in range(4)]
        for oc in range(4):
            P.emit("gpsimd", lambda Eng, oc=oc: Eng.dma_start(
                out=fwo[oc][:].rearrange("p (c n) -> p c n", n=256),
                in_=fWo[:, 256 * oc:256 * (oc + 1)].rearrange("(c p) n -> p c n", p=128)),
                waits=[(fqfree, 12), (oevF, oevF_n)] if oc == 0 else (),
                inc=(d_fwo, 16))
        fwo_loaded = d_fwo.n

        woev2 = P.sem("woev2"); ln2r = P.sem("ln2r"); h2f = P.sem("h2f")
        wfmd = P.sem("wfmd"); bfold = P.sem("bfold")
        wo_ev = {}
        rt_ev = None
        e8_cnt = 0
        for i in range(NT):
            half = 1024 * (i % 2)
            for oc in range(4):
                n = 4 * i + oc
                pq = pB[n % 2]
                w = [(oevF, oevF_n), (d_fwo, fwo_loaded)]
                if n >= 2:
                    w.append(wo_ev[n - 2])
                for g in range(KD):
                    P.emit("tensor", lambda Eng, pq=pq, g=g, i=i, oc=oc: Eng.matmul(
                        pq[:, 0:256],
                        o_fm_ffn[:, 2048 * g + 128 * i:2048 * g + 128 * (i + 1)],
                        fwo[oc][:, 256 * g:256 * (g + 1)],
                        start=(g == 0), stop=False),
                        waits=w if g == 0 else ())
                P.emit("tensor", lambda Eng, pq=pq, oc=oc: Eng.matmul(
                    pq[:, 0:256], ones_bf[0:1, :],
                    borow[0:1, 256 * oc:256 * (oc + 1)],
                    start=False, stop=True),
                    waits=[(d_fb, fb_loaded)], inc=(tT, 1))
                wo_mm = tT.n
                v_op(lambda Eng, pq=pq, i=i, oc=oc, half=half: Eng.tensor_tensor(
                    out=ascr[:, half + 256 * oc:half + 256 * (oc + 1)],
                    in0=pq[:, 0:256],
                    in1=stream[:, 1024 * i + 256 * oc:1024 * i + 256 * (oc + 1)],
                    op=Alu.add),
                    waits=[(tT, wo_mm), (d_rl, 16)], inc=(woev2, 1))
                wo_ev[n] = (woev2, woev2.n)
            a_done = woev2.n
            v_op(lambda Eng, half=half: Eng.reduce_sum(
                colA[:, 0:1], ascr[:, half:half + 1024], axis=AX.X),
                waits=[(woev2, a_done)], guard=True)
            s_op(lambda Eng, half=half: Eng.activation(
                out=junkb[:, 0:1024], in_=ascr[:, half:half + 1024],
                func=Act.Square, accum_out=colA[:, 1:2]),
                waits=[(woev2, a_done), (vg, vg.n)], guard=True)
            sq2 = sg.n
            v_op(lambda Eng: Eng.tensor_scalar(
                out=colA[:, 2:3], in0=colA[:, 0:1], scalar1=1.0 / 1024, scalar2=None,
                op0=Alu.mult), guard=True)
            v_op(lambda Eng: Eng.tensor_scalar(
                out=colA[:, 3:4], in0=colA[:, 1:2], scalar1=1.0 / 1024, scalar2=None,
                op0=Alu.mult), waits=[(sg, sq2)], guard=True)
            v_op(lambda Eng: Eng.tensor_tensor(
                out=colA[:, 4:5], in0=colA[:, 2:3], in1=colA[:, 2:3], op=Alu.mult),
                guard=True)
            v_op(lambda Eng: Eng.tensor_sub(
                colA[:, 5:6], colA[:, 3:4], colA[:, 4:5]), guard=True)
            v_var = vg.n
            s_op(lambda Eng: Eng.activation(
                out=colA[:, 6:7], in_=colA[:, 5:6], func=Act.Sqrt, bias=epsT[:, 1:2]),
                waits=[(vg, v_var)], guard=True)
            s_rt = sg.n
            v_op(lambda Eng: Eng.reciprocal(colA[:, 7:8], colA[:, 6:7]),
                 waits=[(sg, s_rt)], guard=True)
            v_op(lambda Eng, half=half: Eng.tensor_scalar(
                out=qn_sc[:, 0:1024], in0=ascr[:, half:half + 1024],
                scalar1=colA[:, 2:3], scalar2=colA[:, 7:8],
                op0=Alu.subtract, op1=Alu.mult), guard=True, inc=(ln2r, 1))
            nh2_done = ln2r.n
            for c in range(KD):
                tsb, tsf, tsw = TS.acquire()
                P.emit("tensor", lambda Eng, c=c, tsb=tsb: Eng.transpose(
                    tsb, qn_sc[:, 128 * c:128 * (c + 1)], identb[:]),
                    waits=[(ln2r, nh2_done)] + tsw, inc=(tT, 1))
                htr = tT.n
                v_op(lambda Eng, c=c, i=i, tsb=tsb: Eng.scalar_tensor_tensor(
                    out=fmbuf[:, 2048 * c + 128 * i:2048 * c + 128 * (i + 1)],
                    in0=tsb, scalar=lnT[:, 16 + c:17 + c],
                    in1=lnT[:, 24 + c:25 + c].to_broadcast([128, 128]),
                    op0=Alu.mult, op1=Alu.add),
                    waits=[(tT, htr)], inc=(h2f, 1))
                TS.read(h2f)
            h2_tile = h2f.n
            w = [(h2f, h2_tile), (d_mc, d_mc.n)]
            if rt_ev is not None:
                w.append(rt_ev)
            for c in range(KD):
                P.emit("tensor", lambda Eng, c=c, i=i: Eng.matmul(
                    pB[2][:, 0:8],
                    fmbuf[:, 2048 * c + 128 * i:2048 * c + 128 * (i + 1)],
                    wrbf[:, 8 * c:8 * (c + 1)],
                    start=(c == 0), stop=False),
                    waits=w if c == 0 else ())
            P.emit("tensor", lambda Eng: Eng.matmul(
                pB[2][:, 0:8], ones_bf[0:1, :], brbf[:],
                start=False, stop=True), inc=(tT, 1))
            rt_mm = tT.n
            v_op(lambda Eng: Eng.tensor_copy(out=l8[:], in_=pB[2][:, 0:8]),
                 waits=[(tT, rt_mm), (sg, e8_cnt)], guard=True)
            rt_ev = (vg, vg.n)
            v_op(lambda Eng: Eng.reduce_max(t8a[:, 0:1], l8[:], axis=AX.X, negate=True),
                 guard=True)
            v_op(lambda Eng: Eng.tensor_scalar(
                out=t8b[:], in0=l8[:], scalar1=t8a[:, 0:1], scalar2=0.0,
                op0=Alu.add, op1=Alu.is_equal), guard=True)
            v_op(lambda Eng: Eng.scalar_tensor_tensor(
                out=t8b[:], in0=t8b[:], scalar=-1e30, in1=l8[:],
                op0=Alu.mult, op1=Alu.add), guard=True)
            v_op(lambda Eng: Eng.reduce_max(t8a[:, 1:2], t8b[:], axis=AX.X, negate=True),
                 guard=True)
            v_op(lambda Eng: Eng.tensor_scalar(
                out=t8b[:], in0=l8[:], scalar1=t8a[:, 1:2], scalar2=0.0,
                op0=Alu.add, op1=Alu.is_ge), guard=True)
            v_km = vg.n
            s_op(lambda Eng: Eng.activation(
                out=sA[:, 0:8], in_=l8[:], func=Act.Exp, bias=t8a[:, 0:1]),
                waits=[(vg, v_km)], guard=True)
            e8_cnt = sg.n
            v_op(lambda Eng: Eng.tensor_tensor(
                out=sA[:, 0:8], in0=sA[:, 0:8], in1=t8b[:], op=Alu.mult),
                waits=[(sg, e8_cnt)], guard=True)
            v_op(lambda Eng: Eng.reduce_sum(t8a[:, 2:3], sA[:, 0:8], axis=AX.X),
                 guard=True)
            v_op(lambda Eng: Eng.reciprocal(t8a[:, 3:4], t8a[:, 2:3]), guard=True)
            v_op(lambda Eng, i=i: Eng.tensor_scalar(
                out=wcol[:, 8 * i:8 * (i + 1)], in0=sA[:, 0:8],
                scalar1=t8a[:, 3:4], scalar2=None, op0=Alu.mult), guard=True)
            v_op(lambda Eng, i=i: Eng.tensor_copy(
                out=w8b[:], in_=wcol[:, 8 * i:8 * (i + 1)]), guard=True)
            v_w8 = vg.n
            tsb, tsf, tsw = TS.acquire()
            P.emit("tensor", lambda Eng, tsb=tsb: Eng.transpose(
                tsb[0:8, 0:128], w8b[:], identb[:]),
                waits=[(vg, v_w8)] + tsw, inc=(tT, 1))
            wtr = tT.n
            v_op(lambda Eng, i=i, tsb=tsb: Eng.tensor_copy(
                out=wfm[0:8, 128 * i:128 * (i + 1)], in_=tsb[0:8, 0:128]),
                waits=[(tT, wtr)], inc=(wfmd, 1))
            TS.read(wfmd)
            v_op(lambda Eng, i=i, half=half: Eng.tensor_tensor(
                out=stream[:, 1024 * i:1024 * (i + 1)],
                in0=ascr[:, half:half + 1024],
                in1=stream[:, 1024 * i:1024 * (i + 1)], op=Alu.add),
                guard=True, inc=(bfold, 1))
        bfold_all = bfold.n
        wfm_all = wfmd.n

        tap("B", stream, [(bfold, bfold_all)])
        tap("wcolt", wcol[:], [(bfold, bfold_all)])
        tap("h2fm", fmbuf, [(h2f, h2f.n)])

        P.flush("ph2")

        ph2wo.close()
        ph2.close()
        ph1.close()
        ph3 = ExitStack()
        mid_sb = ph3.enter_context(nc.sbuf_tensor("mid_sb", [128, 32 * 1024], BF16))[:]
        w1b = [ph3.enter_context(nc.sbuf_tensor(f"w1b{j}", [128, 8 * 256], BF16))
               for j in range(2)]
        w2b = [ph3.enter_context(nc.sbuf_tensor(f"w2b{j}", [128, 16 * 256], BF16))
               for j in range(2)]
        ysc = ph3.enter_context(nc.sbuf_tensor("ysc", [128, 2 * 1024], F32))[:]

        h2_fm = fmbuf
        pMF = pMisc_h.ap().bitcast(F32)
        eo_tile = [pA[:, 0:512], pA[:, 512:1024], pA[:, 1024:1536],
                   pA[:, 1536:2048], pB[0][:], pB[1][:], pB[2][:],
                   pMF[:, 0:512]]

        d_w1 = [P.sem("d_w10"), P.sem("d_w11")]
        d_w2 = [P.sem("d_w20"), P.sem("d_w21")]
        w1free = P.sem("w1free"); w2free = P.sem("w2free")
        midr = P.sem("midr"); eor = P.sem("eor")
        wb2d = P.sem("wb2d"); yout = P.sem("yout")
        dOp = [P.sem("dOp0"), P.sem("dOp1")]

        for i in range(NT):
            for hf2 in range(2):
                P.emit("tensor", lambda Eng, i=i, hf2=hf2: Eng.matmul(
                    pB[0][:, 0:512], wfm[0:8, 128 * i:128 * (i + 1)],
                    b2bf[0:8, 512 * hf2:512 * (hf2 + 1)],
                    start=True, stop=True),
                    waits=[(wfmd, wfm_all), (d_mc, d_mc.n), (bfold, bfold_all),
                           (wb2d, wb2d.n)],
                    inc=(tT, 1))
                wbm = tT.n
                v_op(lambda Eng, i=i, hf2=hf2: Eng.tensor_tensor(
                    out=stream[:, 1024 * i + 512 * hf2:1024 * i + 512 * (hf2 + 1)],
                    in0=pB[0][:, 0:512],
                    in1=stream[:, 1024 * i + 512 * hf2:1024 * i + 512 * (hf2 + 1)],
                    op=Alu.add),
                    waits=[(tT, wbm)], inc=(wb2d, 1))
        wb2_all = wb2d.n

        w1_loads = {}
        w2_loads = {}
        n_w1 = [0]
        n_w2 = [0]

        def load_w1(e, tc, g):
            idx = n_w1[0]; n_w1[0] += 1
            w = [(wb2d, wb2_all)] if idx < 2 else [(w1free, idx - 1)]
            P.emit("gpsimd", lambda Eng, e=e, g=g, idx=idx: Eng.dma_start(
                out=w1b[idx % 2][:].rearrange("p (c n) -> p c n", n=256),
                in_=ew1[e][:, 256 * g:256 * (g + 1)].rearrange(
                    "(c p) n -> p c n", p=128)),
                waits=w, inc=(d_w1[idx % 2], 16))
            w1_loads[(e, tc, g)] = (d_w1[idx % 2], d_w1[idx % 2].n)

        def load_w2(e, tc, dq, fh):
            idx = n_w2[0]; n_w2[0] += 1
            w = [(wb2d, wb2_all)] if idx < 2 else [(w2free, idx - 1)]
            P.emit("gpsimd", lambda Eng, e=e, dq=dq, fh=fh, idx=idx: Eng.dma_start(
                out=w2b[idx % 2][:].rearrange("p (c n) -> p c n", n=256),
                in_=ew2[e][2048 * fh:2048 * (fh + 1),
                           256 * dq:256 * (dq + 1)].rearrange(
                    "(c p) n -> p c n", p=128)),
                waits=w, inc=(d_w2[idx % 2], 16))
            w2_loads[(e, tc, dq, fh)] = (d_w2[idx % 2], d_w2[idx % 2].n)

        load_w1(0, 0, 0)
        load_w1(0, 0, 1)
        load_w2(0, 0, 0, 0)
        load_w2(0, 0, 0, 1)

        mid_free = {}
        n_mid = [0]
        eo_ev = {t: None for t in range(8)}
        tile_done = {}

        E_RUN = 1 if upto == "moe1" else E
        for e in range(E_RUN):
            for tc in range(2):
                for g in range(16):
                    for fc2 in range(2):
                        fc = 2 * g + fc2
                        acq = n_mid[0]; n_mid[0] += 1
                        pm = pA[:, 1024 * (acq % 2):1024 * (acq % 2) + 1024]
                        w = [w1_loads[(e, tc, g)], (h2f, h2f.n),
                             (wb2d, wb2_all)]
                        if acq >= 2:
                            w.append(mid_free[acq - 2])
                        for t_eo in (2 * (acq % 2), 2 * (acq % 2) + 1):
                            if eo_ev[t_eo] is not None:
                                w.append(eo_ev[t_eo])
                        for k in range(KD):
                            for tq in range(2):
                                P.emit("tensor", lambda Eng, pm=pm, k=k, tq=tq,
                                       fc2=fc2, tc=tc, e=e, g=g: Eng.matmul(
                                    pm[:, 512 * tq:512 * (tq + 1)],
                                    w1b[(w1_loads[(e, tc, g)][0] is d_w1[1]) * 1][
                                        :, 256 * k + 128 * fc2:
                                        256 * k + 128 * (fc2 + 1)],
                                    h2_fm[:, 2048 * k + 1024 * tc + 512 * tq:
                                          2048 * k + 1024 * tc + 512 * (tq + 1)],
                                    start=(k == 0), stop=(k == KD - 1)),
                                    waits=w if (k == 0 and tq == 0) else (),
                                    inc=((w1free, 1) if fc2 == 1 else (midr, 1))
                                    if (k == KD - 1 and tq == 1) else None)
                        mmark = (w1free, w1free.n) if fc2 == 1 else (midr, midr.n)
                        s_op(lambda Eng, pm=pm, fc=fc, e=e: Eng.activation(
                            out=mid_sb[:, 1024 * fc:1024 * (fc + 1)],
                            in_=pm[:, 0:1024], func=Act.Gelu,
                            bias=b1T[:, 32 * e + fc:32 * e + fc + 1]),
                            waits=[mmark], guard=True)
                        mid_free[acq] = (sg, sg.n)
                    nxt = (e, tc, g + 2)
                    if g + 2 >= 16:
                        nxt = (e, tc + 1, g + 2 - 16) if tc == 0 else \
                              (e + 1, 0, g + 2 - 16)
                    if nxt[0] < E and nxt not in w1_loads:
                        load_w1(*nxt)
                mid_ready = (sg, mid_free[n_mid[0] - 1][1])
                for dq in range(4):
                    for fh in range(2):
                        wl = w2_loads[(e, tc, dq, fh)]
                        for fcl in range(16):
                            fc = 16 * fh + fcl
                            for tsub in range(8):
                                w = []
                                if fcl == 0 and fh == 0:
                                    w = [mid_ready, wl]
                                    if eo_ev[tsub] is not None:
                                        w.append(eo_ev[tsub])
                                elif fcl == 0:
                                    w = [wl]
                                inc = None
                                if fh == 1 and fcl == 15:
                                    inc = (w2free, 1) if tsub == 7 else (eor, 1)
                                elif fh == 0 and fcl == 15 and tsub == 7:
                                    inc = (w2free, 1)
                                P.emit("tensor", lambda Eng, fc=fc, tsub=tsub,
                                       fcl=fcl, dq=dq, wl=wl: Eng.matmul(
                                    eo_tile[tsub][:, 256 * (dq % 2):
                                                  256 * (dq % 2) + 256],
                                    mid_sb[:, 1024 * fc + 128 * tsub:
                                           1024 * fc + 128 * (tsub + 1)],
                                    w2b[(wl[0] is d_w2[1]) * 1][
                                        :, 256 * fcl:256 * (fcl + 1)],
                                    start=(fc == 0), stop=(fc == 31)),
                                    waits=w, inc=inc)
                        nxt = (e, tc, dq, fh + 1)
                        if nxt[3] >= 2:
                            nxt = (e, tc, dq + 1, 0)
                        if nxt[2] >= 4:
                            nxt = (e, tc + 1, 0, 0) if tc == 0 else (e + 1, 0, 0, 0)
                        if nxt[0] < E and (nxt not in w2_loads):
                            load_w2(*nxt)
                    eor_base = eor.n
                    w2_last = (w2free, w2free.n)
                    for tsub in range(8):
                        ti = 8 * tc + tsub
                        if tsub == 7:
                            stop_w = [w2_last]
                        else:
                            stop_w = [(eor, eor_base - 7 + tsub + 1)]
                        v_op(lambda Eng, t=tsub, ti=ti, dq=dq, e=e: Eng.scalar_tensor_tensor(
                            out=stream[:, 1024 * ti + 256 * dq:
                                       1024 * ti + 256 * (dq + 1)],
                            in0=eo_tile[t][:, 256 * (dq % 2):256 * (dq % 2) + 256],
                            scalar=wcol[:, 8 * ti + e:8 * ti + e + 1],
                            in1=stream[:, 1024 * ti + 256 * dq:
                                       1024 * ti + 256 * (dq + 1)],
                            op0=Alu.mult, op1=Alu.add),
                            waits=stop_w, guard=True)
                        eo_ev[tsub] = (vg, vg.n)
                        tile_done[ti] = (vg, vg.n)

        tap("preln", stream, [tile_done[i] for i in range(NT)])
        tap("mids", mid_sb, [mid_free[n_mid[0] - 1]])

        for i in range(NT):
            w = [tile_done[i]]
            v_op(lambda Eng, i=i: Eng.reduce_sum(
                colA[:, 0:1], stream[:, 1024 * i:1024 * (i + 1)], axis=AX.X),
                waits=w, guard=True)
            s_op(lambda Eng, i=i: Eng.activation(
                out=junkb[:, 0:1024], in_=stream[:, 1024 * i:1024 * (i + 1)],
                func=Act.Square, accum_out=colA[:, 1:2]),
                waits=w + [(vg, vg.n)], guard=True)
            sqf = sg.n
            v_op(lambda Eng: Eng.tensor_scalar(
                out=colA[:, 2:3], in0=colA[:, 0:1], scalar1=1.0 / 1024, scalar2=None,
                op0=Alu.mult), guard=True)
            v_op(lambda Eng: Eng.tensor_scalar(
                out=colA[:, 3:4], in0=colA[:, 1:2], scalar1=1.0 / 1024, scalar2=None,
                op0=Alu.mult), waits=[(sg, sqf)], guard=True)
            v_op(lambda Eng: Eng.tensor_tensor(
                out=colA[:, 4:5], in0=colA[:, 2:3], in1=colA[:, 2:3], op=Alu.mult),
                guard=True)
            v_op(lambda Eng: Eng.tensor_sub(
                colA[:, 5:6], colA[:, 3:4], colA[:, 4:5]), guard=True)
            v_var = vg.n
            s_op(lambda Eng: Eng.activation(
                out=colA[:, 6:7], in_=colA[:, 5:6], func=Act.Sqrt, bias=epsT[:, 1:2]),
                waits=[(vg, v_var)], guard=True)
            s_rt = sg.n
            v_op(lambda Eng: Eng.reciprocal(colA[:, 7:8], colA[:, 6:7]),
                 waits=[(sg, s_rt)], guard=True)
            yw = [] if i < 2 else [(dOp[i % 2], 16 * (i // 2))]
            v_op(lambda Eng, i=i: Eng.tensor_scalar(
                out=ysc[:, 1024 * (i % 2):1024 * (i % 2 + 1)],
                in0=stream[:, 1024 * i:1024 * (i + 1)],
                scalar1=colA[:, 2:3], scalar2=colA[:, 7:8],
                op0=Alu.subtract, op1=Alu.mult),
                waits=yw, guard=True, inc=(yout, 1))
            yv = yout.n
            P.emit("sync", lambda Eng, i=i: Eng.dma_start(
                out=y_d[128 * i:128 * (i + 1), :],
                in_=ysc[:, 1024 * (i % 2):1024 * (i % 2 + 1)]),
                waits=[(yout, yv)], inc=(dOp[i % 2], 16))
        P.wait("sync", (dOp[0], dOp[0].n), (dOp[1], dOp[1].n))
        P.wait("gpsimd", (dT, dT.n))
        P.flush("ph3")
        ph3.close()
        phP.close()

    return nc, tap_d



_IN_MAP_KEYS = {
    "aWq": "L0_attn_Wq", "aWkv": "L0_attn_Wkv", "aWk": "L0_attn_Wk",
    "aWv": "L0_attn_Wv", "aWo": "L0_attn_Wo", "qg": "L0_attn_q_g",
    "kg": "L0_attn_k_g", "g1": "L0_ffn_ln1_g", "bb1": "L0_ffn_ln1_b",
    "g2": "L0_ffn_ln2_g", "bb2": "L0_ffn_ln2_b",
    "fWq": "L0_ffn_Wq", "fbq": "L0_ffn_bq", "fWk": "L0_ffn_Wk",
    "fbk": "L0_ffn_bk", "fWv": "L0_ffn_Wv", "fbv": "L0_ffn_bv",
    "fWo": "L0_ffn_Wo", "fbo": "L0_ffn_bo", "Wr": "L0_ffn_Wr",
    "br": "L0_ffn_br", "ew1": "L0_ffn_e_w1", "eb1": "L0_ffn_e_b1",
    "ew2": "L0_ffn_e_w2", "eb2": "L0_ffn_e_b2",
}


def make_in_maps(inputs):
    xs = np.ascontiguousarray(inputs["L0_attn_x"])
    base = {k: np.ascontiguousarray(inputs[v]) for k, v in _IN_MAP_KEYS.items()}
    return [dict(base, x=np.ascontiguousarray(xs[i])) for i in range(NCORES)]


def kernel(**inputs):
    nc, _ = build_nc(taps=())
    res = run_bass_kernel_spmd(nc, make_in_maps(inputs), core_ids=list(range(NCORES)))
    return np.stack([r["y"] for r in res.results]).astype(np.float32)


# version 10
# speedup vs baseline: 1.1027x; 1.0043x over previous
import numpy as np
from contextlib import ExitStack

import concourse.bass as bass
import concourse.mybir as mybir
from concourse.bass_utils import run_bass_kernel_spmd

F32 = mybir.dt.float32
BF16 = mybir.dt.bfloat16
Act = mybir.ActivationFunctionType
Alu = mybir.AluOpType
AX = mybir.AxisListType

T, D = 2048, 1024
NT = T // 128
KD = D // 128
MH, ROPE, LAT = 16, 32, 128
E, FF = 8, 4096
NCORES = 8

ENGINES = ("tensor", "vector", "scalar", "gpsimd", "sync")


class Sem:
    __slots__ = ("h", "n", "name")

    def __init__(self, h, name):
        self.h = h
        self.n = 0
        self.name = name


class Prog:
    def __init__(self, nc, stack):
        self.nc = nc
        self.stack = stack
        self.q = {e: [] for e in ENGINES}
        self.lastwait = {e: {} for e in ENGINES}
        self.lastinc = {e: None for e in ENGINES}

    def sem(self, name):
        h = self.stack.enter_context(self.nc.semaphore(name))
        return Sem(h, name)

    def emit(self, eng, fn, waits=(), inc=None):
        wl = []
        for s, v in waits:
            if v <= 0:
                continue
            assert v <= s.n, f"wait {v} > emitted {s.n} on {s.name} (engine {eng})"
            if self.lastwait[eng].get(s.name, -1) < v:
                self.lastwait[eng][s.name] = v
                wl.append((s.h, v))
        il = None
        if inc is not None:
            s, k = inc
            s.n += k
            il = (s.h, k)
            self.lastinc[eng] = (s, s.n)

        def thunk(Eng, wl=wl, il=il, fn=fn):
            for h, v in wl:
                Eng.wait_ge(h, v)
            ins = fn(Eng)
            if il is not None:
                assert ins is not None
                ins.then_inc(*il)

        self.q[eng].append(thunk)

    def wait(self, eng, *waits):
        self.emit(eng, lambda Eng: None, waits=waits)

    def flush(self, name=None):
        qs = {e: fns for e, fns in self.q.items() if fns}
        if qs:
            with self.nc.Block(name) as block:
                for e, fns in qs.items():
                    getattr(block, e)(lambda Eng, fns=fns: [f(Eng) for f in fns])
        self.q = {e: [] for e in ENGINES}


def build_nc(taps=(), upto=None):
    nc = bass.Bass("TRN2", target_bir_lowering=False)
    d_in = {}

    def din(name, shape):
        d_in[name] = nc.dram_tensor(name, shape, F32, kind="ExternalInput").ap()
        return d_in[name]

    x_d = din("x", [T, D])
    aWq = din("aWq", [D, D]); aWkv = din("aWkv", [D, LAT])
    aWk = din("aWk", [LAT, MH * ROPE]); aWv = din("aWv", [LAT, MH * 32])
    aWo = din("aWo", [D, D])
    qg_d = din("qg", [64]); kg_d = din("kg", [ROPE])
    g1_d = din("g1", [D]); bb1_d = din("bb1", [D])
    g2_d = din("g2", [D]); bb2_d = din("bb2", [D])
    fWq = din("fWq", [D, D]); fbq = din("fbq", [D])
    fWk = din("fWk", [D, D]); fbk = din("fbk", [D])
    fWv = din("fWv", [D, D]); fbv = din("fbv", [D])
    fWo = din("fWo", [D, D]); fbo = din("fbo", [D])
    Wr_d = din("Wr", [D, E]); br_d = din("br", [E])
    ew1 = din("ew1", [E, D, FF]); eb1 = din("eb1", [E, FF])
    ew2 = din("ew2", [E, FF, D]); eb2 = din("eb2", [E, D])

    y_d = nc.dram_tensor("y", [T, D], F32, kind="ExternalOutput").ap()
    a1sp = nc.dram_tensor("a1sp", [T, D], F32, kind="Internal").ap()

    taps = list(taps)
    tap_d = {}

    with ExitStack() as st:
        ec = st.enter_context

        stream_h = ec(nc.sbuf_tensor("stream", [128, NT * D], F32))
        fmbuf_h = ec(nc.sbuf_tensor("fmbuf", [128, KD * T], BF16))
        stream = stream_h[:]
        fmbuf = fmbuf_h[:]
        stream_bf = stream_h.ap().bitcast(BF16)

        identb = ec(nc.sbuf_tensor("identb", [128, 128], BF16))
        identf = ec(nc.sbuf_tensor("identf", [128, 128], F32))
        lnT = ec(nc.sbuf_tensor("lnT", [128, 32], F32))
        b1T = ec(nc.sbuf_tensor("b1T", [128, E * 32], F32))
        b2bf = ec(nc.sbuf_tensor("b2bf", [8, D], BF16))
        wrbf = ec(nc.sbuf_tensor("wrbf", [128, KD * E], BF16))
        brbf = ec(nc.sbuf_tensor("brbf", [1, E], BF16))
        wcol = ec(nc.sbuf_tensor("wcol", [128, NT * E], F32))
        wfm = ec(nc.sbuf_tensor("wfm", [8, T], BF16))
        ones_bf = ec(nc.sbuf_tensor("ones_bf", [128, 128], BF16))
        stage = ec(nc.sbuf_tensor("stage", [128, 128], F32))
        junkb = ec(nc.sbuf_tensor("junkb", [128, 2 * 1024], BF16))
        sA = ec(nc.sbuf_tensor("sA", [128, 16], F32))
        sB = ec(nc.sbuf_tensor("sB", [128, 16], F32))
        sC = ec(nc.sbuf_tensor("sC", [128, 16], F32))
        colA = ec(nc.sbuf_tensor("colA", [128, 8], F32))
        m16 = ec(nc.sbuf_tensor("m16", [128, 1], BF16))
        nm32 = ec(nc.sbuf_tensor("nm32", [128, 1], F32))
        nm16 = ec(nc.sbuf_tensor("nm16", [128, 1], BF16))
        sr16 = ec(nc.sbuf_tensor("sr16", [128, 1], BF16))
        nmS = ec(nc.sbuf_tensor("nmS", [128, 16], BF16))[:]
        l8 = ec(nc.sbuf_tensor("l8", [128, 8], F32))
        t8a = ec(nc.sbuf_tensor("t8a", [128, 8], F32))
        t8b = ec(nc.sbuf_tensor("t8b", [128, 8], F32))
        w8b = ec(nc.sbuf_tensor("w8b", [128, 8], BF16))
        qgcol = ec(nc.sbuf_tensor("qgcol", [128, 1], F32))
        kgcol = ec(nc.sbuf_tensor("kgcol", [128, 1], F32))
        epsT = ec(nc.sbuf_tensor("epsT", [128, 2], F32))

        P = Prog(nc, st)
        dS = P.sem("dS"); dO = P.sem("dO"); dT = P.sem("dT")
        d_ln = P.sem("d_ln"); d_b1 = P.sem("d_b1"); d_gn = P.sem("d_gn")
        d_mc = P.sem("d_mc"); d_aw = P.sem("d_aw"); d_wo = P.sem("d_wo")
        vg = P.sem("vg"); sg = P.sem("sg"); gG = P.sem("gG"); tT = P.sem("tT")
        gg = P.sem("gg")

        def dma(eng, sem, dst, src, waits=()):
            P.emit(eng, lambda Eng, d=dst, s=src: Eng.dma_start(out=d, in_=s),
                   waits=waits, inc=(sem, 16))

        def v_op(fn, waits=(), guard=False, inc=None):
            w = list(waits)
            if guard and P.lastinc["vector"] is not None:
                w.append(P.lastinc["vector"])
            P.emit("vector", fn, waits=w, inc=inc if inc is not None else (vg, 1))

        def s_op(fn, waits=(), guard=False, inc=None):
            w = list(waits)
            if guard and P.lastinc["scalar"] is not None:
                w.append(P.lastinc["scalar"])
            P.emit("scalar", fn, waits=w, inc=inc if inc is not None else (sg, 1))

        def g_op(fn, inc=None, waits=(), guard=False):
            w = list(waits)
            if guard and P.lastinc["gpsimd"] is not None:
                w.append(P.lastinc["gpsimd"])
            P.emit("gpsimd", fn, waits=w, inc=inc if inc is not None else (gg, 1))

        def tap(name, ap, waits):
            if name in taps:
                sh = list(ap.shape)
                tap_d[name] = nc.dram_tensor("tap_" + name, sh, F32,
                                             kind="ExternalOutput").ap()
                dma("gpsimd", dT, tap_d[name], ap, waits=waits)

        ph0 = ExitStack()
        pE = ph0.enter_context(nc.psum_tensor("pE", [128, 512], F32))
        pE2 = ph0.enter_context(nc.psum_tensor("pE2", [128, 512], F32))

        dma("sync", dS, stream.rearrange("p (n d) -> p n d", d=D),
            x_d.rearrange("(n p) d -> p n d", p=128))
        x_loaded = dS.n

        for t_ in (identb, identf):
            g_op(lambda Eng, t_=t_: Eng.memset(t_[:], 0.0))
            g_op(lambda Eng, t_=t_: Eng.affine_select(
                out=t_[:], in_=t_[:], compare_op=Alu.not_equal, fill=1.0,
                base=0, pattern=[[-1, 128]], channel_multiplier=1), guard=True)
        g_op(lambda Eng: Eng.memset(ones_bf[:], 1.0))
        g_op(lambda Eng: Eng.memset(epsT[:, 0:1], 1e-6))
        g_op(lambda Eng: Eng.memset(epsT[:, 1:2], 1e-5), inc=(gG, 1))
        ident_ready = gG.n

        dma("sync", d_ln, stage[0:8, :], g1_d.rearrange("(c n) -> c n", n=128))
        dma("sync", d_ln, stage[8:16, :], bb1_d.rearrange("(c n) -> c n", n=128))
        dma("sync", d_ln, stage[16:24, :], g2_d.rearrange("(c n) -> c n", n=128))
        dma("sync", d_ln, stage[24:32, :], bb2_d.rearrange("(c n) -> c n", n=128))
        ln_staged = d_ln.n
        P.emit("tensor", lambda Eng: Eng.transpose(pE[:, 0:128][0:128, 0:32], stage[0:32, :], identf[0:32, 0:32]),
               waits=[(d_ln, ln_staged), (gG, ident_ready)], inc=(tT, 1))
        v_op(lambda Eng: Eng.tensor_copy(out=lnT[:], in_=pE[0:128, 0:32]),
             waits=[(tT, tT.n)])
        lnT_ready = vg.n

        for g in range(2):
            P.emit("sync", lambda Eng, g=g: Eng.dma_start(
                out=stage[:],
                in_=eb1[4 * g:4 * g + 4].rearrange("e (c n) -> (e c) n", n=128)),
                waits=[(vg, lnT_ready)] if g == 0 else [(vg, vg.n)], inc=(dS, 16))
            b1_staged = dS.n
            P.emit("tensor", lambda Eng: Eng.transpose(pE[:, 0:128], stage[:], identf[:]),
                   waits=[(dS, b1_staged), (vg, vg.n)], inc=(tT, 1))
            v_op(lambda Eng, g=g: Eng.tensor_copy(
                out=b1T[:, 128 * g:128 * g + 128], in_=pE[0:128, 0:128]),
                waits=[(tT, tT.n)])
        b1T_ready = vg.n

        dma("gpsimd", d_mc, b2bf[:], eb2)
        dma("gpsimd", d_mc, wrbf[:].rearrange("p (c e) -> p c e", e=E),
            Wr_d.rearrange("(c p) e -> p c e", p=128))
        dma("gpsimd", d_mc, brbf[:], br_d.unsqueeze(0))

        qgrow = ph0.enter_context(nc.sbuf_tensor("qgrow", [1, 128], F32))
        kgrow = ph0.enter_context(nc.sbuf_tensor("kgrow", [1, 128], F32))
        dma("sync", d_gn, stage[0:1, 0:64], qg_d.unsqueeze(0),
            waits=[(vg, b1T_ready)])
        dma("sync", d_gn, stage[32:33, 0:32], kg_d.unsqueeze(0))
        gains_staged = d_gn.n
        v_op(lambda Eng: Eng.tensor_scalar(
            out=qgrow[0:1, :].rearrange("a (j r) -> a j r", r=32),
            in0=stage[0:1, 0:32].unsqueeze(1).to_broadcast([1, 4, 32]),
            scalar1=0.125, scalar2=None, op0=Alu.mult),
            waits=[(d_gn, gains_staged)])
        v_op(lambda Eng: Eng.tensor_copy(
            out=kgrow[0:1, :].rearrange("a (j r) -> a j r", r=32),
            in_=stage[32:33, 0:32].unsqueeze(1).to_broadcast([1, 4, 32])))
        gains_rows = vg.n
        P.emit("tensor", lambda Eng: Eng.transpose(pE[:, 0:128][0:128, 0:1], qgrow[:], identf[0:1, 0:1]),
               waits=[(vg, gains_rows)], inc=(tT, 1))
        v_op(lambda Eng: Eng.tensor_copy(out=qgcol[:], in_=pE[0:128, 0:1]),
             waits=[(tT, tT.n)])
        P.emit("tensor", lambda Eng: Eng.transpose(pE2[:, 0:128][0:128, 0:1], kgrow[:], identf[0:1, 0:1]),
               inc=(tT, 1))
        v_op(lambda Eng: Eng.tensor_copy(out=kgcol[:], in_=pE2[0:128, 0:1]),
             waits=[(tT, tT.n)])
        gains_ready = vg.n

        x_fm = fmbuf
        trf = P.sem("trf")
        tslots = [pE[:, 0:128], pE2[:, 0:128], pE[:, 128:256], pE2[:, 128:256]]
        n_tr = 0
        for i in range(NT):
            for c in range(KD):
                slot = tslots[n_tr % 4]
                w = [(dS, x_loaded), (gG, ident_ready), (vg, gains_ready)]
                if n_tr >= 2:
                    w = [(trf, n_tr - 1)]
                P.emit("tensor", lambda Eng, s=slot, i=i, c=c: Eng.transpose(
                    s, stream[:, 1024 * i + 128 * c:1024 * i + 128 * (c + 1)], identf[:]),
                    waits=w, inc=(tT, 1))
                v_op(lambda Eng, s=slot, i=i, c=c: Eng.tensor_copy(
                    out=x_fm[:, 2048 * c + 128 * i:2048 * c + 128 * (i + 1)], in_=s),
                    waits=[(tT, tT.n)], inc=(trf, 1))
                n_tr += 1
        x_fm_done = trf.n

        tap("x_fm", x_fm, [(trf, x_fm_done)])
        P.flush("ph0")
        ph0.close()
        if upto == "ph0":
            for i in range(NT):
                P.emit("sync", lambda Eng, i=i: Eng.dma_start(
                    out=y_d[128 * i:128 * (i + 1), :],
                    in_=stream[:, 1024 * i:1024 * (i + 1)]),
                    waits=[(trf, x_fm_done)], inc=(dO, 16))
            P.wait("sync", (dO, dO.n))
            P.wait("gpsimd", (dT, dT.n))
            P.flush("tail")
            return nc, tap_d

        ph1 = ExitStack()
        mask128 = ph1.enter_context(nc.sbuf_tensor("mask128", [128, 128], F32))
        mask896 = ph1.enter_context(nc.sbuf_tensor("mask896", [128, 896], F32))
        negm_all = [ph1.enter_context(nc.sbuf_tensor(f"negma{j}", [128, T], BF16))
                    for j in range(2)]
        srow = ph1.enter_context(nc.sbuf_tensor("srow", [1, 512], BF16))
        srec_sb = ph1.enter_context(nc.sbuf_tensor("srec_sb", [128, 512], BF16))
        phat = ph1.enter_context(nc.sbuf_tensor("phat", [128, 3 * 512], BF16))
        qn_sc = ph1.enter_context(nc.sbuf_tensor("qn_sc", [128, 1024], BF16))

        ph1m = ExitStack()
        q_fm = ph1m.enter_context(nc.sbuf_tensor("q_fm", [128, 4 * T], BF16))[:]
        k_fm = ph1m.enter_context(nc.sbuf_tensor("k_fm", [128, 4 * T], BF16))[:]
        v_tm = ph1m.enter_context(nc.sbuf_tensor("v_tm", [128, NT * 528], BF16))[:]
        kv_fm = ph1m.enter_context(nc.sbuf_tensor("kv_fm", [128, T], BF16))[:]
        g_op(lambda Eng: Eng.memset(v_tm[:], 1.0), guard=True,
             waits=[(trf, x_fm_done)], inc=(gg, 1))
        vones1 = gg.n

        phP = ExitStack()
        pA = phP.enter_context(nc.psum_tensor("pA", [128, 2048], F32))
        pB = [phP.enter_context(nc.psum_tensor(f"pB{j}", [128, 512], F32))
              for j in range(3)]
        pMisc_h = phP.enter_context(nc.psum_tensor("pMisc", [128, 1024], BF16))
        pMisc = pMisc_h[:]
        pMiscF = pMisc_h.ap().bitcast(F32)
        pA_bf = pA.bitcast(BF16)

        class TransSlots:

            def __init__(self, slots):
                self.slots = slots
                self.last = [None] * len(slots)
                self.i = 0

            def acquire(self):
                j = self.i % len(self.slots)
                self.i += 1
                self.cur = j
                w = [self.last[j]] if self.last[j] is not None else []
                return self.slots[j][0], self.slots[j][1], w

            def read(self, sem):
                self.last[self.cur] = (sem, sem.n)

        TS = TransSlots([
            (pMisc[:, 0:128], pMiscF[:, 0:128]),
            (pA_bf[:, 2048:2176], pA[:, 1024:1152]),
            (pA_bf[:, 3072:3200], pA[:, 1536:1664]),
        ])

        g_op(lambda Eng: Eng.memset(mask128[:], 0.0))
        g_op(lambda Eng: Eng.affine_select(
            out=mask128[:], in_=mask128[:], compare_op=Alu.is_ge, fill=-1e30,
            base=0, pattern=[[-1, 128]], channel_multiplier=1), guard=True)
        g_op(lambda Eng: Eng.memset(mask896[:], 0.0))
        g_op(lambda Eng: Eng.affine_select(
            out=mask896[:], in_=mask896[:], compare_op=Alu.is_ge, fill=-1e30,
            base=-384, pattern=[[1, 896]], channel_multiplier=-1),
            guard=True, inc=(gG, 1))
        masks_ready = gG.n

        ph1w = ExitStack()
        wq_bufs = [ph1w.enter_context(nc.sbuf_tensor(f"awq{j}", [128, 8 * 256], BF16))
                   for j in range(2)]
        wkv_b = ph1w.enter_context(nc.sbuf_tensor("awkv", [128, 8 * 128], BF16))
        wk_b = ph1w.enter_context(nc.sbuf_tensor("awk", [128, 512], BF16))
        wv_b = ph1w.enter_context(nc.sbuf_tensor("awv", [128, 512], BF16))
        wqfree = P.sem("wqfree")
        d_wq = [P.sem("d_wq0"), P.sem("d_wq1")]
        wq_loads = {}

        def load_wq(oc):
            w = [] if oc < 2 else [(wqfree, oc - 1)]
            P.emit("gpsimd", lambda Eng, oc=oc: Eng.dma_start(
                out=wq_bufs[oc % 2][:].rearrange("p (c n) -> p c n", n=256),
                in_=aWq[:, 256 * oc:256 * (oc + 1)].rearrange("(c p) n -> p c n", p=128)),
                waits=w, inc=(d_wq[oc % 2], 16))
            wq_loads[oc] = (d_wq[oc % 2], d_wq[oc % 2].n)

        load_wq(0)
        load_wq(1)
        dma("gpsimd", d_aw, wkv_b[:].rearrange("p (c n) -> p c n", n=128),
            aWkv.rearrange("(c p) n -> p c n", p=128))
        dma("gpsimd", d_aw, wk_b[:], aWk)
        dma("gpsimd", d_aw, wv_b[:], aWv)
        akv_loaded = d_aw.n

        kvT = P.sem("kvT")
        for tc in range(4):
            pt = pB[tc % 2]
            w = [(d_aw, akv_loaded), (trf, x_fm_done)]
            if tc >= 2:
                w.append((kvT, tc - 1))
            for c in range(KD):
                P.emit("tensor", lambda Eng, pt=pt, c=c, tc=tc: Eng.matmul(
                    pt[:, 0:512],
                    wkv_b[:, 128 * c:128 * (c + 1)],
                    x_fm[:, 2048 * c + 512 * tc:2048 * c + 512 * (tc + 1)],
                    start=(c == 0), stop=(c == KD - 1)),
                    waits=w if c == 0 else (), inc=(tT, 1) if c == KD - 1 else None)
            s_op(lambda Eng, pt=pt, tc=tc: Eng.copy(
                out=kv_fm[:, 512 * tc:512 * (tc + 1)], in_=pt[:, 0:512]),
                waits=[(tT, tT.n)], inc=(kvT, 1))
        kv_done = kvT.n

        kvuse = P.sem("kvuse")
        kfmd = P.sem("kfmd")
        for i in range(NT):
            pk = pB[i % 2]
            w = [(kvT, kv_done), (d_aw, akv_loaded)]
            if i >= 2:
                w.append((kvuse, 2 * (i - 2) + 1))
            P.emit("tensor", lambda Eng, pk=pk, i=i: Eng.matmul(
                pk[:, 0:512], kv_fm[:, 128 * i:128 * (i + 1)], wk_b[:],
                start=True, stop=True), waits=w, inc=(tT, 1))
            k_mm = tT.n
            for h in range(16):
                s_op(lambda Eng, pk=pk, h=h: Eng.activation(
                    out=junkb[:, 0:32], in_=pk[:, 32 * h:32 * (h + 1)],
                    func=Act.Square, accum_out=sA[:, h:h + 1]),
                    waits=[(tT, k_mm), (vg, vg.n)] if h == 0 else (), guard=True)
            s_op(lambda Eng: Eng.activation(
                out=sB[:, 0:16], in_=sA[:, 0:16], func=Act.Sqrt,
                scale=1.0 / 32, bias=epsT[:, 0:1]), guard=True)
            s_rms = sg.n
            v_op(lambda Eng: Eng.reciprocal(sC[:, 0:16], sB[:, 0:16]),
                 waits=[(sg, s_rms)], guard=True)
            v_rstd = vg.n
            for h in range(16):
                v_op(lambda Eng, pk=pk, h=h: Eng.tensor_scalar(
                    out=qn_sc[:, 32 * h:32 * (h + 1)], in0=pk[:, 32 * h:32 * (h + 1)],
                    scalar1=sC[:, h:h + 1], scalar2=None, op0=Alu.mult),
                    waits=[(vg, v_rstd)],
                    inc=(kvuse, 1) if h == 15 else (vg, 1))
            kn_done = kvuse.n
            for b in range(4):
                tsb, tsf, tsw = TS.acquire()
                wtr = [(kvuse, kn_done), (vg, vg.n), (gG, ident_ready)] + tsw
                P.emit("tensor", lambda Eng, b=b, tsb=tsb: Eng.transpose(
                    tsb, qn_sc[:, 128 * b:128 * (b + 1)], identb[:]),
                    waits=wtr, inc=(tT, 1))
                ktr = tT.n
                v_op(lambda Eng, b=b, i=i, tsb=tsb: Eng.tensor_scalar(
                    out=k_fm[:, 2048 * b + 128 * i:2048 * b + 128 * (i + 1)],
                    in0=tsb,
                    scalar1=kgcol[:], scalar2=None, op0=Alu.mult),
                    waits=[(tT, ktr)], inc=(kfmd, 1))
                TS.read(kfmd)
            wv = [(d_aw, akv_loaded)]
            if i >= 1:
                wv.append((kvuse, 2 * (i - 1) + 2))
            P.emit("tensor", lambda Eng, i=i: Eng.matmul(
                pB[2][:, 0:512], kv_fm[:, 128 * i:128 * (i + 1)], wv_b[:],
                start=True, stop=True), waits=wv, inc=(tT, 1))
            v_mm = tT.n
            s_op(lambda Eng, i=i: Eng.copy(
                out=v_tm[:, 528 * i:528 * (i + 1)]
                .rearrange("p (h c) -> p h c", c=33)[:, :, 0:32],
                in_=pB[2][:, 0:512].rearrange("p (h c) -> p h c", c=32)),
                waits=[(tT, v_mm), (gg, vones1)], inc=(kvuse, 1))
        kfm_all = kfmd.n
        vtm_all = kvuse.n

        qfmd = P.sem("qfmd")
        quse = P.sem("quse")
        for oc in range(4):
            for i in range(NT):
                n = oc * NT + i
                pq = pA[:, 512 * (n % 2):512 * (n % 2) + 256]
                w = [wq_loads[oc], (trf, x_fm_done), (kvuse, vtm_all)]
                if n >= 2:
                    w.append((quse, n - 1))
                last_inc = None
                for c in range(KD):
                    if c == KD - 1:
                        last_inc = (wqfree, 1) if i == NT - 1 else (tT, 1)
                    P.emit("tensor", lambda Eng, pq=pq, c=c, i=i, oc=oc: Eng.matmul(
                        pq,
                        x_fm[:, 2048 * c + 128 * i:2048 * c + 128 * (i + 1)],
                        wq_bufs[oc % 2][:, 256 * c:256 * (c + 1)],
                        start=(c == 0), stop=(c == KD - 1)),
                        waits=w if c == 0 else (),
                        inc=last_inc if c == KD - 1 else None)
                q_mm_wait = (wqfree, oc + 1) if i == NT - 1 else (tT, tT.n)
                for h in range(4):
                    s_op(lambda Eng, pq=pq, h=h: Eng.activation(
                        out=junkb[:, 0:64], in_=pq[0:128, 64 * h:64 * (h + 1)],
                        func=Act.Square, accum_out=sA[:, h:h + 1]),
                        waits=[q_mm_wait, (vg, vg.n)] if h == 0 else (), guard=True)
                s_op(lambda Eng: Eng.activation(
                    out=sB[:, 0:4], in_=sA[:, 0:4], func=Act.Sqrt,
                    scale=1.0 / 64, bias=epsT[:, 0:1]), guard=True)
                s_rms = sg.n
                v_op(lambda Eng: Eng.reciprocal(sC[:, 0:4], sB[:, 0:4]),
                     waits=[(sg, s_rms)], guard=True)
                v_rstd = vg.n
                for h in range(4):
                    v_op(lambda Eng, pq=pq, h=h: Eng.tensor_scalar(
                        out=qn_sc[:, 32 * h:32 * (h + 1)],
                        in0=pq[0:128, 64 * h:64 * h + 32],
                        scalar1=sC[:, h:h + 1], scalar2=None, op0=Alu.mult),
                        waits=[(vg, v_rstd)],
                        inc=(quse, 1) if h == 3 else (vg, 1))
                tsb, tsf, tsw = TS.acquire()
                wtr = [(quse, quse.n), (vg, vg.n), (kfmd, kfm_all)] + tsw
                P.emit("tensor", lambda Eng, tsb=tsb: Eng.transpose(
                    tsb, qn_sc[:, 0:128], identb[:]),
                    waits=wtr, inc=(tT, 1))
                qtr = tT.n
                v_op(lambda Eng, oc=oc, i=i, tsb=tsb: Eng.tensor_scalar(
                    out=q_fm[:, 2048 * oc + 128 * i:2048 * oc + 128 * (i + 1)],
                    in0=tsb, scalar1=qgcol[:], scalar2=None, op0=Alu.mult),
                    waits=[(tT, qtr)], inc=(qfmd, 1))
                TS.read(qfmd)
            if oc + 2 < 4:
                load_wq(oc + 2)
        qfm_all = qfmd.n

        ph1w.close()
        tap("q_fm", q_fm, [(qfmd, qfm_all)])
        tap("k_fm", k_fm, [(kfmd, kfm_all)])
        tap("v_tm", v_tm, [(kvuse, vtm_all)])
        tap("kv_fm", kv_fm, [(kvT, kv_done)])
        if upto == "proj":
            P.wait("sync", (qfmd, qfm_all), (kfmd, kfm_all), (kvuse, vtm_all))
            for i in range(NT):
                P.emit("sync", lambda Eng, i=i: Eng.dma_start(
                    out=y_d[128 * i:128 * (i + 1), :],
                    in_=stream[:, 1024 * i:1024 * (i + 1)]), inc=(dO, 16))
            P.wait("sync", (dO, dO.n))
            P.flush("tail")
            ph1m.close()
            ph1.close()
            return nc, tap_d

        def attention(tagp, NH, HDK, HDV, qf, kf, vt, o_fm, vt_stride,
                      prereq, o_extra=lambda h, tc: []):
            hpc = 128 // HDK
            opc = 128 // HDV
            hstep = HDV + 1
            str_ = P.sem(tagp + "str")
            rowd = P.sem(tagp + "rowd")
            expd = P.sem(tagp + "expd")
            pvp = P.sem(tagp + "pvp")
            oev = P.sem(tagp + "oev")
            bmm = P.sem(tagp + "bmm")
            stch = P.sem(tagp + "sch")
            bias_cnt = {-2: 0, -1: 0}
            st_slots = [pB[0][:, 0:512], pB[1][:, 0:512], pMiscF[:, 0:512]]
            st_mark = []
            st_n = [0]
            pa_rel = [0]
            stch_of = {}
            srow_rd = [0]

            pA4 = pA[:, 1536:2048]

            def pvb_of(h, tc):
                return (pB[2][:, 0:512], 0) if (4 * h + tc) % 2 == 0 \
                    else (pA4, 1)

            def st_acquire():
                n = st_n[0]; st_n[0] += 1
                w = []
                if n >= 3:
                    w.append(st_mark[n - 3])
                assert len(st_mark) == n
                return st_slots[n % 3], w

            def stats_mms(h, i):
                cch = h // hpc
                r0 = HDK * (h % hpc)
                tpr = (r0 // 32) * 32
                W = 128 * (i + 1)
                w = list(prereq)
                w.append((vg, pa_rel[0]))
                nsub = (W + 511) // 512
                smm0 = str_.n
                bm_cnt = {}
                P.emit("tensor", lambda Eng: Eng.ldweights(identb[:, 0:1]))
                for j in range(nsub):
                    ww = min(512, W - 512 * j)
                    b = 512 * (j % 3)
                    mw = w if j == 0 else []
                    if j >= 3:
                        mw = mw + [(vg, bm_cnt[j - 3])]
                    P.emit("tensor", lambda Eng, j=j, ww=ww, b=b, cch=cch,
                           r0=r0, tpr=tpr, i=i: Eng.matmul(
                               pA[:, b:b + ww],
                               qf[r0:r0 + HDK, 2048 * cch + 128 * i:
                                  2048 * cch + 128 * (i + 1)],
                               kf[r0:r0 + HDK, 2048 * cch + 512 * j:
                                  2048 * cch + 512 * j + ww],
                               start=True, stop=True, tile_position=(tpr, 0)),
                           waits=mw, inc=(str_, 1))
                    bw = [(str_, smm0 + j + 1)]
                    if j == nsub - 1:
                        v_op(lambda Eng, b=b, ww=ww: Eng.tensor_tensor(
                            out=pA[:, b + ww - 128:b + ww],
                            in0=pA[:, b + ww - 128:b + ww],
                            in1=mask128[:], op=Alu.add),
                            waits=bw + [(gG, masks_ready)])
                        bw = []
                    v_op(lambda Eng, j=j, ww=ww, b=b: Eng.reduce_max(
                        sA[:, 4 + j:5 + j], pA[:, b:b + ww],
                        axis=AX.X), waits=bw, guard=True)
                    bm_cnt[j] = vg.n
                pa_rel[0] = vg.n
                v_op(lambda Eng, nsub=nsub: Eng.reduce_max(
                    sA[:, 0:1], sA[:, 4:4 + nsub], axis=AX.X), guard=True)
                v_op(lambda Eng: Eng.tensor_copy(out=m16[:], in_=sA[:, 0:1]),
                     guard=True)
                v_op(lambda Eng, i=i: Eng.tensor_scalar(
                    out=nmS[:, i:i + 1], in0=m16[:], scalar1=-1.0,
                    scalar2=None, op0=Alu.mult), guard=True, inc=(stch, 1))
                stch_of[(h, i)] = stch.n

            def stats_repl(h, i):
                par = h % 2
                need = stch_of[(h, i)]
                slot, w = st_acquire()
                P.emit("tensor", lambda Eng, slot=slot, i=i: Eng.matmul(
                    slot[:, 0:128],
                    nmS[:, i:i + 1].to_broadcast([128, 128]),
                    identb[:], start=True, stop=True),
                    waits=w + [(stch, need)], inc=(tT, 1))
                t1 = tT.n
                rg = [(bmm, bias_cnt[h - 2])]
                v_op(lambda Eng, slot=slot, par=par, i=i: Eng.tensor_copy(
                    out=negm_all[par][:, 128 * i:128 * (i + 1)],
                    in_=slot[:, 0:128]),
                    waits=[(tT, t1)] + rg, inc=(rowd, 1))
                st_mark.append((rowd, rowd.n))

            ecnt = [0]
            pv_cnt = {}
            exp_of = {}

            def einsum_front(h, tc, sc, rows_ready):
                par = h % 2
                cch = h // hpc
                r0 = HDK * (h % hpc)
                tpr = (r0 // 32) * 32
                m = ecnt[0]
                ecnt[0] += 1
                pstt, w = st_acquire()
                w.append((rowd, rows_ready))
                P.emit("tensor", lambda Eng: Eng.ldweights(identb[:, 0:1]))
                P.emit("tensor", lambda Eng, pstt=pstt, sc=sc, tc=tc, cch=cch,
                       r0=r0, tpr=tpr: Eng.matmul(
                           pstt[:, 0:512],
                           kf[r0:r0 + HDK, 2048 * cch + 128 * sc:
                              2048 * cch + 128 * (sc + 1)],
                           qf[r0:r0 + HDK, 2048 * cch + 512 * tc:
                              2048 * cch + 512 * (tc + 1)],
                           start=True, stop=False, tile_position=(tpr, 0)),
                       waits=w)
                P.emit("tensor", lambda Eng, pstt=pstt, par=par, tc=tc,
                       tpr=tpr: Eng.matmul(
                    pstt[:, 0:512], ones_bf[tpr:tpr + 1, :],
                    negm_all[par][tpr:tpr + 1, 512 * tc:512 * (tc + 1)],
                    start=False, stop=True, tile_position=(tpr, 0)),
                    inc=(bmm, 1))
                stmm = bmm.n
                off = 128 * sc - 512 * tc
                slot = m % 3
                exp_w = [(bmm, stmm)]
                if off >= 0:
                    v_op(lambda Eng, pstt=pstt, off=off: Eng.tensor_tensor(
                        out=pstt[:, 0:512], in0=pstt[:, 0:512],
                        in1=mask896[:, 384 - off:896 - off], op=Alu.add),
                        waits=[(bmm, stmm), (gG, masks_ready)])
                    exp_w = [(vg, vg.n)]
                if m >= 3:
                    exp_w.append((pvp, pv_cnt[m - 3]))
                s_op(lambda Eng, pstt=pstt, slot=slot: Eng.activation(
                    out=phat[:, 512 * slot:512 * (slot + 1)],
                    in_=pstt[:, 0:512], func=Act.Exp),
                    waits=exp_w, inc=(expd, 1))
                st_mark.append((expd, expd.n))
                exp_of[m] = expd.n
                return (m, h, tc, sc, slot, expd.n)

            pend_sig = []
            oev_bank = {0: 0, 1: 0}

            def sig_emit(job):
                h, tc, v_rec = job
                pvt, bk = pvb_of(h, tc)
                orow = HDV * (h % opc)
                ocol = T * (h // opc)
                slot, w = st_acquire()
                P.emit("tensor", lambda Eng, slot=slot: Eng.matmul(
                    slot[0:HDV, 0:512],
                    ones_bf[0:1, 0:HDV], srow[0:1, :],
                    start=True, stop=True, tile_position=(0, 0)),
                    waits=w + [(vg, v_rec)], inc=(tT, 1))
                srow_rd[0] = tT.n
                t1 = tT.n
                v_op(lambda Eng, slot=slot: Eng.tensor_copy(
                    out=srec_sb[0:HDV, :], in_=slot[0:HDV, 0:512]),
                    waits=[(tT, t1)])
                st_mark.append((vg, vg.n))
                v_op(lambda Eng, tc=tc, pvt=pvt, orow=orow, ocol=ocol:
                     Eng.tensor_tensor(
                         out=o_fm[orow:orow + HDV,
                                  ocol + 512 * tc:ocol + 512 * (tc + 1)],
                         in0=pvt[0:HDV, 0:512],
                         in1=srec_sb[0:HDV, :],
                         op=Alu.mult),
                     waits=o_extra(h, tc), inc=(oev, 1))
                oev_bank[bk] = oev.n

            def recip_fn(Eng, pvt):
                with nc.allow_low_precision(reason="bf16 softmax denom row"):
                    return Eng.reciprocal(
                        srow[0:1, :], pvt[HDV:HDV + 1, 0:512])

            def pv_emit(job):
                m, h, tc, sc, slot, pexp = job
                pvt, bk = pvb_of(h, tc)
                nsc = 4 * (tc + 1)
                while pend_sig:
                    sig_emit(pend_sig.pop(0))
                pv_w = [(expd, pexp)]
                if sc == 0:
                    pv_w.append((oev, oev_bank[bk]))
                P.emit("tensor", lambda Eng: Eng.ldweights(identb[:, 0:1]))
                P.emit("tensor", lambda Eng, sc=sc, h=h, slot=slot,
                       pvt=pvt, nsc=nsc: Eng.matmul(
                    pvt[0:HDV + 1, 0:512],
                    vt[:, vt_stride * sc + hstep * h:
                       vt_stride * sc + hstep * h + hstep],
                    phat[:, 512 * slot:512 * (slot + 1)],
                    start=(sc == 0), stop=(sc == nsc - 1),
                    tile_position=(0, 0)),
                    waits=pv_w, inc=(pvp, 1))
                pv_cnt[m] = pvp.n
                if sc == nsc - 1:
                    v_op(lambda Eng, pvt=pvt: recip_fn(Eng, pvt),
                         waits=[(pvp, pvp.n), (tT, srow_rd[0])])
                    pend_sig.append((h, tc, vg.n))

            NHrun = NH
            if upto == "attn1":
                NHrun = 1
            rows_ready = {}
            for i in range(NT):
                stats_mms(0, i)
                stats_repl(0, i)
            rows_ready[0] = rowd.n
            if upto == "attnS":
                return rowd, rowd.n, str_, bmm
            for h in range(NHrun):
                nxt = h + 1 < NHrun
                acts = []
                if nxt:
                    for j in range(NT):
                        acts.append(("mm", j))
                        if j >= 4:
                            acts.append(("rp", j - 4))
                    for j in range(NT - 4, NT):
                        acts.append(("rp", j))

                def do_act(a):
                    if a[0] == "mm":
                        stats_mms(h + 1, a[1])
                    else:
                        stats_repl(h + 1, a[1])

                ai = 0
                pend_pv = []
                slots = [(tc, sc) for tc in range(4) for sc in range(4 * (tc + 1))]
                nsl = len(slots)
                for si, (tc, sc) in enumerate(slots):
                    pend_pv.append(einsum_front(h, tc, sc, rows_ready[h]))
                    if len(pend_pv) > 2:
                        pv_emit(pend_pv.pop(0))
                    want = (len(acts) * (si + 1)) // nsl
                    while ai < want:
                        do_act(acts[ai])
                        ai += 1
                for job in pend_pv:
                    pv_emit(job)
                while pend_sig:
                    sig_emit(pend_sig.pop(0))
                assert ai == len(acts)
                if nxt:
                    rows_ready[h + 1] = rowd.n
                bias_cnt[h] = bmm.n
            return oev, oev.n, str_, bmm

        o_fm_mla = fmbuf[:, 0:4 * T]
        oevM, oevM_n, strM, bmmM = attention(
            "ma", MH, ROPE, 32, q_fm, k_fm, v_tm, o_fm_mla, 528,
            prereq=[(qfmd, qfm_all), (kfmd, kfm_all), (kvuse, vtm_all)])

        tap("o_fm", o_fm_mla, [(oevM, oevM_n)])
        tap("negm0", negm_all[0][0:1, :], [(oevM, oevM_n)])
        if upto in ("attn", "attn1", "attnS"):
            P.wait("sync", (oevM, oevM_n))
            for i in range(NT):
                P.emit("sync", lambda Eng, i=i: Eng.dma_start(
                    out=y_d[128 * i:128 * (i + 1), :],
                    in_=stream[:, 1024 * i:1024 * (i + 1)]), inc=(dO, 16))
            P.wait("sync", (dO, dO.n))
            P.flush("tail")
            ph1m.close()
            ph1.close()
            return nc, tap_d

        wosel = ph1m.enter_context(nc.sbuf_tensor("wosel", [128, 4 * 1024], BF16))
        for h in range(16):
            P.emit("gpsimd", lambda Eng, h=h: Eng.dma_start(
                out=wosel[32 * (h % 4):32 * (h % 4) + 32,
                          1024 * (h // 4):1024 * (h // 4) + 1024],
                in_=aWo[64 * h:64 * h + 32, :]),
                waits=[(wqfree, 4), (kvT, kv_done), (kvuse, vtm_all),
                       (qfmd, qfm_all)] if h == 0 else (),
                inc=(d_wo, 16))
        wosel_loaded = d_wo.n

        pl_sem = P.sem("pl")
        woev = P.sem("woev")
        for i in range(NT):
            for half in range(2):
                n = 2 * i + half
                pq = pB[n % 2]
                w = [(oevM, oevM_n), (d_wo, wosel_loaded)]
                if n >= 2:
                    w.append((woev, n - 1))
                for g in range(4):
                    P.emit("tensor", lambda Eng, pq=pq, g=g, i=i, half=half: Eng.matmul(
                        pq[:, 0:512],
                        o_fm_mla[:, 2048 * g + 128 * i:2048 * g + 128 * (i + 1)],
                        wosel[:, 1024 * g + 512 * half:1024 * g + 512 * (half + 1)],
                        start=(g == 0), stop=(g == 3)),
                        waits=w if g == 0 else (), inc=(tT, 1) if g == 3 else None)
                wo_mm = tT.n
                v_op(lambda Eng, pq=pq, i=i, half=half: Eng.tensor_tensor(
                    out=stream[:, 1024 * i + 512 * half:1024 * i + 512 * (half + 1)],
                    in0=pq[:, 0:512],
                    in1=stream[:, 1024 * i + 512 * half:1024 * i + 512 * (half + 1)],
                    op=Alu.add),
                    waits=[(tT, wo_mm)], inc=(woev, 1))
            s_added = woev.n
            v_op(lambda Eng, i=i: Eng.reduce_sum(
                colA[:, 0:1], stream[:, 1024 * i:1024 * (i + 1)], axis=AX.X),
                waits=[(woev, s_added)], guard=True)
            s_op(lambda Eng, i=i: Eng.activation(
                out=junkb[:, 0:1024], in_=stream[:, 1024 * i:1024 * (i + 1)],
                func=Act.Square, accum_out=colA[:, 1:2]),
                waits=[(woev, s_added), (vg, vg.n)], guard=True)
            s_sq = sg.n
            v_op(lambda Eng: Eng.tensor_scalar(
                out=colA[:, 2:3], in0=colA[:, 0:1], scalar1=1.0 / 1024, scalar2=None,
                op0=Alu.mult), guard=True)
            v_op(lambda Eng: Eng.tensor_scalar(
                out=colA[:, 3:4], in0=colA[:, 1:2], scalar1=1.0 / 1024, scalar2=None,
                op0=Alu.mult), waits=[(sg, s_sq)], guard=True)
            v_op(lambda Eng: Eng.tensor_tensor(
                out=colA[:, 4:5], in0=colA[:, 2:3], in1=colA[:, 2:3], op=Alu.mult),
                guard=True)
            v_op(lambda Eng: Eng.tensor_sub(
                colA[:, 5:6], colA[:, 3:4], colA[:, 4:5]), guard=True)
            v_var = vg.n
            s_op(lambda Eng: Eng.activation(
                out=colA[:, 6:7], in_=colA[:, 5:6], func=Act.Sqrt, bias=epsT[:, 1:2]),
                waits=[(vg, v_var)], guard=True)
            s_sqrt = sg.n
            v_op(lambda Eng: Eng.reciprocal(colA[:, 7:8], colA[:, 6:7]),
                 waits=[(sg, s_sqrt)], guard=True)
            v_op(lambda Eng, i=i: Eng.tensor_scalar(
                out=stream[:, 1024 * i:1024 * (i + 1)],
                in0=stream[:, 1024 * i:1024 * (i + 1)],
                scalar1=colA[:, 2:3], scalar2=colA[:, 7:8],
                op0=Alu.subtract, op1=Alu.mult), guard=True, inc=(pl_sem, 1))
        a1_done = pl_sem.n

        tap("a1", stream, [(pl_sem, a1_done)])

        P.flush("ph1")
        ph1m.close()

        ph2 = ExitStack()
        v_tm2 = ph2.enter_context(nc.sbuf_tensor("v_tm2", [128, NT * 1040], BF16))[:]
        ascr_h = ph2.enter_context(nc.sbuf_tensor("ascr", [128, 2 * 1024], F32))
        ascr = ascr_h[:]
        borow = ph2.enter_context(nc.sbuf_tensor("borow", [1, D], BF16))
        g_op(lambda Eng: Eng.memset(v_tm2[:], 1.0), guard=True,
             waits=[(pl_sem, a1_done)], inc=(gg, 1))
        vones2 = gg.n
        q_fm2 = stream_bf[:, 0:16384]
        k_fm2 = stream_bf[:, 16384:32768]
        h_fm = fmbuf
        d_sp = P.sem("d_sp"); d_rl = P.sem("d_rl"); d_fb = P.sem("d_fb")
        d_fwo = P.sem("d_fwo")
        lnr = P.sem("lnr"); hfm = P.sem("hfm")

        sq_last = 0
        for i in range(NT):
            rw = [(pl_sem, a1_done)]
            v_op(lambda Eng, i=i: Eng.reduce_sum(
                colA[:, 0:1], stream[:, 1024 * i:1024 * (i + 1)], axis=AX.X),
                waits=rw, guard=True)
            s_op(lambda Eng, i=i: Eng.activation(
                out=junkb[:, 0:1024], in_=stream[:, 1024 * i:1024 * (i + 1)],
                func=Act.Square, accum_out=colA[:, 1:2]),
                waits=rw + [(vg, vg.n)], guard=True)
            sq_last = sg.n
            v_op(lambda Eng: Eng.tensor_scalar(
                out=colA[:, 2:3], in0=colA[:, 0:1], scalar1=1.0 / 1024, scalar2=None,
                op0=Alu.mult), guard=True)
            v_op(lambda Eng: Eng.tensor_scalar(
                out=colA[:, 3:4], in0=colA[:, 1:2], scalar1=1.0 / 1024, scalar2=None,
                op0=Alu.mult), waits=[(sg, sq_last)], guard=True)
            v_op(lambda Eng: Eng.tensor_tensor(
                out=colA[:, 4:5], in0=colA[:, 2:3], in1=colA[:, 2:3], op=Alu.mult),
                guard=True)
            v_op(lambda Eng: Eng.tensor_sub(
                colA[:, 5:6], colA[:, 3:4], colA[:, 4:5]), guard=True)
            v_var = vg.n
            s_op(lambda Eng: Eng.activation(
                out=colA[:, 6:7], in_=colA[:, 5:6], func=Act.Sqrt, bias=epsT[:, 1:2]),
                waits=[(vg, v_var)], guard=True)
            s_rt = sg.n
            v_op(lambda Eng: Eng.reciprocal(colA[:, 7:8], colA[:, 6:7]),
                 waits=[(sg, s_rt)], guard=True)
            v_op(lambda Eng, i=i: Eng.tensor_scalar(
                out=qn_sc[:, 0:1024], in0=stream[:, 1024 * i:1024 * (i + 1)],
                scalar1=colA[:, 2:3], scalar2=colA[:, 7:8],
                op0=Alu.subtract, op1=Alu.mult), guard=True, inc=(lnr, 1))
            nh_done = lnr.n
            for c in range(KD):
                tsb, tsf, tsw = TS.acquire()
                P.emit("tensor", lambda Eng, c=c, tsb=tsb: Eng.transpose(
                    tsb, qn_sc[:, 128 * c:128 * (c + 1)], identb[:]),
                    waits=[(lnr, nh_done)] + tsw, inc=(tT, 1))
                htr = tT.n
                v_op(lambda Eng, c=c, i=i, tsb=tsb: Eng.scalar_tensor_tensor(
                    out=h_fm[:, 2048 * c + 128 * i:2048 * c + 128 * (i + 1)],
                    in0=tsb, scalar=lnT[:, c:c + 1],
                    in1=lnT[:, 8 + c:9 + c].to_broadcast([128, 128]),
                    op0=Alu.mult, op1=Alu.add),
                    waits=[(tT, htr)], inc=(hfm, 1))
                TS.read(hfm)
        hfm_all = hfm.n

        dma("sync", d_sp, a1sp.rearrange("(n p) d -> p n d", p=128),
            stream.rearrange("p (n d) -> p n d", d=D),
            waits=[(lnr, lnr.n), (sg, sq_last)])

        ph2w = ExitStack()
        fwb = [ph2w.enter_context(nc.sbuf_tensor(f"fwb{j}", [128, 8 * 256], BF16))
               for j in range(2)]
        brow_qkv = [ph2w.enter_context(nc.sbuf_tensor(f"brqkv{j}", [1, D], BF16))
                    for j in range(3)]
        for j, bd in enumerate((fbq, fbk, fbv)):
            dma("gpsimd", d_fb, brow_qkv[j][:], bd.unsqueeze(0),
                waits=[(pl_sem, a1_done)] if j == 0 else ())
        dma("gpsimd", d_fb, borow[:], fbo.unsqueeze(0))
        fb_loaded = d_fb.n

        d_fq = [P.sem("d_fq0"), P.sem("d_fq1")]
        fqfree = P.sem("fqfree")
        wsrc = [fWq, fWk, fWv]
        fq_loads = {}

        def load_fw(idx):
            W, oc = wsrc[idx // 4], idx % 4
            w = [(pl_sem, a1_done)] if idx < 2 else [(fqfree, idx - 1)]
            P.emit("gpsimd", lambda Eng, W=W, oc=oc, idx=idx: Eng.dma_start(
                out=fwb[idx % 2][:].rearrange("p (c n) -> p c n", n=256),
                in_=W[:, 256 * oc:256 * (oc + 1)].rearrange("(c p) n -> p c n", p=128)),
                waits=w, inc=(d_fq[idx % 2], 16))
            fq_loads[idx] = (d_fq[idx % 2], d_fq[idx % 2].n)

        load_fw(0)
        load_fw(1)
        qfm2 = P.sem("qfm2"); kfm2 = P.sem("kfm2"); vtm2 = P.sem("vtm2")
        proj_ev = {}
        for idx in range(12):
            kind, oc = idx // 4, idx % 4
            for i in range(NT):
                n = idx * NT + i
                pq = pA[:, 512 * (n % 2):512 * (n % 2) + 256]
                w = [fq_loads[idx], (hfm, hfm_all), (d_sp, 16)]
                if n >= 2:
                    w.append(proj_ev[n - 2])
                for c in range(KD):
                    P.emit("tensor", lambda Eng, pq=pq, c=c, i=i, idx=idx: Eng.matmul(
                        pq,
                        h_fm[:, 2048 * c + 128 * i:2048 * c + 128 * (i + 1)],
                        fwb[idx % 2][:, 256 * c:256 * (c + 1)],
                        start=(c == 0), stop=False),
                        waits=w if c == 0 else ())
                last_inc = (fqfree, 1) if i == NT - 1 else (tT, 1)
                P.emit("tensor", lambda Eng, pq=pq, kind=kind, oc=oc: Eng.matmul(
                    pq, ones_bf[0:1, :],
                    brow_qkv[kind][0:1, 256 * oc:256 * (oc + 1)],
                    start=False, stop=True),
                    waits=[(d_fb, fb_loaded)], inc=last_inc)
                mm_wait = (fqfree, (idx // 4) * 0 + (fqfree.n)) if i == NT - 1 else (tT, tT.n)
                if kind == 2:
                    s_op(lambda Eng, pq=pq, i=i, oc=oc: Eng.copy(
                        out=v_tm2[:, 1040 * i + 260 * oc:1040 * i + 260 * (oc + 1)]
                        .rearrange("p (h c) -> p h c", c=65)[:, :, 0:64],
                        in_=pq.rearrange("p (h c) -> p h c", c=64)),
                        waits=[mm_wait, (gg, vones2)], inc=(vtm2, 1))
                    proj_ev[n] = (vtm2, vtm2.n)
                else:
                    sc1 = 0.125 if kind == 0 else 1.0
                    v_op(lambda Eng, pq=pq, sc1=sc1: Eng.tensor_scalar(
                        out=qn_sc[:, 0:256], in0=pq,
                        scalar1=sc1, scalar2=None, op0=Alu.mult),
                        waits=[mm_wait, (hfm, hfm_all)], guard=True)
                    proj_ev[n] = (vg, vg.n)
                    dest = q_fm2 if kind == 0 else k_fm2
                    dsem2 = qfm2 if kind == 0 else kfm2
                    for b in range(2):
                        ch = 2 * oc + b
                        tsb, tsf, tsw = TS.acquire()
                        P.emit("tensor", lambda Eng, b=b, tsb=tsb: Eng.transpose(
                            tsb, qn_sc[:, 128 * b:128 * (b + 1)], identb[:]),
                            waits=[(vg, vg.n)] + tsw, inc=(tT, 1))
                        qtr = tT.n
                        v_op(lambda Eng, dest=dest, ch=ch, i=i, tsb=tsb:
                             Eng.tensor_copy(
                                 out=dest[:, 2048 * ch + 128 * i:
                                          2048 * ch + 128 * (i + 1)],
                                 in_=tsb),
                             waits=[(tT, qtr)], inc=(dsem2, 1))
                        TS.read(dsem2)
            if idx + 2 < 12:
                load_fw(idx + 2)
        qfm2_all = qfm2.n
        kfm2_all = kfm2.n
        vtm2_all = vtm2.n

        tap("h_fm", h_fm, [(hfm, hfm_all)])
        tap("qf2", q_fm2, [(qfm2, qfm2_all)])
        tap("kf2", k_fm2, [(kfm2, kfm2_all)])
        tap("vt2", v_tm2, [(vtm2, vtm2_all)])
        if upto == "fproj":
            P.wait("sync", (qfm2, qfm2_all), (kfm2, kfm2_all), (vtm2, vtm2_all))
            P.emit("sync", lambda Eng: Eng.dma_start(
                out=y_d[0:128, :], in_=stream[:, 0:1024]),
                waits=[(qfm2, qfm2_all)], inc=(dO, 16))
            P.wait("sync", (dO, dO.n))
            P.wait("gpsimd", (dT, dT.n))
            P.flush("tail")
            ph2w.close()
            ph2.close()
            ph1.close()
            return nc, tap_d

        o_fm_ffn = fmbuf
        oevF, oevF_n, strF, bmmF = attention(
            "ff", 16, 64, 64, q_fm2, k_fm2, v_tm2, o_fm_ffn, 1040,
            prereq=[(qfm2, qfm2_all), (kfm2, kfm2_all), (vtm2, vtm2_all)])

        tap("o_ffn", o_fm_ffn, [(oevF, oevF_n)])

        dma("sync", d_rl, stream.rearrange("p (n d) -> p n d", d=D),
            a1sp.rearrange("(n p) d -> p n d", p=128),
            waits=[(strF, strF.n), (bmmF, bmmF.n), (oevF, oevF_n)])

        ph2w.close()
        ph2wo = ExitStack()
        fwo = [ph2wo.enter_context(nc.sbuf_tensor(f"fwo{j}", [128, 8 * 256], BF16))
               for j in range(4)]
        for oc in range(4):
            P.emit("gpsimd", lambda Eng, oc=oc: Eng.dma_start(
                out=fwo[oc][:].rearrange("p (c n) -> p c n", n=256),
                in_=fWo[:, 256 * oc:256 * (oc + 1)].rearrange("(c p) n -> p c n", p=128)),
                waits=[(fqfree, 12), (oevF, oevF_n)] if oc == 0 else (),
                inc=(d_fwo, 16))
        fwo_loaded = d_fwo.n

        woev2 = P.sem("woev2"); ln2r = P.sem("ln2r"); h2f = P.sem("h2f")
        wfmd = P.sem("wfmd"); bfold = P.sem("bfold")
        wo_ev = {}
        rt_ev = None
        e8_cnt = 0
        for i in range(NT):
            half = 1024 * (i % 2)
            for oc in range(4):
                n = 4 * i + oc
                pq = pB[n % 2]
                w = [(oevF, oevF_n), (d_fwo, fwo_loaded)]
                if n >= 2:
                    w.append(wo_ev[n - 2])
                for g in range(KD):
                    P.emit("tensor", lambda Eng, pq=pq, g=g, i=i, oc=oc: Eng.matmul(
                        pq[:, 0:256],
                        o_fm_ffn[:, 2048 * g + 128 * i:2048 * g + 128 * (i + 1)],
                        fwo[oc][:, 256 * g:256 * (g + 1)],
                        start=(g == 0), stop=False),
                        waits=w if g == 0 else ())
                P.emit("tensor", lambda Eng, pq=pq, oc=oc: Eng.matmul(
                    pq[:, 0:256], ones_bf[0:1, :],
                    borow[0:1, 256 * oc:256 * (oc + 1)],
                    start=False, stop=True),
                    waits=[(d_fb, fb_loaded)], inc=(tT, 1))
                wo_mm = tT.n
                v_op(lambda Eng, pq=pq, i=i, oc=oc, half=half: Eng.tensor_tensor(
                    out=ascr[:, half + 256 * oc:half + 256 * (oc + 1)],
                    in0=pq[:, 0:256],
                    in1=stream[:, 1024 * i + 256 * oc:1024 * i + 256 * (oc + 1)],
                    op=Alu.add),
                    waits=[(tT, wo_mm), (d_rl, 16)], inc=(woev2, 1))
                wo_ev[n] = (woev2, woev2.n)
            a_done = woev2.n
            v_op(lambda Eng, half=half: Eng.reduce_sum(
                colA[:, 0:1], ascr[:, half:half + 1024], axis=AX.X),
                waits=[(woev2, a_done)], guard=True)
            s_op(lambda Eng, half=half: Eng.activation(
                out=junkb[:, 0:1024], in_=ascr[:, half:half + 1024],
                func=Act.Square, accum_out=colA[:, 1:2]),
                waits=[(woev2, a_done), (vg, vg.n)], guard=True)
            sq2 = sg.n
            v_op(lambda Eng: Eng.tensor_scalar(
                out=colA[:, 2:3], in0=colA[:, 0:1], scalar1=1.0 / 1024, scalar2=None,
                op0=Alu.mult), guard=True)
            v_op(lambda Eng: Eng.tensor_scalar(
                out=colA[:, 3:4], in0=colA[:, 1:2], scalar1=1.0 / 1024, scalar2=None,
                op0=Alu.mult), waits=[(sg, sq2)], guard=True)
            v_op(lambda Eng: Eng.tensor_tensor(
                out=colA[:, 4:5], in0=colA[:, 2:3], in1=colA[:, 2:3], op=Alu.mult),
                guard=True)
            v_op(lambda Eng: Eng.tensor_sub(
                colA[:, 5:6], colA[:, 3:4], colA[:, 4:5]), guard=True)
            v_var = vg.n
            s_op(lambda Eng: Eng.activation(
                out=colA[:, 6:7], in_=colA[:, 5:6], func=Act.Sqrt, bias=epsT[:, 1:2]),
                waits=[(vg, v_var)], guard=True)
            s_rt = sg.n
            v_op(lambda Eng: Eng.reciprocal(colA[:, 7:8], colA[:, 6:7]),
                 waits=[(sg, s_rt)], guard=True)
            v_op(lambda Eng, half=half: Eng.tensor_scalar(
                out=qn_sc[:, 0:1024], in0=ascr[:, half:half + 1024],
                scalar1=colA[:, 2:3], scalar2=colA[:, 7:8],
                op0=Alu.subtract, op1=Alu.mult), guard=True, inc=(ln2r, 1))
            nh2_done = ln2r.n
            for c in range(KD):
                tsb, tsf, tsw = TS.acquire()
                P.emit("tensor", lambda Eng, c=c, tsb=tsb: Eng.transpose(
                    tsb, qn_sc[:, 128 * c:128 * (c + 1)], identb[:]),
                    waits=[(ln2r, nh2_done)] + tsw, inc=(tT, 1))
                htr = tT.n
                v_op(lambda Eng, c=c, i=i, tsb=tsb: Eng.scalar_tensor_tensor(
                    out=fmbuf[:, 2048 * c + 128 * i:2048 * c + 128 * (i + 1)],
                    in0=tsb, scalar=lnT[:, 16 + c:17 + c],
                    in1=lnT[:, 24 + c:25 + c].to_broadcast([128, 128]),
                    op0=Alu.mult, op1=Alu.add),
                    waits=[(tT, htr)], inc=(h2f, 1))
                TS.read(h2f)
            h2_tile = h2f.n
            w = [(h2f, h2_tile), (d_mc, d_mc.n)]
            if rt_ev is not None:
                w.append(rt_ev)
            for c in range(KD):
                P.emit("tensor", lambda Eng, c=c, i=i: Eng.matmul(
                    pB[2][:, 0:8],
                    fmbuf[:, 2048 * c + 128 * i:2048 * c + 128 * (i + 1)],
                    wrbf[:, 8 * c:8 * (c + 1)],
                    start=(c == 0), stop=False),
                    waits=w if c == 0 else ())
            P.emit("tensor", lambda Eng: Eng.matmul(
                pB[2][:, 0:8], ones_bf[0:1, :], brbf[:],
                start=False, stop=True), inc=(tT, 1))
            rt_mm = tT.n
            v_op(lambda Eng: Eng.tensor_copy(out=l8[:], in_=pB[2][:, 0:8]),
                 waits=[(tT, rt_mm), (sg, e8_cnt)], guard=True)
            rt_ev = (vg, vg.n)
            v_op(lambda Eng: Eng.reduce_max(t8a[:, 0:1], l8[:], axis=AX.X, negate=True),
                 guard=True)
            v_op(lambda Eng: Eng.tensor_scalar(
                out=t8b[:], in0=l8[:], scalar1=t8a[:, 0:1], scalar2=0.0,
                op0=Alu.add, op1=Alu.is_equal), guard=True)
            v_op(lambda Eng: Eng.scalar_tensor_tensor(
                out=t8b[:], in0=t8b[:], scalar=-1e30, in1=l8[:],
                op0=Alu.mult, op1=Alu.add), guard=True)
            v_op(lambda Eng: Eng.reduce_max(t8a[:, 1:2], t8b[:], axis=AX.X, negate=True),
                 guard=True)
            v_op(lambda Eng: Eng.tensor_scalar(
                out=t8b[:], in0=l8[:], scalar1=t8a[:, 1:2], scalar2=0.0,
                op0=Alu.add, op1=Alu.is_ge), guard=True)
            v_km = vg.n
            s_op(lambda Eng: Eng.activation(
                out=sA[:, 0:8], in_=l8[:], func=Act.Exp, bias=t8a[:, 0:1]),
                waits=[(vg, v_km)], guard=True)
            e8_cnt = sg.n
            v_op(lambda Eng: Eng.tensor_tensor(
                out=sA[:, 0:8], in0=sA[:, 0:8], in1=t8b[:], op=Alu.mult),
                waits=[(sg, e8_cnt)], guard=True)
            v_op(lambda Eng: Eng.reduce_sum(t8a[:, 2:3], sA[:, 0:8], axis=AX.X),
                 guard=True)
            v_op(lambda Eng: Eng.reciprocal(t8a[:, 3:4], t8a[:, 2:3]), guard=True)
            v_op(lambda Eng, i=i: Eng.tensor_scalar(
                out=wcol[:, 8 * i:8 * (i + 1)], in0=sA[:, 0:8],
                scalar1=t8a[:, 3:4], scalar2=None, op0=Alu.mult), guard=True)
            v_op(lambda Eng, i=i: Eng.tensor_copy(
                out=w8b[:], in_=wcol[:, 8 * i:8 * (i + 1)]), guard=True)
            v_w8 = vg.n
            tsb, tsf, tsw = TS.acquire()
            P.emit("tensor", lambda Eng, tsb=tsb: Eng.transpose(
                tsb[0:8, 0:128], w8b[:], identb[:]),
                waits=[(vg, v_w8)] + tsw, inc=(tT, 1))
            wtr = tT.n
            v_op(lambda Eng, i=i, tsb=tsb: Eng.tensor_copy(
                out=wfm[0:8, 128 * i:128 * (i + 1)], in_=tsb[0:8, 0:128]),
                waits=[(tT, wtr)], inc=(wfmd, 1))
            TS.read(wfmd)
            v_op(lambda Eng, i=i, half=half: Eng.tensor_tensor(
                out=stream[:, 1024 * i:1024 * (i + 1)],
                in0=ascr[:, half:half + 1024],
                in1=stream[:, 1024 * i:1024 * (i + 1)], op=Alu.add),
                guard=True, inc=(bfold, 1))
        bfold_all = bfold.n
        wfm_all = wfmd.n

        tap("B", stream, [(bfold, bfold_all)])
        tap("wcolt", wcol[:], [(bfold, bfold_all)])
        tap("h2fm", fmbuf, [(h2f, h2f.n)])

        P.flush("ph2")

        ph2wo.close()
        ph2.close()
        ph1.close()
        ph3 = ExitStack()
        mid_sb = ph3.enter_context(nc.sbuf_tensor("mid_sb", [128, 32 * 1024], BF16))[:]
        w1b = [ph3.enter_context(nc.sbuf_tensor(f"w1b{j}", [128, 8 * 256], BF16))
               for j in range(2)]
        w2b = [ph3.enter_context(nc.sbuf_tensor(f"w2b{j}", [128, 16 * 256], BF16))
               for j in range(2)]
        ysc = ph3.enter_context(nc.sbuf_tensor("ysc", [128, 2 * 1024], F32))[:]

        h2_fm = fmbuf
        pMF = pMisc_h.ap().bitcast(F32)
        eo_tile = [pA[:, 0:512], pA[:, 512:1024], pA[:, 1024:1536],
                   pA[:, 1536:2048], pB[0][:], pB[1][:], pB[2][:],
                   pMF[:, 0:512]]

        d_w1 = [P.sem("d_w10"), P.sem("d_w11")]
        d_w2 = [P.sem("d_w20"), P.sem("d_w21")]
        w1free = P.sem("w1free"); w2free = P.sem("w2free")
        midr = P.sem("midr"); eor = P.sem("eor")
        wb2d = P.sem("wb2d"); yout = P.sem("yout")
        dOp = [P.sem("dOp0"), P.sem("dOp1")]

        for i in range(NT):
            for hf2 in range(2):
                P.emit("tensor", lambda Eng, i=i, hf2=hf2: Eng.matmul(
                    pB[0][:, 0:512], wfm[0:8, 128 * i:128 * (i + 1)],
                    b2bf[0:8, 512 * hf2:512 * (hf2 + 1)],
                    start=True, stop=True),
                    waits=[(wfmd, wfm_all), (d_mc, d_mc.n), (bfold, bfold_all),
                           (wb2d, wb2d.n)],
                    inc=(tT, 1))
                wbm = tT.n
                v_op(lambda Eng, i=i, hf2=hf2: Eng.tensor_tensor(
                    out=stream[:, 1024 * i + 512 * hf2:1024 * i + 512 * (hf2 + 1)],
                    in0=pB[0][:, 0:512],
                    in1=stream[:, 1024 * i + 512 * hf2:1024 * i + 512 * (hf2 + 1)],
                    op=Alu.add),
                    waits=[(tT, wbm)], inc=(wb2d, 1))
        wb2_all = wb2d.n

        w1_loads = {}
        w2_loads = {}
        n_w1 = [0]
        n_w2 = [0]

        def load_w1(e, tc, g):
            idx = n_w1[0]; n_w1[0] += 1
            w = [(wb2d, wb2_all)] if idx < 2 else [(w1free, idx - 1)]
            P.emit("gpsimd", lambda Eng, e=e, g=g, idx=idx: Eng.dma_start(
                out=w1b[idx % 2][:].rearrange("p (c n) -> p c n", n=256),
                in_=ew1[e][:, 256 * g:256 * (g + 1)].rearrange(
                    "(c p) n -> p c n", p=128)),
                waits=w, inc=(d_w1[idx % 2], 16))
            w1_loads[(e, tc, g)] = (d_w1[idx % 2], d_w1[idx % 2].n)

        def load_w2(e, tc, dq, fh):
            idx = n_w2[0]; n_w2[0] += 1
            w = [(wb2d, wb2_all)] if idx < 2 else [(w2free, idx - 1)]
            P.emit("gpsimd", lambda Eng, e=e, dq=dq, fh=fh, idx=idx: Eng.dma_start(
                out=w2b[idx % 2][:].rearrange("p (c n) -> p c n", n=256),
                in_=ew2[e][2048 * fh:2048 * (fh + 1),
                           256 * dq:256 * (dq + 1)].rearrange(
                    "(c p) n -> p c n", p=128)),
                waits=w, inc=(d_w2[idx % 2], 16))
            w2_loads[(e, tc, dq, fh)] = (d_w2[idx % 2], d_w2[idx % 2].n)

        load_w1(0, 0, 0)
        load_w1(0, 0, 1)
        load_w2(0, 0, 0, 0)
        load_w2(0, 0, 0, 1)

        mid_free = {}
        n_mid = [0]
        eo_ev = {t: None for t in range(8)}
        tile_done = {}

        E_RUN = 1 if upto == "moe1" else E
        for e in range(E_RUN):
            for tc in range(2):
                for g in range(16):
                    for fc2 in range(2):
                        fc = 2 * g + fc2
                        acq = n_mid[0]; n_mid[0] += 1
                        pm = pA[:, 1024 * (acq % 2):1024 * (acq % 2) + 1024]
                        w = [w1_loads[(e, tc, g)], (h2f, h2f.n),
                             (wb2d, wb2_all)]
                        if acq >= 2:
                            w.append(mid_free[acq - 2])
                        for t_eo in (2 * (acq % 2), 2 * (acq % 2) + 1):
                            if eo_ev[t_eo] is not None:
                                w.append(eo_ev[t_eo])
                        for k in range(KD):
                            for tq in range(2):
                                P.emit("tensor", lambda Eng, pm=pm, k=k, tq=tq,
                                       fc2=fc2, tc=tc, e=e, g=g: Eng.matmul(
                                    pm[:, 512 * tq:512 * (tq + 1)],
                                    w1b[(w1_loads[(e, tc, g)][0] is d_w1[1]) * 1][
                                        :, 256 * k + 128 * fc2:
                                        256 * k + 128 * (fc2 + 1)],
                                    h2_fm[:, 2048 * k + 1024 * tc + 512 * tq:
                                          2048 * k + 1024 * tc + 512 * (tq + 1)],
                                    start=(k == 0), stop=(k == KD - 1)),
                                    waits=w if (k == 0 and tq == 0) else (),
                                    inc=((w1free, 1) if fc2 == 1 else (midr, 1))
                                    if (k == KD - 1 and tq == 1) else None)
                        mmark = (w1free, w1free.n) if fc2 == 1 else (midr, midr.n)
                        s_op(lambda Eng, pm=pm, fc=fc, e=e: Eng.activation(
                            out=mid_sb[:, 1024 * fc:1024 * (fc + 1)],
                            in_=pm[:, 0:1024], func=Act.Gelu,
                            bias=b1T[:, 32 * e + fc:32 * e + fc + 1]),
                            waits=[mmark], guard=True)
                        mid_free[acq] = (sg, sg.n)
                    nxt = (e, tc, g + 2)
                    if g + 2 >= 16:
                        nxt = (e, tc + 1, g + 2 - 16) if tc == 0 else \
                              (e + 1, 0, g + 2 - 16)
                    if nxt[0] < E and nxt not in w1_loads:
                        load_w1(*nxt)
                mid_ready = (sg, mid_free[n_mid[0] - 1][1])
                for dq in range(4):
                    for fh in range(2):
                        wl = w2_loads[(e, tc, dq, fh)]
                        for fcl in range(16):
                            fc = 16 * fh + fcl
                            for tsub in range(8):
                                w = []
                                if fcl == 0 and fh == 0:
                                    w = [mid_ready, wl]
                                    if eo_ev[tsub] is not None:
                                        w.append(eo_ev[tsub])
                                elif fcl == 0:
                                    w = [wl]
                                inc = None
                                if fh == 1 and fcl == 15:
                                    inc = (w2free, 1) if tsub == 7 else (eor, 1)
                                elif fh == 0 and fcl == 15 and tsub == 7:
                                    inc = (w2free, 1)
                                P.emit("tensor", lambda Eng, fc=fc, tsub=tsub,
                                       fcl=fcl, dq=dq, wl=wl: Eng.matmul(
                                    eo_tile[tsub][:, 256 * (dq % 2):
                                                  256 * (dq % 2) + 256],
                                    mid_sb[:, 1024 * fc + 128 * tsub:
                                           1024 * fc + 128 * (tsub + 1)],
                                    w2b[(wl[0] is d_w2[1]) * 1][
                                        :, 256 * fcl:256 * (fcl + 1)],
                                    start=(fc == 0), stop=(fc == 31)),
                                    waits=w, inc=inc)
                        nxt = (e, tc, dq, fh + 1)
                        if nxt[3] >= 2:
                            nxt = (e, tc, dq + 1, 0)
                        if nxt[2] >= 4:
                            nxt = (e, tc + 1, 0, 0) if tc == 0 else (e + 1, 0, 0, 0)
                        if nxt[0] < E and (nxt not in w2_loads):
                            load_w2(*nxt)
                    eor_base = eor.n
                    w2_last = (w2free, w2free.n)
                    for tsub in range(8):
                        ti = 8 * tc + tsub
                        if tsub == 7:
                            stop_w = [w2_last]
                        else:
                            stop_w = [(eor, eor_base - 7 + tsub + 1)]
                        v_op(lambda Eng, t=tsub, ti=ti, dq=dq, e=e: Eng.scalar_tensor_tensor(
                            out=stream[:, 1024 * ti + 256 * dq:
                                       1024 * ti + 256 * (dq + 1)],
                            in0=eo_tile[t][:, 256 * (dq % 2):256 * (dq % 2) + 256],
                            scalar=wcol[:, 8 * ti + e:8 * ti + e + 1],
                            in1=stream[:, 1024 * ti + 256 * dq:
                                       1024 * ti + 256 * (dq + 1)],
                            op0=Alu.mult, op1=Alu.add),
                            waits=stop_w, guard=True)
                        eo_ev[tsub] = (vg, vg.n)
                        tile_done[ti] = (vg, vg.n)

        tap("preln", stream, [tile_done[i] for i in range(NT)])
        tap("mids", mid_sb, [mid_free[n_mid[0] - 1]])

        for i in range(NT):
            w = [tile_done[i]]
            v_op(lambda Eng, i=i: Eng.reduce_sum(
                colA[:, 0:1], stream[:, 1024 * i:1024 * (i + 1)], axis=AX.X),
                waits=w, guard=True)
            s_op(lambda Eng, i=i: Eng.activation(
                out=junkb[:, 0:1024], in_=stream[:, 1024 * i:1024 * (i + 1)],
                func=Act.Square, accum_out=colA[:, 1:2]),
                waits=w + [(vg, vg.n)], guard=True)
            sqf = sg.n
            v_op(lambda Eng: Eng.tensor_scalar(
                out=colA[:, 2:3], in0=colA[:, 0:1], scalar1=1.0 / 1024, scalar2=None,
                op0=Alu.mult), guard=True)
            v_op(lambda Eng: Eng.tensor_scalar(
                out=colA[:, 3:4], in0=colA[:, 1:2], scalar1=1.0 / 1024, scalar2=None,
                op0=Alu.mult), waits=[(sg, sqf)], guard=True)
            v_op(lambda Eng: Eng.tensor_tensor(
                out=colA[:, 4:5], in0=colA[:, 2:3], in1=colA[:, 2:3], op=Alu.mult),
                guard=True)
            v_op(lambda Eng: Eng.tensor_sub(
                colA[:, 5:6], colA[:, 3:4], colA[:, 4:5]), guard=True)
            v_var = vg.n
            s_op(lambda Eng: Eng.activation(
                out=colA[:, 6:7], in_=colA[:, 5:6], func=Act.Sqrt, bias=epsT[:, 1:2]),
                waits=[(vg, v_var)], guard=True)
            s_rt = sg.n
            v_op(lambda Eng: Eng.reciprocal(colA[:, 7:8], colA[:, 6:7]),
                 waits=[(sg, s_rt)], guard=True)
            yw = [] if i < 2 else [(dOp[i % 2], 16 * (i // 2))]
            v_op(lambda Eng, i=i: Eng.tensor_scalar(
                out=ysc[:, 1024 * (i % 2):1024 * (i % 2 + 1)],
                in0=stream[:, 1024 * i:1024 * (i + 1)],
                scalar1=colA[:, 2:3], scalar2=colA[:, 7:8],
                op0=Alu.subtract, op1=Alu.mult),
                waits=yw, guard=True, inc=(yout, 1))
            yv = yout.n
            P.emit("sync", lambda Eng, i=i: Eng.dma_start(
                out=y_d[128 * i:128 * (i + 1), :],
                in_=ysc[:, 1024 * (i % 2):1024 * (i % 2 + 1)]),
                waits=[(yout, yv)], inc=(dOp[i % 2], 16))
        P.wait("sync", (dOp[0], dOp[0].n), (dOp[1], dOp[1].n))
        P.wait("gpsimd", (dT, dT.n))
        P.flush("ph3")
        ph3.close()
        phP.close()

    return nc, tap_d



_IN_MAP_KEYS = {
    "aWq": "L0_attn_Wq", "aWkv": "L0_attn_Wkv", "aWk": "L0_attn_Wk",
    "aWv": "L0_attn_Wv", "aWo": "L0_attn_Wo", "qg": "L0_attn_q_g",
    "kg": "L0_attn_k_g", "g1": "L0_ffn_ln1_g", "bb1": "L0_ffn_ln1_b",
    "g2": "L0_ffn_ln2_g", "bb2": "L0_ffn_ln2_b",
    "fWq": "L0_ffn_Wq", "fbq": "L0_ffn_bq", "fWk": "L0_ffn_Wk",
    "fbk": "L0_ffn_bk", "fWv": "L0_ffn_Wv", "fbv": "L0_ffn_bv",
    "fWo": "L0_ffn_Wo", "fbo": "L0_ffn_bo", "Wr": "L0_ffn_Wr",
    "br": "L0_ffn_br", "ew1": "L0_ffn_e_w1", "eb1": "L0_ffn_e_b1",
    "ew2": "L0_ffn_e_w2", "eb2": "L0_ffn_e_b2",
}


def make_in_maps(inputs):
    xs = np.ascontiguousarray(inputs["L0_attn_x"])
    base = {k: np.ascontiguousarray(inputs[v]) for k, v in _IN_MAP_KEYS.items()}
    return [dict(base, x=np.ascontiguousarray(xs[i])) for i in range(NCORES)]


def kernel(**inputs):
    nc, _ = build_nc(taps=())
    res = run_bass_kernel_spmd(nc, make_in_maps(inputs), core_ids=list(range(NCORES)))
    return np.stack([r["y"] for r in res.results]).astype(np.float32)


# version 13 (final)
# speedup vs baseline: 1.1529x; 1.0455x over previous
import numpy as np
from contextlib import ExitStack

import concourse.bass as bass
import concourse.mybir as mybir
from concourse.bass_utils import run_bass_kernel_spmd

F32 = mybir.dt.float32
BF16 = mybir.dt.bfloat16
Act = mybir.ActivationFunctionType
Alu = mybir.AluOpType
AX = mybir.AxisListType

T, D = 2048, 1024
NT = T // 128
KD = D // 128
MH, ROPE, LAT = 16, 32, 128
E, FF = 8, 4096
NCORES = 8

ENGINES = ("tensor", "vector", "scalar", "gpsimd", "sync")


class Sem:
    __slots__ = ("h", "n", "name")

    def __init__(self, h, name):
        self.h = h
        self.n = 0
        self.name = name


class Prog:
    def __init__(self, nc, stack):
        self.nc = nc
        self.stack = stack
        self.q = {e: [] for e in ENGINES}
        self.lastwait = {e: {} for e in ENGINES}
        self.lastinc = {e: None for e in ENGINES}

    def sem(self, name):
        h = self.stack.enter_context(self.nc.semaphore(name))
        return Sem(h, name)

    def emit(self, eng, fn, waits=(), inc=None):
        wl = []
        for s, v in waits:
            if v <= 0:
                continue
            assert v <= s.n, f"wait {v} > emitted {s.n} on {s.name} (engine {eng})"
            if self.lastwait[eng].get(s.name, -1) < v:
                self.lastwait[eng][s.name] = v
                wl.append((s.h, v))
        il = None
        if inc is not None:
            s, k = inc
            s.n += k
            il = (s.h, k)
            self.lastinc[eng] = (s, s.n)

        def thunk(Eng, wl=wl, il=il, fn=fn):
            for h, v in wl:
                Eng.wait_ge(h, v)
            ins = fn(Eng)
            if il is not None:
                assert ins is not None
                ins.then_inc(*il)

        self.q[eng].append(thunk)

    def wait(self, eng, *waits):
        self.emit(eng, lambda Eng: None, waits=waits)

    def flush(self, name=None):
        qs = {e: fns for e, fns in self.q.items() if fns}
        if qs:
            with self.nc.Block(name) as block:
                for e, fns in qs.items():
                    getattr(block, e)(lambda Eng, fns=fns: [f(Eng) for f in fns])
        self.q = {e: [] for e in ENGINES}


def build_nc(taps=(), upto=None):
    nc = bass.Bass("TRN2", target_bir_lowering=False)
    d_in = {}

    def din(name, shape):
        d_in[name] = nc.dram_tensor(name, shape, F32, kind="ExternalInput").ap()
        return d_in[name]

    x_d = din("x", [T, D])
    aWq = din("aWq", [D, D]); aWkv = din("aWkv", [D, LAT])
    aWk = din("aWk", [LAT, MH * ROPE]); aWv = din("aWv", [LAT, MH * 32])
    aWo = din("aWo", [D, D])
    qg_d = din("qg", [64]); kg_d = din("kg", [ROPE])
    g1_d = din("g1", [D]); bb1_d = din("bb1", [D])
    g2_d = din("g2", [D]); bb2_d = din("bb2", [D])
    fWq = din("fWq", [D, D]); fbq = din("fbq", [D])
    fWk = din("fWk", [D, D]); fbk = din("fbk", [D])
    fWv = din("fWv", [D, D]); fbv = din("fbv", [D])
    fWo = din("fWo", [D, D]); fbo = din("fbo", [D])
    Wr_d = din("Wr", [D, E]); br_d = din("br", [E])
    ew1 = din("ew1", [E, D, FF]); eb1 = din("eb1", [E, FF])
    ew2 = din("ew2", [E, FF, D]); eb2 = din("eb2", [E, D])

    y_d = nc.dram_tensor("y", [T, D], F32, kind="ExternalOutput").ap()
    a1sp = nc.dram_tensor("a1sp", [T, D], F32, kind="Internal").ap()

    taps = list(taps)
    tap_d = {}

    with ExitStack() as st:
        ec = st.enter_context

        stream_h = ec(nc.sbuf_tensor("stream", [128, NT * D], F32))
        fmbuf_h = ec(nc.sbuf_tensor("fmbuf", [128, KD * T], BF16))
        stream = stream_h[:]
        fmbuf = fmbuf_h[:]
        stream_bf = stream_h.ap().bitcast(BF16)

        identb = ec(nc.sbuf_tensor("identb", [128, 128], BF16))
        identf = ec(nc.sbuf_tensor("identf", [128, 128], F32))
        lnT = ec(nc.sbuf_tensor("lnT", [128, 32], F32))
        b1T = ec(nc.sbuf_tensor("b1T", [128, E * 32], F32))
        b2bf = ec(nc.sbuf_tensor("b2bf", [8, D], BF16))
        wrbf = ec(nc.sbuf_tensor("wrbf", [128, KD * E], BF16))
        brbf = ec(nc.sbuf_tensor("brbf", [1, E], BF16))
        wcol = ec(nc.sbuf_tensor("wcol", [128, NT * E], F32))
        wfm = ec(nc.sbuf_tensor("wfm", [8, T], BF16))
        ones_bf = ec(nc.sbuf_tensor("ones_bf", [128, 128], BF16))
        stage = ec(nc.sbuf_tensor("stage", [128, 128], F32))
        junkb = ec(nc.sbuf_tensor("junkb", [128, 2 * 1024], BF16))
        sA = ec(nc.sbuf_tensor("sA", [128, 16], F32))
        sB = ec(nc.sbuf_tensor("sB", [128, 16], F32))
        sC = ec(nc.sbuf_tensor("sC", [128, 16], F32))
        colA = ec(nc.sbuf_tensor("colA", [128, 8], F32))
        m16 = ec(nc.sbuf_tensor("m16", [128, 1], BF16))
        nm32 = ec(nc.sbuf_tensor("nm32", [128, 1], F32))
        nm16 = ec(nc.sbuf_tensor("nm16", [128, 1], BF16))
        sr16 = ec(nc.sbuf_tensor("sr16", [128, 1], BF16))
        nmS = ec(nc.sbuf_tensor("nmS", [128, 16], BF16))[:]
        l8 = ec(nc.sbuf_tensor("l8", [128, 8], F32))
        t8a = ec(nc.sbuf_tensor("t8a", [128, 8], F32))
        t8b = ec(nc.sbuf_tensor("t8b", [128, 8], F32))
        w8b = ec(nc.sbuf_tensor("w8b", [128, 8], BF16))
        qgcol = ec(nc.sbuf_tensor("qgcol", [128, 1], F32))
        kgcol = ec(nc.sbuf_tensor("kgcol", [128, 1], F32))
        epsT = ec(nc.sbuf_tensor("epsT", [128, 2], F32))

        P = Prog(nc, st)
        dS = P.sem("dS"); dO = P.sem("dO"); dT = P.sem("dT")
        d_ln = P.sem("d_ln"); d_b1 = P.sem("d_b1"); d_gn = P.sem("d_gn")
        d_mc = P.sem("d_mc"); d_aw = P.sem("d_aw"); d_wo = P.sem("d_wo")
        vg = P.sem("vg"); sg = P.sem("sg"); gG = P.sem("gG"); tT = P.sem("tT")
        gg = P.sem("gg")

        def dma(eng, sem, dst, src, waits=()):
            P.emit(eng, lambda Eng, d=dst, s=src: Eng.dma_start(out=d, in_=s),
                   waits=waits, inc=(sem, 16))

        def v_op(fn, waits=(), guard=False, inc=None):
            w = list(waits)
            if guard and P.lastinc["vector"] is not None:
                w.append(P.lastinc["vector"])
            P.emit("vector", fn, waits=w, inc=inc if inc is not None else (vg, 1))

        def s_op(fn, waits=(), guard=False, inc=None):
            w = list(waits)
            if guard and P.lastinc["scalar"] is not None:
                w.append(P.lastinc["scalar"])
            P.emit("scalar", fn, waits=w, inc=inc if inc is not None else (sg, 1))

        def g_op(fn, inc=None, waits=(), guard=False):
            w = list(waits)
            if guard and P.lastinc["gpsimd"] is not None:
                w.append(P.lastinc["gpsimd"])
            P.emit("gpsimd", fn, waits=w, inc=inc if inc is not None else (gg, 1))

        def tap(name, ap, waits):
            if name in taps:
                sh = list(ap.shape)
                tap_d[name] = nc.dram_tensor("tap_" + name, sh, F32,
                                             kind="ExternalOutput").ap()
                dma("gpsimd", dT, tap_d[name], ap, waits=waits)

        ph0 = ExitStack()
        pE = ph0.enter_context(nc.psum_tensor("pE", [128, 512], F32))
        pE2 = ph0.enter_context(nc.psum_tensor("pE2", [128, 512], F32))

        dma("sync", dS, stream.rearrange("p (n d) -> p n d", d=D),
            x_d.rearrange("(n p) d -> p n d", p=128))
        x_loaded = dS.n

        for t_ in (identb, identf):
            g_op(lambda Eng, t_=t_: Eng.memset(t_[:], 0.0))
            g_op(lambda Eng, t_=t_: Eng.affine_select(
                out=t_[:], in_=t_[:], compare_op=Alu.not_equal, fill=1.0,
                base=0, pattern=[[-1, 128]], channel_multiplier=1), guard=True)
        g_op(lambda Eng: Eng.memset(ones_bf[:], 1.0))
        g_op(lambda Eng: Eng.memset(epsT[:, 0:1], 1e-6))
        g_op(lambda Eng: Eng.memset(epsT[:, 1:2], 1e-5), inc=(gG, 1))
        ident_ready = gG.n

        dma("sync", d_ln, stage[0:8, :], g1_d.rearrange("(c n) -> c n", n=128))
        dma("sync", d_ln, stage[8:16, :], bb1_d.rearrange("(c n) -> c n", n=128))
        dma("sync", d_ln, stage[16:24, :], g2_d.rearrange("(c n) -> c n", n=128))
        dma("sync", d_ln, stage[24:32, :], bb2_d.rearrange("(c n) -> c n", n=128))
        ln_staged = d_ln.n
        P.emit("tensor", lambda Eng: Eng.transpose(pE[:, 0:128][0:128, 0:32], stage[0:32, :], identf[0:32, 0:32]),
               waits=[(d_ln, ln_staged), (gG, ident_ready)], inc=(tT, 1))
        v_op(lambda Eng: Eng.tensor_copy(out=lnT[:], in_=pE[0:128, 0:32]),
             waits=[(tT, tT.n)])
        lnT_ready = vg.n

        for g in range(2):
            P.emit("sync", lambda Eng, g=g: Eng.dma_start(
                out=stage[:],
                in_=eb1[4 * g:4 * g + 4].rearrange("e (c n) -> (e c) n", n=128)),
                waits=[(vg, lnT_ready)] if g == 0 else [(vg, vg.n)], inc=(dS, 16))
            b1_staged = dS.n
            P.emit("tensor", lambda Eng: Eng.transpose(pE[:, 0:128], stage[:], identf[:]),
                   waits=[(dS, b1_staged), (vg, vg.n)], inc=(tT, 1))
            v_op(lambda Eng, g=g: Eng.tensor_copy(
                out=b1T[:, 128 * g:128 * g + 128], in_=pE[0:128, 0:128]),
                waits=[(tT, tT.n)])
        b1T_ready = vg.n

        dma("gpsimd", d_mc, b2bf[:], eb2)
        dma("gpsimd", d_mc, wrbf[:].rearrange("p (c e) -> p c e", e=E),
            Wr_d.rearrange("(c p) e -> p c e", p=128))
        dma("gpsimd", d_mc, brbf[:], br_d.unsqueeze(0))

        qgrow = ph0.enter_context(nc.sbuf_tensor("qgrow", [1, 128], F32))
        kgrow = ph0.enter_context(nc.sbuf_tensor("kgrow", [1, 128], F32))
        dma("sync", d_gn, stage[0:1, 0:64], qg_d.unsqueeze(0),
            waits=[(vg, b1T_ready)])
        dma("sync", d_gn, stage[32:33, 0:32], kg_d.unsqueeze(0))
        gains_staged = d_gn.n
        v_op(lambda Eng: Eng.tensor_scalar(
            out=qgrow[0:1, :].rearrange("a (j r) -> a j r", r=32),
            in0=stage[0:1, 0:32].unsqueeze(1).to_broadcast([1, 4, 32]),
            scalar1=0.125, scalar2=None, op0=Alu.mult),
            waits=[(d_gn, gains_staged)])
        v_op(lambda Eng: Eng.tensor_copy(
            out=kgrow[0:1, :].rearrange("a (j r) -> a j r", r=32),
            in_=stage[32:33, 0:32].unsqueeze(1).to_broadcast([1, 4, 32])))
        gains_rows = vg.n
        P.emit("tensor", lambda Eng: Eng.transpose(pE[:, 0:128][0:128, 0:1], qgrow[:], identf[0:1, 0:1]),
               waits=[(vg, gains_rows)], inc=(tT, 1))
        v_op(lambda Eng: Eng.tensor_copy(out=qgcol[:], in_=pE[0:128, 0:1]),
             waits=[(tT, tT.n)])
        P.emit("tensor", lambda Eng: Eng.transpose(pE2[:, 0:128][0:128, 0:1], kgrow[:], identf[0:1, 0:1]),
               inc=(tT, 1))
        v_op(lambda Eng: Eng.tensor_copy(out=kgcol[:], in_=pE2[0:128, 0:1]),
             waits=[(tT, tT.n)])
        gains_ready = vg.n

        x_fm = fmbuf
        trf = P.sem("trf")
        tslots = [pE[:, 0:128], pE2[:, 0:128], pE[:, 128:256], pE2[:, 128:256]]
        n_tr = 0
        for i in range(NT):
            for c in range(KD):
                slot = tslots[n_tr % 4]
                w = [(dS, x_loaded), (gG, ident_ready), (vg, gains_ready)]
                if n_tr >= 2:
                    w = [(trf, n_tr - 1)]
                P.emit("tensor", lambda Eng, s=slot, i=i, c=c: Eng.transpose(
                    s, stream[:, 1024 * i + 128 * c:1024 * i + 128 * (c + 1)], identf[:]),
                    waits=w, inc=(tT, 1))
                v_op(lambda Eng, s=slot, i=i, c=c: Eng.tensor_copy(
                    out=x_fm[:, 2048 * c + 128 * i:2048 * c + 128 * (i + 1)], in_=s),
                    waits=[(tT, tT.n)], inc=(trf, 1))
                n_tr += 1
        x_fm_done = trf.n

        tap("x_fm", x_fm, [(trf, x_fm_done)])
        P.flush("ph0")
        ph0.close()
        if upto == "ph0":
            for i in range(NT):
                P.emit("sync", lambda Eng, i=i: Eng.dma_start(
                    out=y_d[128 * i:128 * (i + 1), :],
                    in_=stream[:, 1024 * i:1024 * (i + 1)]),
                    waits=[(trf, x_fm_done)], inc=(dO, 16))
            P.wait("sync", (dO, dO.n))
            P.wait("gpsimd", (dT, dT.n))
            P.flush("tail")
            return nc, tap_d

        ph1 = ExitStack()
        mask128 = ph1.enter_context(nc.sbuf_tensor("mask128", [128, 128], F32))
        mask896 = ph1.enter_context(nc.sbuf_tensor("mask896", [128, 896], F32))
        negm_all = [ph1.enter_context(nc.sbuf_tensor(f"negma{j}", [128, T], BF16))
                    for j in range(2)]
        srow = ph1.enter_context(nc.sbuf_tensor("srow", [1, 512], BF16))
        srec_sb = ph1.enter_context(nc.sbuf_tensor("srec_sb", [128, 512], BF16))
        phat = ph1.enter_context(nc.sbuf_tensor("phat", [128, 3 * 512], BF16))
        qn_sc = ph1.enter_context(nc.sbuf_tensor("qn_sc", [128, 1024], BF16))

        ph1m = ExitStack()
        q_fm = ph1m.enter_context(nc.sbuf_tensor("q_fm", [128, 4 * T], BF16))[:]
        k_fm = ph1m.enter_context(nc.sbuf_tensor("k_fm", [128, 4 * T], BF16))[:]
        v_tm = ph1m.enter_context(nc.sbuf_tensor("v_tm", [128, NT * 528], BF16))[:]
        kv_fm = ph1m.enter_context(nc.sbuf_tensor("kv_fm", [128, T], BF16))[:]
        g_op(lambda Eng: Eng.memset(v_tm[:], 1.0), guard=True,
             waits=[(trf, x_fm_done)], inc=(gg, 1))
        vones1 = gg.n

        phP = ExitStack()
        pA = phP.enter_context(nc.psum_tensor("pA", [128, 2048], F32))
        pB = [phP.enter_context(nc.psum_tensor(f"pB{j}", [128, 512], F32))
              for j in range(3)]
        pMisc_h = phP.enter_context(nc.psum_tensor("pMisc", [128, 1024], BF16))
        pMisc = pMisc_h[:]
        pMiscF = pMisc_h.ap().bitcast(F32)
        pA_bf = pA.bitcast(BF16)

        class TransSlots:

            def __init__(self, slots):
                self.slots = slots
                self.last = [None] * len(slots)
                self.i = 0

            def acquire(self):
                j = self.i % len(self.slots)
                self.i += 1
                self.cur = j
                w = [self.last[j]] if self.last[j] is not None else []
                return self.slots[j][0], self.slots[j][1], w

            def read(self, sem):
                self.last[self.cur] = (sem, sem.n)

        TS = TransSlots([
            (pMisc[:, 0:128], pMiscF[:, 0:128]),
            (pA_bf[:, 2048:2176], pA[:, 1024:1152]),
            (pA_bf[:, 3072:3200], pA[:, 1536:1664]),
        ])

        g_op(lambda Eng: Eng.memset(mask128[:], 0.0))
        g_op(lambda Eng: Eng.affine_select(
            out=mask128[:], in_=mask128[:], compare_op=Alu.is_ge, fill=-1e30,
            base=0, pattern=[[-1, 128]], channel_multiplier=1), guard=True)
        g_op(lambda Eng: Eng.memset(mask896[:], 0.0))
        g_op(lambda Eng: Eng.affine_select(
            out=mask896[:], in_=mask896[:], compare_op=Alu.is_ge, fill=-1e30,
            base=-384, pattern=[[1, 896]], channel_multiplier=-1),
            guard=True, inc=(gG, 1))
        masks_ready = gG.n

        ph1w = ExitStack()
        wq_bufs = [ph1w.enter_context(nc.sbuf_tensor(f"awq{j}", [128, 8 * 256], BF16))
                   for j in range(2)]
        wkv_b = ph1w.enter_context(nc.sbuf_tensor("awkv", [128, 8 * 128], BF16))
        wk_b = ph1w.enter_context(nc.sbuf_tensor("awk", [128, 512], BF16))
        wv_b = ph1w.enter_context(nc.sbuf_tensor("awv", [128, 512], BF16))
        wqfree = P.sem("wqfree")
        d_wq = [P.sem("d_wq0"), P.sem("d_wq1")]
        wq_loads = {}

        def load_wq(oc):
            w = [] if oc < 2 else [(wqfree, oc - 1)]
            P.emit("gpsimd", lambda Eng, oc=oc: Eng.dma_start(
                out=wq_bufs[oc % 2][:].rearrange("p (c n) -> p c n", n=256),
                in_=aWq[:, 256 * oc:256 * (oc + 1)].rearrange("(c p) n -> p c n", p=128)),
                waits=w, inc=(d_wq[oc % 2], 16))
            wq_loads[oc] = (d_wq[oc % 2], d_wq[oc % 2].n)

        load_wq(0)
        load_wq(1)
        dma("gpsimd", d_aw, wkv_b[:].rearrange("p (c n) -> p c n", n=128),
            aWkv.rearrange("(c p) n -> p c n", p=128))
        dma("gpsimd", d_aw, wk_b[:], aWk)
        dma("gpsimd", d_aw, wv_b[:], aWv)
        akv_loaded = d_aw.n

        kvT = P.sem("kvT")
        for tc in range(4):
            pt = pB[tc % 2]
            w = [(d_aw, akv_loaded), (trf, x_fm_done)]
            if tc >= 2:
                w.append((kvT, tc - 1))
            for c in range(KD):
                P.emit("tensor", lambda Eng, pt=pt, c=c, tc=tc: Eng.matmul(
                    pt[:, 0:512],
                    wkv_b[:, 128 * c:128 * (c + 1)],
                    x_fm[:, 2048 * c + 512 * tc:2048 * c + 512 * (tc + 1)],
                    start=(c == 0), stop=(c == KD - 1)),
                    waits=w if c == 0 else (), inc=(tT, 1) if c == KD - 1 else None)
            s_op(lambda Eng, pt=pt, tc=tc: Eng.copy(
                out=kv_fm[:, 512 * tc:512 * (tc + 1)], in_=pt[:, 0:512]),
                waits=[(tT, tT.n)], inc=(kvT, 1))
        kv_done = kvT.n

        kvuse = P.sem("kvuse")
        kmlt = P.sem("kmlt")
        kfmd = P.sem("kfmd")
        pend_k = []
        ktr_of = {}
        kn_of = {}

        def k_transpose(job):
            i = job
            for b in range(4):
                tsb, tsf, tsw = TS.acquire()
                wtr = [(kmlt, kn_of[i]), (gG, ident_ready)] + tsw
                P.emit("tensor", lambda Eng, b=b, i=i, tsb=tsb: Eng.transpose(
                    tsb, qn_sc[:, 512 * (i % 2) + 128 * b:
                               512 * (i % 2) + 128 * (b + 1)], identb[:]),
                    waits=wtr, inc=(tT, 1))
                ktr_of[i] = tT.n
                v_op(lambda Eng, b=b, i=i, tsb=tsb: Eng.tensor_scalar(
                    out=k_fm[:, 2048 * b + 128 * i:2048 * b + 128 * (i + 1)],
                    in0=tsb,
                    scalar1=kgcol[:], scalar2=None, op0=Alu.mult),
                    waits=[(tT, ktr_of[i])], inc=(kfmd, 1))
                TS.read(kfmd)

        for i in range(NT):
            pk = pB[i % 2]
            w = [(kvT, kv_done), (d_aw, akv_loaded)]
            if i >= 2:
                w.append((kmlt, i - 1))
            P.emit("tensor", lambda Eng, pk=pk, i=i: Eng.matmul(
                pk[:, 0:512], kv_fm[:, 128 * i:128 * (i + 1)], wk_b[:],
                start=True, stop=True), waits=w, inc=(tT, 1))
            k_mm = tT.n
            for h in range(16):
                s_op(lambda Eng, pk=pk, h=h: Eng.activation(
                    out=junkb[:, 0:32], in_=pk[:, 32 * h:32 * (h + 1)],
                    func=Act.Square, accum_out=sA[:, h:h + 1]),
                    waits=[(tT, k_mm), (vg, vg.n)] if h == 0 else (), guard=True)
            s_op(lambda Eng: Eng.activation(
                out=sB[:, 0:16], in_=sA[:, 0:16], func=Act.Sqrt,
                scale=1.0 / 32, bias=epsT[:, 0:1]), guard=True)
            s_rms = sg.n
            v_op(lambda Eng: Eng.reciprocal(sC[:, 0:16], sB[:, 0:16]),
                 waits=[(sg, s_rms)], guard=True)
            v_rstd = vg.n
            for h in range(16):
                mw = [(vg, v_rstd)]
                if h == 0 and i >= 2:
                    mw.append((tT, ktr_of[i - 2]))
                v_op(lambda Eng, pk=pk, h=h, i=i: Eng.tensor_scalar(
                    out=qn_sc[:, 512 * (i % 2) + 32 * h:
                              512 * (i % 2) + 32 * (h + 1)],
                    in0=pk[:, 32 * h:32 * (h + 1)],
                    scalar1=sC[:, h:h + 1], scalar2=None, op0=Alu.mult),
                    waits=mw,
                    inc=(kmlt, 1) if h == 15 else (vg, 1))
            kn_of[i] = kmlt.n
            pend_k.append(i)
            if len(pend_k) > 1:
                k_transpose(pend_k.pop(0))
            wv = [(d_aw, akv_loaded)]
            if i >= 1:
                wv.append((kvuse, i))
            P.emit("tensor", lambda Eng, i=i: Eng.matmul(
                pB[2][:, 0:512], kv_fm[:, 128 * i:128 * (i + 1)], wv_b[:],
                start=True, stop=True), waits=wv, inc=(tT, 1))
            v_mm = tT.n
            s_op(lambda Eng, i=i: Eng.copy(
                out=v_tm[:, 528 * i:528 * (i + 1)]
                .rearrange("p (h c) -> p h c", c=33)[:, :, 0:32],
                in_=pB[2][:, 0:512].rearrange("p (h c) -> p h c", c=32)),
                waits=[(tT, v_mm), (gg, vones1)], inc=(kvuse, 1))
        for job in pend_k:
            k_transpose(job)
        kfm_all = kfmd.n
        vtm_all = kvuse.n

        qfmd = P.sem("qfmd")
        quse = P.sem("quse")
        pend_q = []
        qtr_of = {}
        quse_of = {}

        def q_transpose(job):
            n, oc, i = job
            tsb, tsf, tsw = TS.acquire()
            wtr = [(quse, quse_of[n]), (kfmd, kfm_all)] + tsw
            P.emit("tensor", lambda Eng, tsb=tsb, n=n: Eng.transpose(
                tsb, qn_sc[:, 128 * (n % 2):128 * (n % 2) + 128], identb[:]),
                waits=wtr, inc=(tT, 1))
            qtr_of[n] = tT.n
            v_op(lambda Eng, oc=oc, i=i, tsb=tsb: Eng.tensor_scalar(
                out=q_fm[:, 2048 * oc + 128 * i:2048 * oc + 128 * (i + 1)],
                in0=tsb, scalar1=qgcol[:], scalar2=None, op0=Alu.mult),
                waits=[(tT, qtr_of[n])], inc=(qfmd, 1))
            TS.read(qfmd)

        for oc in range(4):
            for i in range(NT):
                n = oc * NT + i
                pq = pA[:, 512 * (n % 2):512 * (n % 2) + 256]
                w = [wq_loads[oc], (trf, x_fm_done), (kvuse, vtm_all)]
                if n >= 2:
                    w.append((quse, n - 1))
                last_inc = None
                for c in range(KD):
                    if c == KD - 1:
                        last_inc = (wqfree, 1) if i == NT - 1 else (tT, 1)
                    P.emit("tensor", lambda Eng, pq=pq, c=c, i=i, oc=oc: Eng.matmul(
                        pq,
                        x_fm[:, 2048 * c + 128 * i:2048 * c + 128 * (i + 1)],
                        wq_bufs[oc % 2][:, 256 * c:256 * (c + 1)],
                        start=(c == 0), stop=(c == KD - 1)),
                        waits=w if c == 0 else (),
                        inc=last_inc if c == KD - 1 else None)
                q_mm_wait = (wqfree, oc + 1) if i == NT - 1 else (tT, tT.n)
                for h in range(4):
                    s_op(lambda Eng, pq=pq, h=h: Eng.activation(
                        out=junkb[:, 0:64], in_=pq[0:128, 64 * h:64 * (h + 1)],
                        func=Act.Square, accum_out=sA[:, h:h + 1]),
                        waits=[q_mm_wait, (vg, vg.n)] if h == 0 else (), guard=True)
                s_op(lambda Eng: Eng.activation(
                    out=sB[:, 0:4], in_=sA[:, 0:4], func=Act.Sqrt,
                    scale=1.0 / 64, bias=epsT[:, 0:1]), guard=True)
                s_rms = sg.n
                v_op(lambda Eng: Eng.reciprocal(sC[:, 0:4], sB[:, 0:4]),
                     waits=[(sg, s_rms)], guard=True)
                v_rstd = vg.n
                for h in range(4):
                    mw = [(vg, v_rstd)]
                    if h == 0 and n >= 2:
                        mw.append((tT, qtr_of[n - 2]))
                    v_op(lambda Eng, pq=pq, h=h, n=n: Eng.tensor_scalar(
                        out=qn_sc[:, 128 * (n % 2) + 32 * h:
                                  128 * (n % 2) + 32 * (h + 1)],
                        in0=pq[0:128, 64 * h:64 * h + 32],
                        scalar1=sC[:, h:h + 1], scalar2=None, op0=Alu.mult),
                        waits=mw,
                        inc=(quse, 1) if h == 3 else (vg, 1))
                quse_of[n] = quse.n
                pend_q.append((n, oc, i))
                if len(pend_q) > 1:
                    q_transpose(pend_q.pop(0))
            if oc + 2 < 4:
                load_wq(oc + 2)
        for job in pend_q:
            q_transpose(job)
        qfm_all = qfmd.n

        ph1w.close()
        tap("q_fm", q_fm, [(qfmd, qfm_all)])
        tap("k_fm", k_fm, [(kfmd, kfm_all)])
        tap("v_tm", v_tm, [(kvuse, vtm_all)])
        tap("kv_fm", kv_fm, [(kvT, kv_done)])
        if upto == "proj":
            P.wait("sync", (qfmd, qfm_all), (kfmd, kfm_all), (kvuse, vtm_all))
            for i in range(NT):
                P.emit("sync", lambda Eng, i=i: Eng.dma_start(
                    out=y_d[128 * i:128 * (i + 1), :],
                    in_=stream[:, 1024 * i:1024 * (i + 1)]), inc=(dO, 16))
            P.wait("sync", (dO, dO.n))
            P.flush("tail")
            ph1m.close()
            ph1.close()
            return nc, tap_d

        def attention(tagp, NH, HDK, HDV, qf, kf, vt, o_fm, vt_stride,
                      prereq, o_extra=lambda h, tc: []):
            hpc = 128 // HDK
            opc = 128 // HDV
            hstep = HDV + 1
            str_ = P.sem(tagp + "str")
            rowd = P.sem(tagp + "rowd")
            expd = P.sem(tagp + "expd")
            pvp = P.sem(tagp + "pvp")
            oev = P.sem(tagp + "oev")
            bmm = P.sem(tagp + "bmm")
            stch = P.sem(tagp + "sch")
            bias_cnt = {-2: 0, -1: 0}
            st_slots = [pB[0][:, 0:512], pB[1][:, 0:512], pMiscF[:, 0:512]]
            st_mark = []
            st_n = [0]
            pa_rel = [0]
            stch_of = {}
            srow_rd = [0]

            pA4 = pA[:, 1536:2048]

            def pvb_of(h, tc):
                return (pB[2][:, 0:512], 0) if (4 * h + tc) % 2 == 0 \
                    else (pA4, 1)

            def st_acquire():
                n = st_n[0]; st_n[0] += 1
                w = []
                if n >= 3:
                    w.append(st_mark[n - 3])
                assert len(st_mark) == n
                return st_slots[n % 3], w

            def stats_mms(h, i):
                cch = h // hpc
                r0 = HDK * (h % hpc)
                tpr = (r0 // 32) * 32
                W = 128 * (i + 1)
                w = list(prereq)
                w.append((vg, pa_rel[0]))
                nsub = (W + 511) // 512
                smm0 = str_.n
                bm_cnt = {}
                P.emit("tensor", lambda Eng: Eng.ldweights(identb[:, 0:1]))
                for j in range(nsub):
                    ww = min(512, W - 512 * j)
                    b = 512 * (j % 3)
                    mw = w if j == 0 else []
                    if j >= 3:
                        mw = mw + [(vg, bm_cnt[j - 3])]
                    P.emit("tensor", lambda Eng, j=j, ww=ww, b=b, cch=cch,
                           r0=r0, tpr=tpr, i=i: Eng.matmul(
                               pA[:, b:b + ww],
                               qf[r0:r0 + HDK, 2048 * cch + 128 * i:
                                  2048 * cch + 128 * (i + 1)],
                               kf[r0:r0 + HDK, 2048 * cch + 512 * j:
                                  2048 * cch + 512 * j + ww],
                               start=True, stop=True, tile_position=(tpr, 0)),
                           waits=mw, inc=(str_, 1))
                    bw = [(str_, smm0 + j + 1)]
                    if j == nsub - 1:
                        v_op(lambda Eng, b=b, ww=ww: Eng.tensor_tensor(
                            out=pA[:, b + ww - 128:b + ww],
                            in0=pA[:, b + ww - 128:b + ww],
                            in1=mask128[:], op=Alu.add),
                            waits=bw + [(gG, masks_ready)])
                        bw = []
                    v_op(lambda Eng, j=j, ww=ww, b=b: Eng.reduce_max(
                        sA[:, 4 + j:5 + j], pA[:, b:b + ww],
                        axis=AX.X), waits=bw, guard=True)
                    bm_cnt[j] = vg.n
                pa_rel[0] = vg.n
                v_op(lambda Eng, nsub=nsub: Eng.reduce_max(
                    sA[:, 0:1], sA[:, 4:4 + nsub], axis=AX.X), guard=True)
                v_op(lambda Eng: Eng.tensor_copy(out=m16[:], in_=sA[:, 0:1]),
                     guard=True)
                v_op(lambda Eng, i=i: Eng.tensor_scalar(
                    out=nmS[:, i:i + 1], in0=m16[:], scalar1=-1.0,
                    scalar2=None, op0=Alu.mult), guard=True, inc=(stch, 1))
                stch_of[(h, i)] = stch.n

            def stats_repl(h, i):
                par = h % 2
                need = stch_of[(h, i)]
                slot, w = st_acquire()
                P.emit("tensor", lambda Eng, slot=slot, i=i: Eng.matmul(
                    slot[:, 0:128],
                    nmS[:, i:i + 1].to_broadcast([128, 128]),
                    identb[:], start=True, stop=True),
                    waits=w + [(stch, need)], inc=(tT, 1))
                t1 = tT.n
                rg = [(bmm, bias_cnt[h - 2])]
                v_op(lambda Eng, slot=slot, par=par, i=i: Eng.tensor_copy(
                    out=negm_all[par][:, 128 * i:128 * (i + 1)],
                    in_=slot[:, 0:128]),
                    waits=[(tT, t1)] + rg, inc=(rowd, 1))
                st_mark.append((rowd, rowd.n))

            ecnt = [0]
            pv_cnt = {}
            exp_of = {}

            def einsum_front(h, tc, sc, rows_ready):
                par = h % 2
                cch = h // hpc
                r0 = HDK * (h % hpc)
                tpr = (r0 // 32) * 32
                m = ecnt[0]
                ecnt[0] += 1
                pstt, w = st_acquire()
                w.append((rowd, rows_ready))
                P.emit("tensor", lambda Eng: Eng.ldweights(identb[:, 0:1]))
                P.emit("tensor", lambda Eng, pstt=pstt, sc=sc, tc=tc, cch=cch,
                       r0=r0, tpr=tpr: Eng.matmul(
                           pstt[:, 0:512],
                           kf[r0:r0 + HDK, 2048 * cch + 128 * sc:
                              2048 * cch + 128 * (sc + 1)],
                           qf[r0:r0 + HDK, 2048 * cch + 512 * tc:
                              2048 * cch + 512 * (tc + 1)],
                           start=True, stop=False, tile_position=(tpr, 0)),
                       waits=w)
                P.emit("tensor", lambda Eng, pstt=pstt, par=par, tc=tc,
                       tpr=tpr: Eng.matmul(
                    pstt[:, 0:512], ones_bf[tpr:tpr + 1, :],
                    negm_all[par][tpr:tpr + 1, 512 * tc:512 * (tc + 1)],
                    start=False, stop=True, tile_position=(tpr, 0)),
                    inc=(bmm, 1))
                stmm = bmm.n
                off = 128 * sc - 512 * tc
                slot = m % 3
                exp_w = [(bmm, stmm)]
                if off >= 0:
                    v_op(lambda Eng, pstt=pstt, off=off: Eng.tensor_tensor(
                        out=pstt[:, 0:512], in0=pstt[:, 0:512],
                        in1=mask896[:, 384 - off:896 - off], op=Alu.add),
                        waits=[(bmm, stmm), (gG, masks_ready)])
                    exp_w = [(vg, vg.n)]
                if m >= 3:
                    exp_w.append((pvp, pv_cnt[m - 3]))
                s_op(lambda Eng, pstt=pstt, slot=slot: Eng.activation(
                    out=phat[:, 512 * slot:512 * (slot + 1)],
                    in_=pstt[:, 0:512], func=Act.Exp),
                    waits=exp_w, inc=(expd, 1))
                st_mark.append((expd, expd.n))
                exp_of[m] = expd.n
                return (m, h, tc, sc, slot, expd.n)

            pend_sig = []
            oev_bank = {0: 0, 1: 0}

            def sig_emit(job):
                h, tc, v_rec = job
                pvt, bk = pvb_of(h, tc)
                orow = HDV * (h % opc)
                ocol = T * (h // opc)
                slot, w = st_acquire()
                P.emit("tensor", lambda Eng, slot=slot: Eng.matmul(
                    slot[0:HDV, 0:512],
                    ones_bf[0:1, 0:HDV], srow[0:1, :],
                    start=True, stop=True, tile_position=(0, 0)),
                    waits=w + [(vg, v_rec)], inc=(tT, 1))
                srow_rd[0] = tT.n
                t1 = tT.n
                v_op(lambda Eng, slot=slot: Eng.tensor_copy(
                    out=srec_sb[0:HDV, :], in_=slot[0:HDV, 0:512]),
                    waits=[(tT, t1)])
                st_mark.append((vg, vg.n))
                v_op(lambda Eng, tc=tc, pvt=pvt, orow=orow, ocol=ocol:
                     Eng.tensor_tensor(
                         out=o_fm[orow:orow + HDV,
                                  ocol + 512 * tc:ocol + 512 * (tc + 1)],
                         in0=pvt[0:HDV, 0:512],
                         in1=srec_sb[0:HDV, :],
                         op=Alu.mult),
                     waits=[(vg, vg.n)] + o_extra(h, tc), inc=(oev, 1))
                oev_bank[bk] = oev.n

            def recip_fn(Eng, pvt):
                with nc.allow_low_precision(reason="bf16 softmax denom row"):
                    return Eng.reciprocal(
                        srow[0:1, :], pvt[HDV:HDV + 1, 0:512])

            def pv_emit(job):
                m, h, tc, sc, slot, pexp = job
                pvt, bk = pvb_of(h, tc)
                nsc = 4 * (tc + 1)
                while pend_sig:
                    sig_emit(pend_sig.pop(0))
                pv_w = [(expd, pexp)]
                if sc == 0:
                    pv_w.append((oev, oev_bank[bk]))
                P.emit("tensor", lambda Eng: Eng.ldweights(identb[:, 0:1]))
                P.emit("tensor", lambda Eng, sc=sc, h=h, slot=slot,
                       pvt=pvt, nsc=nsc: Eng.matmul(
                    pvt[0:HDV + 1, 0:512],
                    vt[:, vt_stride * sc + hstep * h:
                       vt_stride * sc + hstep * h + hstep],
                    phat[:, 512 * slot:512 * (slot + 1)],
                    start=(sc == 0), stop=(sc == nsc - 1),
                    tile_position=(0, 0)),
                    waits=pv_w, inc=(pvp, 1))
                pv_cnt[m] = pvp.n
                if sc == nsc - 1:
                    v_op(lambda Eng, pvt=pvt: recip_fn(Eng, pvt),
                         waits=[(pvp, pvp.n), (tT, srow_rd[0])])
                    pend_sig.append((h, tc, vg.n))

            NHrun = NH
            if upto == "attn1":
                NHrun = 1
            rows_ready = {}
            for i in range(NT):
                stats_mms(0, i)
                stats_repl(0, i)
            rows_ready[0] = rowd.n
            if upto == "attnS":
                return rowd, rowd.n, str_, bmm
            for h in range(NHrun):
                nxt = h + 1 < NHrun
                acts = []
                if nxt:
                    for j in range(NT):
                        acts.append(("mm", j))
                        if j >= 4:
                            acts.append(("rp", j - 4))
                    for j in range(NT - 4, NT):
                        acts.append(("rp", j))

                def do_act(a):
                    if a[0] == "mm":
                        stats_mms(h + 1, a[1])
                    else:
                        stats_repl(h + 1, a[1])

                ai = 0
                pend_pv = []
                slots = [(tc, sc) for tc in range(4) for sc in range(4 * (tc + 1))]
                nsl = len(slots)
                for si, (tc, sc) in enumerate(slots):
                    pend_pv.append(einsum_front(h, tc, sc, rows_ready[h]))
                    if len(pend_pv) > 2:
                        pv_emit(pend_pv.pop(0))
                    want = (len(acts) * (si + 1)) // nsl
                    while ai < want:
                        do_act(acts[ai])
                        ai += 1
                for job in pend_pv:
                    pv_emit(job)
                while pend_sig:
                    sig_emit(pend_sig.pop(0))
                assert ai == len(acts)
                if nxt:
                    rows_ready[h + 1] = rowd.n
                bias_cnt[h] = bmm.n
            return oev, oev.n, str_, bmm

        o_fm_mla = fmbuf[:, 0:4 * T]
        oevM, oevM_n, strM, bmmM = attention(
            "ma", MH, ROPE, 32, q_fm, k_fm, v_tm, o_fm_mla, 528,
            prereq=[(qfmd, qfm_all), (kfmd, kfm_all), (kvuse, vtm_all)])

        tap("o_fm", o_fm_mla, [(oevM, oevM_n)])
        tap("negm0", negm_all[0][0:1, :], [(oevM, oevM_n)])
        if upto in ("attn", "attn1", "attnS"):
            P.wait("sync", (oevM, oevM_n))
            for i in range(NT):
                P.emit("sync", lambda Eng, i=i: Eng.dma_start(
                    out=y_d[128 * i:128 * (i + 1), :],
                    in_=stream[:, 1024 * i:1024 * (i + 1)]), inc=(dO, 16))
            P.wait("sync", (dO, dO.n))
            P.flush("tail")
            ph1m.close()
            ph1.close()
            return nc, tap_d

        wosel = ph1m.enter_context(nc.sbuf_tensor("wosel", [128, 4 * 1024], BF16))
        for h in range(16):
            P.emit("gpsimd", lambda Eng, h=h: Eng.dma_start(
                out=wosel[32 * (h % 4):32 * (h % 4) + 32,
                          1024 * (h // 4):1024 * (h // 4) + 1024],
                in_=aWo[64 * h:64 * h + 32, :]),
                waits=[(wqfree, 4), (kvT, kv_done), (kvuse, vtm_all),
                       (qfmd, qfm_all)] if h == 0 else (),
                inc=(d_wo, 16))
        wosel_loaded = d_wo.n

        pl_sem = P.sem("pl")
        woev = P.sem("woev")
        for i in range(NT):
            for half in range(2):
                n = 2 * i + half
                pq = pB[n % 2]
                w = [(oevM, oevM_n), (d_wo, wosel_loaded)]
                if n >= 2:
                    w.append((woev, n - 1))
                for g in range(4):
                    P.emit("tensor", lambda Eng, pq=pq, g=g, i=i, half=half: Eng.matmul(
                        pq[:, 0:512],
                        o_fm_mla[:, 2048 * g + 128 * i:2048 * g + 128 * (i + 1)],
                        wosel[:, 1024 * g + 512 * half:1024 * g + 512 * (half + 1)],
                        start=(g == 0), stop=(g == 3)),
                        waits=w if g == 0 else (), inc=(tT, 1) if g == 3 else None)
                wo_mm = tT.n
                v_op(lambda Eng, pq=pq, i=i, half=half: Eng.tensor_tensor(
                    out=stream[:, 1024 * i + 512 * half:1024 * i + 512 * (half + 1)],
                    in0=pq[:, 0:512],
                    in1=stream[:, 1024 * i + 512 * half:1024 * i + 512 * (half + 1)],
                    op=Alu.add),
                    waits=[(tT, wo_mm)], inc=(woev, 1))
            s_added = woev.n
            v_op(lambda Eng, i=i: Eng.reduce_sum(
                colA[:, 0:1], stream[:, 1024 * i:1024 * (i + 1)], axis=AX.X),
                waits=[(woev, s_added)], guard=True)
            s_op(lambda Eng, i=i: Eng.activation(
                out=junkb[:, 0:1024], in_=stream[:, 1024 * i:1024 * (i + 1)],
                func=Act.Square, accum_out=colA[:, 1:2]),
                waits=[(woev, s_added), (vg, vg.n)], guard=True)
            s_sq = sg.n
            v_op(lambda Eng: Eng.tensor_scalar(
                out=colA[:, 2:3], in0=colA[:, 0:1], scalar1=1.0 / 1024, scalar2=None,
                op0=Alu.mult), guard=True)
            v_op(lambda Eng: Eng.tensor_scalar(
                out=colA[:, 3:4], in0=colA[:, 1:2], scalar1=1.0 / 1024, scalar2=None,
                op0=Alu.mult), waits=[(sg, s_sq)], guard=True)
            v_op(lambda Eng: Eng.tensor_tensor(
                out=colA[:, 4:5], in0=colA[:, 2:3], in1=colA[:, 2:3], op=Alu.mult),
                guard=True)
            v_op(lambda Eng: Eng.tensor_sub(
                colA[:, 5:6], colA[:, 3:4], colA[:, 4:5]), guard=True)
            v_var = vg.n
            s_op(lambda Eng: Eng.activation(
                out=colA[:, 6:7], in_=colA[:, 5:6], func=Act.Sqrt, bias=epsT[:, 1:2]),
                waits=[(vg, v_var)], guard=True)
            s_sqrt = sg.n
            v_op(lambda Eng: Eng.reciprocal(colA[:, 7:8], colA[:, 6:7]),
                 waits=[(sg, s_sqrt)], guard=True)
            v_op(lambda Eng, i=i: Eng.tensor_scalar(
                out=stream[:, 1024 * i:1024 * (i + 1)],
                in0=stream[:, 1024 * i:1024 * (i + 1)],
                scalar1=colA[:, 2:3], scalar2=colA[:, 7:8],
                op0=Alu.subtract, op1=Alu.mult), guard=True, inc=(pl_sem, 1))
        a1_done = pl_sem.n

        tap("a1", stream, [(pl_sem, a1_done)])

        P.flush("ph1")
        ph1m.close()

        ph2 = ExitStack()
        v_tm2 = ph2.enter_context(nc.sbuf_tensor("v_tm2", [128, NT * 1040], BF16))[:]
        ascr_h = ph2.enter_context(nc.sbuf_tensor("ascr", [128, 2 * 1024], F32))
        ascr = ascr_h[:]
        ascr_bf = ascr_h.ap().bitcast(BF16)
        borow = ph2.enter_context(nc.sbuf_tensor("borow", [1, D], BF16))
        g_op(lambda Eng: Eng.memset(v_tm2[:], 1.0), guard=True,
             waits=[(pl_sem, a1_done)], inc=(gg, 1))
        vones2 = gg.n
        q_fm2 = stream_bf[:, 0:16384]
        k_fm2 = stream_bf[:, 16384:32768]
        h_fm = fmbuf
        d_sp = P.sem("d_sp"); d_rl = P.sem("d_rl"); d_fb = P.sem("d_fb")
        d_fwo = P.sem("d_fwo")
        lnr = P.sem("lnr"); hfm = P.sem("hfm")

        sq_last = 0
        pend_ln = []
        lntr_of = {-2: 0, -1: 0}
        lnr_of = {}

        def ln_stage(i):
            return qn_sc[:, 0:1024] if i % 2 == 0 else junkb[:, 1024:2048]

        def ln1_transpose(i):
            stg = ln_stage(i)
            for c in range(KD):
                tsb, tsf, tsw = TS.acquire()
                P.emit("tensor", lambda Eng, c=c, tsb=tsb, stg=stg: Eng.transpose(
                    tsb, stg[:, 128 * c:128 * (c + 1)], identb[:]),
                    waits=[(lnr, lnr_of[i])] + tsw, inc=(tT, 1))
                htr = tT.n
                lntr_of[i] = htr
                v_op(lambda Eng, c=c, i=i, tsb=tsb: Eng.scalar_tensor_tensor(
                    out=h_fm[:, 2048 * c + 128 * i:2048 * c + 128 * (i + 1)],
                    in0=tsb, scalar=lnT[:, c:c + 1],
                    in1=lnT[:, 8 + c:9 + c].to_broadcast([128, 128]),
                    op0=Alu.mult, op1=Alu.add),
                    waits=[(tT, htr)], inc=(hfm, 1))
                TS.read(hfm)

        for i in range(NT):
            rw = [(pl_sem, a1_done)]
            v_op(lambda Eng, i=i: Eng.reduce_sum(
                colA[:, 0:1], stream[:, 1024 * i:1024 * (i + 1)], axis=AX.X),
                waits=rw, guard=True)
            s_op(lambda Eng, i=i: Eng.activation(
                out=junkb[:, 0:1024], in_=stream[:, 1024 * i:1024 * (i + 1)],
                func=Act.Square, accum_out=colA[:, 1:2]),
                waits=rw + [(vg, vg.n)], guard=True)
            sq_last = sg.n
            v_op(lambda Eng: Eng.tensor_scalar(
                out=colA[:, 2:3], in0=colA[:, 0:1], scalar1=1.0 / 1024, scalar2=None,
                op0=Alu.mult), guard=True)
            v_op(lambda Eng: Eng.tensor_scalar(
                out=colA[:, 3:4], in0=colA[:, 1:2], scalar1=1.0 / 1024, scalar2=None,
                op0=Alu.mult), waits=[(sg, sq_last)], guard=True)
            v_op(lambda Eng: Eng.tensor_tensor(
                out=colA[:, 4:5], in0=colA[:, 2:3], in1=colA[:, 2:3], op=Alu.mult),
                guard=True)
            v_op(lambda Eng: Eng.tensor_sub(
                colA[:, 5:6], colA[:, 3:4], colA[:, 4:5]), guard=True)
            v_var = vg.n
            s_op(lambda Eng: Eng.activation(
                out=colA[:, 6:7], in_=colA[:, 5:6], func=Act.Sqrt, bias=epsT[:, 1:2]),
                waits=[(vg, v_var)], guard=True)
            s_rt = sg.n
            v_op(lambda Eng: Eng.reciprocal(colA[:, 7:8], colA[:, 6:7]),
                 waits=[(sg, s_rt)], guard=True)
            v_op(lambda Eng, i=i: Eng.tensor_scalar(
                out=ln_stage(i), in0=stream[:, 1024 * i:1024 * (i + 1)],
                scalar1=colA[:, 2:3], scalar2=colA[:, 7:8],
                op0=Alu.subtract, op1=Alu.mult),
                waits=[(tT, lntr_of[i - 2])], guard=True, inc=(lnr, 1))
            lnr_of[i] = lnr.n
            pend_ln.append(i)
            if len(pend_ln) > 1:
                ln1_transpose(pend_ln.pop(0))
        for i in pend_ln:
            ln1_transpose(i)
        hfm_all = hfm.n

        dma("sync", d_sp, a1sp.rearrange("(n p) d -> p n d", p=128),
            stream.rearrange("p (n d) -> p n d", d=D),
            waits=[(lnr, lnr.n), (sg, sq_last)])

        ph2w = ExitStack()
        fwb = [ph2w.enter_context(nc.sbuf_tensor(f"fwb{j}", [128, 8 * 256], BF16))
               for j in range(2)]
        brow_qkv = [ph2w.enter_context(nc.sbuf_tensor(f"brqkv{j}", [1, D], BF16))
                    for j in range(3)]
        for j, bd in enumerate((fbq, fbk, fbv)):
            dma("gpsimd", d_fb, brow_qkv[j][:], bd.unsqueeze(0),
                waits=[(pl_sem, a1_done)] if j == 0 else ())
        dma("gpsimd", d_fb, borow[:], fbo.unsqueeze(0))
        fb_loaded = d_fb.n

        d_fq = [P.sem("d_fq0"), P.sem("d_fq1")]
        fqfree = P.sem("fqfree")
        wsrc = [fWq, fWk, fWv]
        fq_loads = {}

        def load_fw(idx):
            W, oc = wsrc[idx // 4], idx % 4
            w = [(pl_sem, a1_done)] if idx < 2 else [(fqfree, idx - 1)]
            P.emit("gpsimd", lambda Eng, W=W, oc=oc, idx=idx: Eng.dma_start(
                out=fwb[idx % 2][:].rearrange("p (c n) -> p c n", n=256),
                in_=W[:, 256 * oc:256 * (oc + 1)].rearrange("(c p) n -> p c n", p=128)),
                waits=w, inc=(d_fq[idx % 2], 16))
            fq_loads[idx] = (d_fq[idx % 2], d_fq[idx % 2].n)

        load_fw(0)
        load_fw(1)
        qfm2 = P.sem("qfm2"); kfm2 = P.sem("kfm2"); vtm2 = P.sem("vtm2")
        proj_ev = {}
        pend_f = []
        ftr_of = {-2: 0, -1: 0}

        def f_transpose(job):
            n, i, oc, dest, dsem2 = job
            for b in range(2):
                ch = 2 * oc + b
                stile = ch + (8 if dest is k_fm2 else 0)
                tsb, tsf, tsw = TS.acquire()
                P.emit("tensor", lambda Eng, b=b, n=n, tsb=tsb: Eng.transpose(
                    tsb, ascr_bf[:, 256 * (n % 2) + 128 * b:
                                 256 * (n % 2) + 128 * (b + 1)], identb[:]),
                    waits=[proj_ev[n]] + tsw, inc=(tT, 1))
                ftr_of[n] = tT.n
                v_op(lambda Eng, dest=dest, ch=ch, i=i, tsb=tsb:
                     Eng.tensor_copy(
                         out=dest[:, 2048 * ch + 128 * i:
                                  2048 * ch + 128 * (i + 1)],
                         in_=tsb),
                     waits=[(tT, tT.n), (hfm, 8 * (stile + 1))],
                     inc=(dsem2, 1))
                TS.read(dsem2)

        for idx in range(12):
            kind, oc = idx // 4, idx % 4
            for i in range(NT):
                n = idx * NT + i
                pq = pA[:, 512 * (n % 2):512 * (n % 2) + 256]
                w = [fq_loads[idx], (hfm, 8 * (i + 1)), (d_sp, 16)]
                if n >= 2:
                    w.append(proj_ev[n - 2])
                for c in range(KD):
                    P.emit("tensor", lambda Eng, pq=pq, c=c, i=i, idx=idx: Eng.matmul(
                        pq,
                        h_fm[:, 2048 * c + 128 * i:2048 * c + 128 * (i + 1)],
                        fwb[idx % 2][:, 256 * c:256 * (c + 1)],
                        start=(c == 0), stop=False),
                        waits=w if c == 0 else ())
                last_inc = (fqfree, 1) if i == NT - 1 else (tT, 1)
                P.emit("tensor", lambda Eng, pq=pq, kind=kind, oc=oc: Eng.matmul(
                    pq, ones_bf[0:1, :],
                    brow_qkv[kind][0:1, 256 * oc:256 * (oc + 1)],
                    start=False, stop=True),
                    waits=[(d_fb, fb_loaded)], inc=last_inc)
                mm_wait = (fqfree, (idx // 4) * 0 + (fqfree.n)) if i == NT - 1 else (tT, tT.n)
                if kind == 2:
                    s_op(lambda Eng, pq=pq, i=i, oc=oc: Eng.copy(
                        out=v_tm2[:, 1040 * i + 260 * oc:1040 * i + 260 * (oc + 1)]
                        .rearrange("p (h c) -> p h c", c=65)[:, :, 0:64],
                        in_=pq.rearrange("p (h c) -> p h c", c=64)),
                        waits=[mm_wait, (gg, vones2)], inc=(vtm2, 1))
                    proj_ev[n] = (vtm2, vtm2.n)
                else:
                    sc1 = 0.125 if kind == 0 else 1.0
                    v_op(lambda Eng, pq=pq, sc1=sc1, n=n: Eng.tensor_scalar(
                        out=ascr_bf[:, 256 * (n % 2):256 * (n % 2) + 256],
                        in0=pq,
                        scalar1=sc1, scalar2=None, op0=Alu.mult),
                        waits=[mm_wait, (tT, ftr_of[n - 2])], guard=True)
                    proj_ev[n] = (vg, vg.n)
                    dest = q_fm2 if kind == 0 else k_fm2
                    dsem2 = qfm2 if kind == 0 else kfm2
                    pend_f.append((n, i, oc, dest, dsem2))
                    if len(pend_f) > 1:
                        f_transpose(pend_f.pop(0))
            if idx + 2 < 12:
                load_fw(idx + 2)
        for job in pend_f:
            f_transpose(job)
        qfm2_all = qfm2.n
        kfm2_all = kfm2.n
        vtm2_all = vtm2.n

        tap("h_fm", h_fm, [(hfm, hfm_all)])
        tap("qf2", q_fm2, [(qfm2, qfm2_all)])
        tap("kf2", k_fm2, [(kfm2, kfm2_all)])
        tap("vt2", v_tm2, [(vtm2, vtm2_all)])
        if upto == "fproj":
            P.wait("sync", (qfm2, qfm2_all), (kfm2, kfm2_all), (vtm2, vtm2_all))
            P.emit("sync", lambda Eng: Eng.dma_start(
                out=y_d[0:128, :], in_=stream[:, 0:1024]),
                waits=[(qfm2, qfm2_all)], inc=(dO, 16))
            P.wait("sync", (dO, dO.n))
            P.wait("gpsimd", (dT, dT.n))
            P.flush("tail")
            ph2w.close()
            ph2.close()
            ph1.close()
            return nc, tap_d

        o_fm_ffn = fmbuf
        oevF, oevF_n, strF, bmmF = attention(
            "ff", 16, 64, 64, q_fm2, k_fm2, v_tm2, o_fm_ffn, 1040,
            prereq=[(qfm2, qfm2_all), (kfm2, kfm2_all), (vtm2, vtm2_all)])

        tap("o_ffn", o_fm_ffn, [(oevF, oevF_n)])

        dma("sync", d_rl, stream.rearrange("p (n d) -> p n d", d=D),
            a1sp.rearrange("(n p) d -> p n d", p=128),
            waits=[(strF, strF.n), (bmmF, bmmF.n), (oevF, oevF_n)])

        ph2w.close()
        ph2wo = ExitStack()
        fwo = [ph2wo.enter_context(nc.sbuf_tensor(f"fwo{j}", [128, 8 * 256], BF16))
               for j in range(4)]
        for oc in range(4):
            P.emit("gpsimd", lambda Eng, oc=oc: Eng.dma_start(
                out=fwo[oc][:].rearrange("p (c n) -> p c n", n=256),
                in_=fWo[:, 256 * oc:256 * (oc + 1)].rearrange("(c p) n -> p c n", p=128)),
                waits=[(fqfree, 12), (oevF, oevF_n)] if oc == 0 else (),
                inc=(d_fwo, 16))
        fwo_loaded = d_fwo.n

        woev2 = P.sem("woev2"); ln2r = P.sem("ln2r"); h2f = P.sem("h2f")
        wfmd = P.sem("wfmd"); bfold = P.sem("bfold")
        wo_ev = {}
        rt_state = {"rt_ev": None, "e8_cnt": 0}
        ln2r_of = {}
        h2tr_of = {-2: 0, -1: 0}
        pend_h2 = []

        def ln2_stage(i):
            return qn_sc[:, 0:1024] if i % 2 == 0 else junkb[:, 1024:2048]

        def stage_b(i):
            half = 1024 * (i % 2)
            stg = ln2_stage(i)
            for c in range(KD):
                tsb, tsf, tsw = TS.acquire()
                P.emit("tensor", lambda Eng, c=c, tsb=tsb, stg=stg: Eng.transpose(
                    tsb, stg[:, 128 * c:128 * (c + 1)], identb[:]),
                    waits=[(ln2r, ln2r_of[i])] + tsw, inc=(tT, 1))
                htr = tT.n
                h2tr_of[i] = htr
                v_op(lambda Eng, c=c, i=i, tsb=tsb: Eng.scalar_tensor_tensor(
                    out=fmbuf[:, 2048 * c + 128 * i:2048 * c + 128 * (i + 1)],
                    in0=tsb, scalar=lnT[:, 16 + c:17 + c],
                    in1=lnT[:, 24 + c:25 + c].to_broadcast([128, 128]),
                    op0=Alu.mult, op1=Alu.add),
                    waits=[(tT, htr)], inc=(h2f, 1))
                TS.read(h2f)
            h2_tile = h2f.n
            w = [(h2f, h2_tile), (d_mc, d_mc.n)]
            if rt_state["rt_ev"] is not None:
                w.append(rt_state["rt_ev"])
            for c in range(KD):
                P.emit("tensor", lambda Eng, c=c, i=i: Eng.matmul(
                    pB[2][:, 0:8],
                    fmbuf[:, 2048 * c + 128 * i:2048 * c + 128 * (i + 1)],
                    wrbf[:, 8 * c:8 * (c + 1)],
                    start=(c == 0), stop=False),
                    waits=w if c == 0 else ())
            P.emit("tensor", lambda Eng: Eng.matmul(
                pB[2][:, 0:8], ones_bf[0:1, :], brbf[:],
                start=False, stop=True), inc=(tT, 1))
            rt_mm = tT.n
            v_op(lambda Eng: Eng.tensor_copy(out=l8[:], in_=pB[2][:, 0:8]),
                 waits=[(tT, rt_mm), (sg, rt_state["e8_cnt"])], guard=True)
            rt_state["rt_ev"] = (vg, vg.n)
            v_op(lambda Eng: Eng.reduce_max(t8a[:, 0:1], l8[:], axis=AX.X, negate=True),
                 guard=True)
            v_op(lambda Eng: Eng.tensor_scalar(
                out=t8b[:], in0=l8[:], scalar1=t8a[:, 0:1], scalar2=0.0,
                op0=Alu.add, op1=Alu.is_equal), guard=True)
            v_op(lambda Eng: Eng.scalar_tensor_tensor(
                out=t8b[:], in0=t8b[:], scalar=-1e30, in1=l8[:],
                op0=Alu.mult, op1=Alu.add), guard=True)
            v_op(lambda Eng: Eng.reduce_max(t8a[:, 1:2], t8b[:], axis=AX.X, negate=True),
                 guard=True)
            v_op(lambda Eng: Eng.tensor_scalar(
                out=t8b[:], in0=l8[:], scalar1=t8a[:, 1:2], scalar2=0.0,
                op0=Alu.add, op1=Alu.is_ge), guard=True)
            v_km = vg.n
            s_op(lambda Eng: Eng.activation(
                out=sA[:, 0:8], in_=l8[:], func=Act.Exp, bias=t8a[:, 0:1]),
                waits=[(vg, v_km)], guard=True)
            rt_state["e8_cnt"] = sg.n
            v_op(lambda Eng: Eng.tensor_tensor(
                out=sA[:, 0:8], in0=sA[:, 0:8], in1=t8b[:], op=Alu.mult),
                waits=[(sg, rt_state["e8_cnt"])], guard=True)
            v_op(lambda Eng: Eng.reduce_sum(t8a[:, 2:3], sA[:, 0:8], axis=AX.X),
                 guard=True)
            v_op(lambda Eng: Eng.reciprocal(t8a[:, 3:4], t8a[:, 2:3]), guard=True)
            v_op(lambda Eng, i=i: Eng.tensor_scalar(
                out=wcol[:, 8 * i:8 * (i + 1)], in0=sA[:, 0:8],
                scalar1=t8a[:, 3:4], scalar2=None, op0=Alu.mult), guard=True)
            v_op(lambda Eng, i=i: Eng.tensor_copy(
                out=w8b[:], in_=wcol[:, 8 * i:8 * (i + 1)]), guard=True)
            v_w8 = vg.n
            tsb, tsf, tsw = TS.acquire()
            P.emit("tensor", lambda Eng, tsb=tsb: Eng.transpose(
                tsb[0:8, 0:128], w8b[:], identb[:]),
                waits=[(vg, v_w8)] + tsw, inc=(tT, 1))
            wtr = tT.n
            v_op(lambda Eng, i=i, tsb=tsb: Eng.tensor_copy(
                out=wfm[0:8, 128 * i:128 * (i + 1)], in_=tsb[0:8, 0:128]),
                waits=[(tT, wtr)], inc=(wfmd, 1))
            TS.read(wfmd)
            v_op(lambda Eng, i=i, half=half: Eng.tensor_tensor(
                out=stream[:, 1024 * i:1024 * (i + 1)],
                in0=ascr[:, half:half + 1024],
                in1=stream[:, 1024 * i:1024 * (i + 1)], op=Alu.add),
                guard=True, inc=(bfold, 1))

        for i in range(NT):
            half = 1024 * (i % 2)
            for oc in range(4):
                n = 4 * i + oc
                pq = pB[n % 2]
                w = [(oevF, oevF_n), (d_fwo, fwo_loaded)]
                if n >= 2:
                    w.append(wo_ev[n - 2])
                for g in range(KD):
                    P.emit("tensor", lambda Eng, pq=pq, g=g, i=i, oc=oc: Eng.matmul(
                        pq[:, 0:256],
                        o_fm_ffn[:, 2048 * g + 128 * i:2048 * g + 128 * (i + 1)],
                        fwo[oc][:, 256 * g:256 * (g + 1)],
                        start=(g == 0), stop=False),
                        waits=w if g == 0 else ())
                P.emit("tensor", lambda Eng, pq=pq, oc=oc: Eng.matmul(
                    pq[:, 0:256], ones_bf[0:1, :],
                    borow[0:1, 256 * oc:256 * (oc + 1)],
                    start=False, stop=True),
                    waits=[(d_fb, fb_loaded)], inc=(tT, 1))
                wo_mm = tT.n
                v_op(lambda Eng, pq=pq, i=i, oc=oc, half=half: Eng.tensor_tensor(
                    out=ascr[:, half + 256 * oc:half + 256 * (oc + 1)],
                    in0=pq[:, 0:256],
                    in1=stream[:, 1024 * i + 256 * oc:1024 * i + 256 * (oc + 1)],
                    op=Alu.add),
                    waits=[(tT, wo_mm), (d_rl, 16), (bfold, max(0, i - 1))],
                    inc=(woev2, 1))
                wo_ev[n] = (woev2, woev2.n)
            a_done = woev2.n
            v_op(lambda Eng, half=half: Eng.reduce_sum(
                colA[:, 0:1], ascr[:, half:half + 1024], axis=AX.X),
                waits=[(woev2, a_done)], guard=True)
            s_op(lambda Eng, half=half: Eng.activation(
                out=junkb[:, 0:1024], in_=ascr[:, half:half + 1024],
                func=Act.Square, accum_out=colA[:, 1:2]),
                waits=[(woev2, a_done), (vg, vg.n)], guard=True)
            sq2 = sg.n
            v_op(lambda Eng: Eng.tensor_scalar(
                out=colA[:, 2:3], in0=colA[:, 0:1], scalar1=1.0 / 1024, scalar2=None,
                op0=Alu.mult), guard=True)
            v_op(lambda Eng: Eng.tensor_scalar(
                out=colA[:, 3:4], in0=colA[:, 1:2], scalar1=1.0 / 1024, scalar2=None,
                op0=Alu.mult), waits=[(sg, sq2)], guard=True)
            v_op(lambda Eng: Eng.tensor_tensor(
                out=colA[:, 4:5], in0=colA[:, 2:3], in1=colA[:, 2:3], op=Alu.mult),
                guard=True)
            v_op(lambda Eng: Eng.tensor_sub(
                colA[:, 5:6], colA[:, 3:4], colA[:, 4:5]), guard=True)
            v_var = vg.n
            s_op(lambda Eng: Eng.activation(
                out=colA[:, 6:7], in_=colA[:, 5:6], func=Act.Sqrt, bias=epsT[:, 1:2]),
                waits=[(vg, v_var)], guard=True)
            s_rt = sg.n
            v_op(lambda Eng: Eng.reciprocal(colA[:, 7:8], colA[:, 6:7]),
                 waits=[(sg, s_rt)], guard=True)
            v_op(lambda Eng, half=half, i=i: Eng.tensor_scalar(
                out=ln2_stage(i), in0=ascr[:, half:half + 1024],
                scalar1=colA[:, 2:3], scalar2=colA[:, 7:8],
                op0=Alu.subtract, op1=Alu.mult),
                waits=[(tT, h2tr_of[i - 2])], guard=True, inc=(ln2r, 1))
            ln2r_of[i] = ln2r.n
            pend_h2.append(i)
            if len(pend_h2) > 1:
                stage_b(pend_h2.pop(0))
        for i in pend_h2:
            stage_b(i)
        bfold_all = bfold.n
        wfm_all = wfmd.n

        tap("B", stream, [(bfold, bfold_all)])
        tap("wcolt", wcol[:], [(bfold, bfold_all)])
        tap("h2fm", fmbuf, [(h2f, h2f.n)])

        P.flush("ph2")

        ph2wo.close()
        ph2.close()
        ph1.close()
        ph3 = ExitStack()
        mid_sb = ph3.enter_context(nc.sbuf_tensor("mid_sb", [128, 32 * 1024], BF16))[:]
        w1b = [ph3.enter_context(nc.sbuf_tensor(f"w1b{j}", [128, 8 * 256], BF16))
               for j in range(2)]
        w2b = [ph3.enter_context(nc.sbuf_tensor(f"w2b{j}", [128, 16 * 256], BF16))
               for j in range(2)]
        ysc = ph3.enter_context(nc.sbuf_tensor("ysc", [128, 2 * 1024], F32))[:]

        h2_fm = fmbuf
        pMF = pMisc_h.ap().bitcast(F32)
        eo_tile = [pA[:, 0:512], pA[:, 512:1024], pA[:, 1024:1536],
                   pA[:, 1536:2048], pB[0][:], pB[1][:], pB[2][:],
                   pMF[:, 0:512]]

        d_w1 = [P.sem("d_w10"), P.sem("d_w11")]
        d_w2 = [P.sem("d_w20"), P.sem("d_w21")]
        w1free = P.sem("w1free"); w2free = P.sem("w2free")
        midr = P.sem("midr"); eor = P.sem("eor")
        wb2d = P.sem("wb2d"); yout = P.sem("yout")
        dOp = [P.sem("dOp0"), P.sem("dOp1")]

        for i in range(NT):
            for hf2 in range(2):
                P.emit("tensor", lambda Eng, i=i, hf2=hf2: Eng.matmul(
                    pB[0][:, 0:512], wfm[0:8, 128 * i:128 * (i + 1)],
                    b2bf[0:8, 512 * hf2:512 * (hf2 + 1)],
                    start=True, stop=True),
                    waits=[(wfmd, wfm_all), (d_mc, d_mc.n), (bfold, bfold_all),
                           (wb2d, wb2d.n)],
                    inc=(tT, 1))
                wbm = tT.n
                v_op(lambda Eng, i=i, hf2=hf2: Eng.tensor_tensor(
                    out=stream[:, 1024 * i + 512 * hf2:1024 * i + 512 * (hf2 + 1)],
                    in0=pB[0][:, 0:512],
                    in1=stream[:, 1024 * i + 512 * hf2:1024 * i + 512 * (hf2 + 1)],
                    op=Alu.add),
                    waits=[(tT, wbm)], inc=(wb2d, 1))
        wb2_all = wb2d.n

        w1_loads = {}
        w2_loads = {}
        n_w1 = [0]
        n_w2 = [0]

        def load_w1(e, tc, g):
            idx = n_w1[0]; n_w1[0] += 1
            w = [(wb2d, wb2_all)] if idx < 2 else [(w1free, idx - 1)]
            P.emit("gpsimd", lambda Eng, e=e, g=g, idx=idx: Eng.dma_start(
                out=w1b[idx % 2][:].rearrange("p (c n) -> p c n", n=256),
                in_=ew1[e][:, 256 * g:256 * (g + 1)].rearrange(
                    "(c p) n -> p c n", p=128)),
                waits=w, inc=(d_w1[idx % 2], 16))
            w1_loads[(e, tc, g)] = (d_w1[idx % 2], d_w1[idx % 2].n)

        def load_w2(e, tc, dq, fh):
            idx = n_w2[0]; n_w2[0] += 1
            w = [(wb2d, wb2_all)] if idx < 2 else [(w2free, idx - 1)]
            P.emit("gpsimd", lambda Eng, e=e, dq=dq, fh=fh, idx=idx: Eng.dma_start(
                out=w2b[idx % 2][:].rearrange("p (c n) -> p c n", n=256),
                in_=ew2[e][2048 * fh:2048 * (fh + 1),
                           256 * dq:256 * (dq + 1)].rearrange(
                    "(c p) n -> p c n", p=128)),
                waits=w, inc=(d_w2[idx % 2], 16))
            w2_loads[(e, tc, dq, fh)] = (d_w2[idx % 2], d_w2[idx % 2].n)

        load_w1(0, 0, 0)
        load_w1(0, 0, 1)
        load_w2(0, 0, 0, 0)
        load_w2(0, 0, 0, 1)

        mid_free = {}
        n_mid = [0]
        eo_ev = {t: None for t in range(8)}
        tile_done = {}

        E_RUN = 1 if upto == "moe1" else E
        for e in range(E_RUN):
            for tc in range(2):
                for g in range(16):
                    for fc2 in range(2):
                        fc = 2 * g + fc2
                        acq = n_mid[0]; n_mid[0] += 1
                        pm = pA[:, 1024 * (acq % 2):1024 * (acq % 2) + 1024]
                        w = [w1_loads[(e, tc, g)], (h2f, h2f.n),
                             (wb2d, wb2_all)]
                        if acq >= 2:
                            w.append(mid_free[acq - 2])
                        for t_eo in (2 * (acq % 2), 2 * (acq % 2) + 1):
                            if eo_ev[t_eo] is not None:
                                w.append(eo_ev[t_eo])
                        for k in range(KD):
                            for tq in range(2):
                                P.emit("tensor", lambda Eng, pm=pm, k=k, tq=tq,
                                       fc2=fc2, tc=tc, e=e, g=g: Eng.matmul(
                                    pm[:, 512 * tq:512 * (tq + 1)],
                                    w1b[(w1_loads[(e, tc, g)][0] is d_w1[1]) * 1][
                                        :, 256 * k + 128 * fc2:
                                        256 * k + 128 * (fc2 + 1)],
                                    h2_fm[:, 2048 * k + 1024 * tc + 512 * tq:
                                          2048 * k + 1024 * tc + 512 * (tq + 1)],
                                    start=(k == 0), stop=(k == KD - 1)),
                                    waits=w if (k == 0 and tq == 0) else (),
                                    inc=((w1free, 1) if fc2 == 1 else (midr, 1))
                                    if (k == KD - 1 and tq == 1) else None)
                        mmark = (w1free, w1free.n) if fc2 == 1 else (midr, midr.n)
                        s_op(lambda Eng, pm=pm, fc=fc, e=e: Eng.activation(
                            out=mid_sb[:, 1024 * fc:1024 * (fc + 1)],
                            in_=pm[:, 0:1024], func=Act.Gelu,
                            bias=b1T[:, 32 * e + fc:32 * e + fc + 1]),
                            waits=[mmark], guard=True)
                        mid_free[acq] = (sg, sg.n)
                    nxt = (e, tc, g + 2)
                    if g + 2 >= 16:
                        nxt = (e, tc + 1, g + 2 - 16) if tc == 0 else \
                              (e + 1, 0, g + 2 - 16)
                    if nxt[0] < E and nxt not in w1_loads:
                        load_w1(*nxt)
                mid_ready = (sg, mid_free[n_mid[0] - 1][1])
                for dq in range(4):
                    for fh in range(2):
                        wl = w2_loads[(e, tc, dq, fh)]
                        for fcl in range(16):
                            fc = 16 * fh + fcl
                            for tsub in range(8):
                                w = []
                                if fcl == 0 and fh == 0:
                                    w = [mid_ready, wl]
                                    if eo_ev[tsub] is not None:
                                        w.append(eo_ev[tsub])
                                elif fcl == 0:
                                    w = [wl]
                                inc = None
                                if fh == 1 and fcl == 15:
                                    inc = (w2free, 1) if tsub == 7 else (eor, 1)
                                elif fh == 0 and fcl == 15 and tsub == 7:
                                    inc = (w2free, 1)
                                P.emit("tensor", lambda Eng, fc=fc, tsub=tsub,
                                       fcl=fcl, dq=dq, wl=wl: Eng.matmul(
                                    eo_tile[tsub][:, 256 * (dq % 2):
                                                  256 * (dq % 2) + 256],
                                    mid_sb[:, 1024 * fc + 128 * tsub:
                                           1024 * fc + 128 * (tsub + 1)],
                                    w2b[(wl[0] is d_w2[1]) * 1][
                                        :, 256 * fcl:256 * (fcl + 1)],
                                    start=(fc == 0), stop=(fc == 31)),
                                    waits=w, inc=inc)
                        nxt = (e, tc, dq, fh + 1)
                        if nxt[3] >= 2:
                            nxt = (e, tc, dq + 1, 0)
                        if nxt[2] >= 4:
                            nxt = (e, tc + 1, 0, 0) if tc == 0 else (e + 1, 0, 0, 0)
                        if nxt[0] < E and (nxt not in w2_loads):
                            load_w2(*nxt)
                    eor_base = eor.n
                    w2_last = (w2free, w2free.n)
                    for tsub in range(8):
                        ti = 8 * tc + tsub
                        if tsub == 7:
                            stop_w = [w2_last]
                        else:
                            stop_w = [(eor, eor_base - 7 + tsub + 1)]
                        v_op(lambda Eng, t=tsub, ti=ti, dq=dq, e=e: Eng.scalar_tensor_tensor(
                            out=stream[:, 1024 * ti + 256 * dq:
                                       1024 * ti + 256 * (dq + 1)],
                            in0=eo_tile[t][:, 256 * (dq % 2):256 * (dq % 2) + 256],
                            scalar=wcol[:, 8 * ti + e:8 * ti + e + 1],
                            in1=stream[:, 1024 * ti + 256 * dq:
                                       1024 * ti + 256 * (dq + 1)],
                            op0=Alu.mult, op1=Alu.add),
                            waits=stop_w, guard=True)
                        eo_ev[tsub] = (vg, vg.n)
                        tile_done[ti] = (vg, vg.n)

        tap("preln", stream, [tile_done[i] for i in range(NT)])
        tap("mids", mid_sb, [mid_free[n_mid[0] - 1]])

        for i in range(NT):
            w = [tile_done[i]]
            v_op(lambda Eng, i=i: Eng.reduce_sum(
                colA[:, 0:1], stream[:, 1024 * i:1024 * (i + 1)], axis=AX.X),
                waits=w, guard=True)
            s_op(lambda Eng, i=i: Eng.activation(
                out=junkb[:, 0:1024], in_=stream[:, 1024 * i:1024 * (i + 1)],
                func=Act.Square, accum_out=colA[:, 1:2]),
                waits=w + [(vg, vg.n)], guard=True)
            sqf = sg.n
            v_op(lambda Eng: Eng.tensor_scalar(
                out=colA[:, 2:3], in0=colA[:, 0:1], scalar1=1.0 / 1024, scalar2=None,
                op0=Alu.mult), guard=True)
            v_op(lambda Eng: Eng.tensor_scalar(
                out=colA[:, 3:4], in0=colA[:, 1:2], scalar1=1.0 / 1024, scalar2=None,
                op0=Alu.mult), waits=[(sg, sqf)], guard=True)
            v_op(lambda Eng: Eng.tensor_tensor(
                out=colA[:, 4:5], in0=colA[:, 2:3], in1=colA[:, 2:3], op=Alu.mult),
                guard=True)
            v_op(lambda Eng: Eng.tensor_sub(
                colA[:, 5:6], colA[:, 3:4], colA[:, 4:5]), guard=True)
            v_var = vg.n
            s_op(lambda Eng: Eng.activation(
                out=colA[:, 6:7], in_=colA[:, 5:6], func=Act.Sqrt, bias=epsT[:, 1:2]),
                waits=[(vg, v_var)], guard=True)
            s_rt = sg.n
            v_op(lambda Eng: Eng.reciprocal(colA[:, 7:8], colA[:, 6:7]),
                 waits=[(sg, s_rt)], guard=True)
            yw = [] if i < 2 else [(dOp[i % 2], 16 * (i // 2))]
            v_op(lambda Eng, i=i: Eng.tensor_scalar(
                out=ysc[:, 1024 * (i % 2):1024 * (i % 2 + 1)],
                in0=stream[:, 1024 * i:1024 * (i + 1)],
                scalar1=colA[:, 2:3], scalar2=colA[:, 7:8],
                op0=Alu.subtract, op1=Alu.mult),
                waits=yw, guard=True, inc=(yout, 1))
            yv = yout.n
            P.emit("sync", lambda Eng, i=i: Eng.dma_start(
                out=y_d[128 * i:128 * (i + 1), :],
                in_=ysc[:, 1024 * (i % 2):1024 * (i % 2 + 1)]),
                waits=[(yout, yv)], inc=(dOp[i % 2], 16))
        P.wait("sync", (dOp[0], dOp[0].n), (dOp[1], dOp[1].n))
        P.wait("gpsimd", (dT, dT.n))
        P.flush("ph3")
        ph3.close()
        phP.close()

    return nc, tap_d



_IN_MAP_KEYS = {
    "aWq": "L0_attn_Wq", "aWkv": "L0_attn_Wkv", "aWk": "L0_attn_Wk",
    "aWv": "L0_attn_Wv", "aWo": "L0_attn_Wo", "qg": "L0_attn_q_g",
    "kg": "L0_attn_k_g", "g1": "L0_ffn_ln1_g", "bb1": "L0_ffn_ln1_b",
    "g2": "L0_ffn_ln2_g", "bb2": "L0_ffn_ln2_b",
    "fWq": "L0_ffn_Wq", "fbq": "L0_ffn_bq", "fWk": "L0_ffn_Wk",
    "fbk": "L0_ffn_bk", "fWv": "L0_ffn_Wv", "fbv": "L0_ffn_bv",
    "fWo": "L0_ffn_Wo", "fbo": "L0_ffn_bo", "Wr": "L0_ffn_Wr",
    "br": "L0_ffn_br", "ew1": "L0_ffn_e_w1", "eb1": "L0_ffn_e_b1",
    "ew2": "L0_ffn_e_w2", "eb2": "L0_ffn_e_b2",
}


def make_in_maps(inputs):
    xs = np.ascontiguousarray(inputs["L0_attn_x"])
    base = {k: np.ascontiguousarray(inputs[v]) for k, v in _IN_MAP_KEYS.items()}
    return [dict(base, x=np.ascontiguousarray(xs[i])) for i in range(NCORES)]


def kernel(**inputs):
    nc, _ = build_nc(taps=())
    res = run_bass_kernel_spmd(nc, make_in_maps(inputs), core_ids=list(range(NCORES)))
    return np.stack([r["y"] for r in res.results]).astype(np.float32)
